# version 16
# baseline (speedup 1.0000x reference)
import os
import numpy as np
import ml_dtypes
import concourse.bass as bass
import concourse.mybir as mybir
from concourse.bass_utils import run_bass_kernel_spmd
from concourse.aot_env import allow_ici_all_to_all

F32 = mybir.dt.float32
BF16 = mybir.dt.bfloat16
I32 = mybir.dt.int32
AF = mybir.ActivationFunctionType
ALU = mybir.AluOpType

NC = 8
S = 8192
T = 1024
D = 2048
DC = D // 128
HD = 128
NHL = 2
FI = 8192
IC = FI // 128
QS = 512
NQS = S // QS
INV_SQRT_HD = 1.0 / np.sqrt(128.0)
EPS = 1e-5

A_XLN = 0
A_QT = 0
A_KT = 4096
A_V = 12288
A_WO = 0
A_HT = 16384
A_WQKV = 20480
A_ZTMY = 32768
A_XG = 32768
A_PT = 45056
A_ZT = 47104
A_WS = 49152
A_END = 57344

F_ACC = 0
F_ACC2 = 1024
F_SG = 2048
F_T1 = 3072
F_T2 = 4096
F_RSTDB = 0
F_MRSTDB = 1024
F_END = 5120


class Plan:
    ENGS = ("pe", "ac", "dv", "gp", "sy")

    def __init__(self):
        self.ops = {e: [] for e in self.ENGS}
        self.cnt = {}

    def post(self, sem, amt):
        self.cnt[sem] = self.cnt.get(sem, 0) + amt
        return self.cnt[sem]

    def emit(self, eng, fn, waits=(), inc=None, amt=1):
        after = self.post(inc, 1 if amt == "dflt" else amt) if inc is not None else None
        self.ops[eng].append((fn, tuple(waits), inc, after, amt))
        return after


def build_program():
    nc = bass.Bass("TRN2", target_bir_lowering=False, detect_race_conditions=False)

    xT_in = nc.dram_tensor("xT_in", [D, T], F32, kind="ExternalInput")
    wqT_a = nc.dram_tensor("wqT_a", [NHL, D, HD], F32, kind="ExternalInput")
    wkT_a = nc.dram_tensor("wkT_a", [NHL, D, HD], F32, kind="ExternalInput")
    wvT_a = nc.dram_tensor("wvT_a", [NHL, D, HD], F32, kind="ExternalInput")
    woT_a = nc.dram_tensor("woT_a", [D, D], F32, kind="ExternalInput")
    wqT_b = nc.dram_tensor("wqT_b", [NHL, D, HD], F32, kind="ExternalInput")
    wkT_b = nc.dram_tensor("wkT_b", [NHL, D, HD], F32, kind="ExternalInput")
    wvT_b = nc.dram_tensor("wvT_b", [NHL, D, HD], F32, kind="ExternalInput")
    woT_b = nc.dram_tensor("woT_b", [D, D], F32, kind="ExternalInput")
    wgT = nc.dram_tensor("wgT", [D, FI], I32, kind="ExternalInput")
    wuT = nc.dram_tensor("wuT", [D, FI], I32, kind="ExternalInput")
    wdT = nc.dram_tensor("wdT", [FI, D], I32, kind="ExternalInput")
    gsT = nc.dram_tensor("gsT", [128, IC], F32, kind="ExternalInput")
    usT = nc.dram_tensor("usT", [128, IC], F32, kind="ExternalInput")
    dsT = nc.dram_tensor("dsT", [128, DC], F32, kind="ExternalInput")
    cvec_i = nc.dram_tensor("cvec", [128, 8], F32, kind="ExternalInput")
    idn_i = nc.dram_tensor("idn", [128, 128], BF16, kind="ExternalInput")
    mask_i = nc.dram_tensor("mask", [128, 128], BF16, kind="ExternalInput")
    outT = nc.dram_tensor("outT", [D, T], F32, kind="ExternalOutput")

    xg_in = [nc.dram_tensor(f"xg_in{j}", [D, T], BF16) for j in range(2)]
    xg_out = [nc.dram_tensor(f"xg_out{j}", [NC * D, T], BF16, addr_space="Shared")
              for j in range(2)]
    za_in = [[nc.dram_tensor(f"za_in{j}_{par}", [NC, NHL * HD, T // 2], BF16)
              for par in range(2)] for j in range(2)]
    za_out = [[nc.dram_tensor(f"za_out{j}_{par}", [D, T // 2], BF16)
               for par in range(2)] for j in range(2)]

    from contextlib import ExitStack
    with ExitStack() as ctx:
        E = ctx.enter_context
        abf = E(nc.sbuf_tensor("abf", [128, A_END], BF16))
        af = E(nc.sbuf_tensor("af", [128, F_END], F32))
        res = E(nc.sbuf_tensor("res", [128, DC, T], F32))
        rows = E(nc.sbuf_tensor("rows", [1, 2048], F32))
        sb_idn = E(nc.sbuf_tensor("sb_idn", [128, 128], BF16))
        sb_mask = E(nc.sbuf_tensor("sb_mask", [128, 128], BF16))
        sb_gs = E(nc.sbuf_tensor("sb_gs", [128, IC], F32))
        sb_us = E(nc.sbuf_tensor("sb_us", [128, IC], F32))
        sb_ds = E(nc.sbuf_tensor("sb_ds", [128, DC], F32))
        sb_cv = E(nc.sbuf_tensor("sb_cv", [128, 8], F32))
        ones_f = E(nc.sbuf_tensor("ones_f", [128, 1], F32))
        ones_b = E(nc.sbuf_tensor("ones_b", [128, 1], BF16))
        onesr_f = E(nc.sbuf_tensor("onesr_f", [1, 128], F32))
        pA = E(nc.psum_tensor("pA", [128, 1024], F32))
        pB = E(nc.psum_tensor("pB", [128, 1024], F32))
        pP = E(nc.psum_tensor("pP", [128, 1024], F32))
        pD = E(nc.psum_tensor("pD", [128, 512], F32))
        pT = E(nc.psum_tensor("pT", [128, 1024], BF16))
        dma_sem_names = ["ld", "xgw", "xg0", "xg1", "xg2", "zt0", "zt1", "my",
                         "wqkv", "wo", "wg0", "wg1", "wd0", "wd1"]
        dma_sems = {n: E(nc.semaphore(f"s_{n}")) for n in dma_sem_names}
        s_pe = E(nc.semaphore("s_pe"))
        s_gpc = E(nc.semaphore("s_gpc"))
        s_ac = E(nc.semaphore("s_ac"))
        s_dv = E(nc.semaphore("s_dv"))
        s_cc = [E(nc.semaphore(f"s_cc{j}")) for j in range(4)]
        s_out = E(nc.semaphore("s_out"))
        block = E(nc.Block())

        P = Plan()
        SEMS = {"pe": s_pe, "ac": s_ac, "dv": s_dv, "gpc": s_gpc,
                "cc0": s_cc[0], "cc1": s_cc[1], "cc2": s_cc[2], "cc3": s_cc[3],
                "out": s_out, **dma_sems}

        def BF(c0, n):
            return abf[:, c0:c0 + n]

        def FF(c0, n):
            return af[:, c0:c0 + n]

        for name, dst, src in (("idn", sb_idn, idn_i), ("mask", sb_mask, mask_i),
                               ("gs", sb_gs, gsT), ("us", sb_us, usT), ("ds", sb_ds, dsT),
                               ("cv", sb_cv, cvec_i)):
            P.emit("sy", lambda e, d=dst, s=src: e.dma_start(out=d[:, :], in_=s.ap()),
                   inc="ld", amt=16)
        ev_consts = P.cnt["ld"]
        P.emit("dv", lambda e: e.memset(ones_f[:, :], 1.0), inc="dv")
        P.emit("dv", lambda e: e.memset(ones_b[:, :], 1.0), inc="dv")
        P.emit("dv", lambda e: e.memset(onesr_f[:, :], 1.0), inc="dv")
        ev_ones = P.cnt["dv"]

        st = {"res_dv": 0, "lnp": {}, "next_ln": True}

        def ln_chunk(c, wsem, wval):
            stl = st["lnp"]
            acc, acc2 = FF(F_ACC, 1024), FF(F_T1, 1024)
            sq = FF(F_SG, 1024)
            w = [(wsem, wval)] if wsem else []
            if c == 0:
                stl["acc"] = P.emit("dv", lambda e: e.tensor_copy(acc, res[:, 0, :]),
                                    waits=w + [("ld", ev_consts)], inc="dv")
            else:
                stl["acc"] = P.emit("dv", lambda e, c=c: e.tensor_add(acc, acc, res[:, c, :]),
                                    waits=w, inc="dv")
            wts = list(w)
            if c > 0:
                wts.append(("dv", stl["sqa"]))
            ev_s = P.emit("ac", lambda e, c=c: e.activation(sq, res[:, c, :], AF.Square),
                          waits=wts, inc="ac")
            if c == 0:
                stl["sqa"] = P.emit("dv", lambda e: e.tensor_copy(acc2, sq),
                                    waits=[("ac", ev_s)], inc="dv")
            else:
                stl["sqa"] = P.emit("dv", lambda e, c=c: e.tensor_add(acc2, acc2, sq),
                                    waits=[("ac", ev_s)], inc="dv")

        def ln_finish(write_xg_idx):
            acc, acc2 = FF(F_ACC, 1024), FF(F_T1, 1024)
            ev_acc = st["lnp"]["acc"]
            ev_acc2 = st["lnp"]["sqa"]
            ln_ac_bar = P.cnt.get("ac", 0)
            r_sum = [rows[0:1, 0:512], rows[0:1, 1024:1536]]
            r_sq = [rows[0:1, 512:1024], rows[0:1, 1536:2048]]
            ev_row_cp = 0
            for th in range(2):
                ev_mm = P.emit("pe", lambda e, th=th: e.matmul(
                    pD[0:1, :], ones_f[:, 0:1], acc[:, th * 512:(th + 1) * 512],
                    start=True, stop=True),
                    waits=[("dv", ev_acc), ("dv", ev_row_cp), ("ac", ln_ac_bar)], inc="pe")
                ev_row_cp = P.emit("dv", lambda e, th=th: e.tensor_copy(r_sum[th], pD[0:1, :]),
                                   waits=[("pe", ev_mm)], inc="dv")
                ev_mm = P.emit("pe", lambda e, th=th: e.matmul(
                    pD[0:1, :], ones_f[:, 0:1], acc2[:, th * 512:(th + 1) * 512],
                    start=True, stop=True),
                    waits=[("dv", ev_acc2), ("dv", ev_row_cp)], inc="pe")
                ev_row_cp = P.emit("dv", lambda e, th=th: e.tensor_copy(r_sq[th], pD[0:1, :]),
                                   waits=[("pe", ev_mm)], inc="dv")
            tmp = af[0:1, 0:512]
            ev_bc = {}
            for th in range(2):
                P.emit("dv", lambda e, th=th: e.tensor_scalar_mul(r_sum[th], r_sum[th], sb_cv[0:1, 3:4]), inc="dv")
                P.emit("dv", lambda e, th=th: e.tensor_scalar_mul(r_sq[th], r_sq[th], sb_cv[0:1, 3:4]), inc="dv")
                P.emit("dv", lambda e, th=th: e.tensor_mul(tmp, r_sum[th], r_sum[th]), inc="dv")
                ev_var = P.emit("dv", lambda e, th=th: e.tensor_sub(r_sq[th], r_sq[th], tmp), inc="dv")
                ev_ln = P.emit("ac", lambda e, th=th: e.activation(r_sq[th], r_sq[th], AF.Ln, bias=sb_cv[0:1, 0:1]),
                               waits=[("dv", ev_var)], inc="ac")
                ev_rstd = P.emit("ac", lambda e, th=th: e.activation(r_sq[th], r_sq[th], AF.Exp, scale=sb_cv[0:1, 1:2]),
                                 inc="ac")
                ev_mr = P.emit("dv", lambda e, th=th: e.tensor_mul(r_sum[th], r_sum[th], r_sq[th]),
                               waits=[("ac", ev_rstd)], inc="dv")
                ev1 = P.emit("pe", lambda e, th=th: e.matmul(
                    pA[:, th * 512:(th + 1) * 512], onesr_f[0:1, :], r_sq[th],
                    start=True, stop=True), waits=[("ac", ev_rstd), ("dv", ev_ones)], inc="pe")
                ev2 = P.emit("pe", lambda e, th=th: e.matmul(
                    pB[:, th * 512:(th + 1) * 512], onesr_f[0:1, :], r_sum[th],
                    start=True, stop=True), waits=[("dv", ev_mr)], inc="pe")
                P.emit("ac", lambda e, th=th: e.copy(
                    FF(F_RSTDB, 1024)[:, th * 512:(th + 1) * 512], pA[:, th * 512:(th + 1) * 512]),
                    waits=[("pe", ev1)], inc="ac")
                ev_bc[th] = P.emit("ac", lambda e, th=th: e.copy(
                    FF(F_MRSTDB, 1024)[:, th * 512:(th + 1) * 512], pB[:, th * 512:(th + 1) * 512]),
                    waits=[("pe", ev2)], inc="ac")
            t1 = [FF(F_SG, 1024), FF(F_T2, 1024)]
            last_norm = 0
            ev_xgw = []
            for c in range(DC):
                P.emit("dv", lambda e, c=c: e.tensor_mul(t1[c % 2], res[:, c, :], FF(F_RSTDB, 1024)),
                       waits=[("ac", ev_bc[1])], inc="dv")
                last_norm = P.emit("dv", lambda e, c=c: e.tensor_sub(
                    BF(A_XLN + c * 1024, 1024), t1[c % 2], FF(F_MRSTDB, 1024)), inc="dv")
                if write_xg_idx is not None and c == DC - 1:
                    ev_xgw.append(P.emit("sy", lambda e, j=write_xg_idx: e.dma_start(
                        out=xg_in[j].ap().rearrange("(c p) t -> p c t", p=128),
                        in_=abf[:, 0:DC * 1024].rearrange("p (c t) -> p c t", t=1024)),
                        waits=[("dv", last_norm)], inc="xgw", amt=16))
            st["ln_dv"] = last_norm
            st["ln_ac"] = ev_bc[1]
            if write_xg_idx is not None:
                return ev_xgw
            return None

        vcp_list = [0]
        zt_store_by_par = {0: 0, 1: 0}
        rowmm_by_sumpar = {0: 0, 1: 0}
        ztm_by_bank = {0: 0, 1: 0}
        xg_consumed = {0: 0, 1: 0, 2: 0}

        def emit_attn(lyr, phase=9):
            cc_g = f"cc{lyr * 2}"
            cc_z = f"cc{lyr * 2 + 1}"
            wq, wk, wv = (wqT_a, wkT_a, wvT_a) if lyr == 0 else (wqT_b, wkT_b, wvT_b)
            wo = woT_a if lyr == 0 else woT_b

            wq_waits = []
            if lyr == 1:
                wq_waits = [("pe", P.cnt["pe"]), ("dv", P.cnt["dv"])]
            ev_wqkv = {}
            for h in range(NHL):
                for pi, wsrc in enumerate((wq, wk, wv)):
                    ev_wqkv[(h, pi)] = P.emit("gp", lambda e, h=h, pi=pi, wsrc=wsrc: e.dma_start(
                        out=BF(A_WQKV + h * 6144 + pi * 2048, 2048).rearrange("p (c i) -> p c i", i=128),
                        in_=wsrc.ap()[h, :, :].rearrange("(c p) i -> p c i", p=128)),
                        waits=wq_waits, inc="wqkv", amt=16)
                    wq_waits = []
            ev_wqkv_all = P.cnt["wqkv"]

            ev_xgw = ln_finish(lyr)
            ev_ccg = P.emit("gp", lambda e, j=lyr: e.collective_compute(
                "AllGather", ALU.bypass, replica_groups=[list(range(NC))],
                ins=[xg_in[j].ap().opt()], outs=[xg_out[j].ap().opt()]),
                waits=[("xgw", ev_xgw[0])], inc=cc_g, amt="dflt")

            pre_waits = [("pe", P.cnt["pe"])] if lyr == 1 else []
            ev_pre = {}
            for ic in range(2):
                for nm, wsrc, off in (("g", wgT, A_WS), ("u", wuT, A_WS + 4096)):
                    ev_pre[(nm, ic)] = P.emit("gp", lambda e, ic=ic, wsrc=wsrc, off=off: e.dma_start(
                        out=BF(off + ic * 2048, 2048).rearrange("p (c i) -> p c i", i=128),
                        in_=wsrc.ap()[:, ic * 128:(ic + 1) * 128].rearrange("(c p) i -> p c i", p=128)),
                        waits=pre_waits, inc=f"wg{ic % 2}", amt=16)
                    pre_waits = []
            st["ffn_pre"] = ev_pre
            if phase <= 1:
                P.emit("gp", lambda e: e.engine_nop(), waits=[(cc_g, 1)])
                return

            entry_dv = P.cnt["dv"]
            ztm_by_bank[0] = max(ztm_by_bank[0], entry_dv)
            ztm_by_bank[1] = max(ztm_by_bank[1], entry_dv)
            bank_free = {0: ("dv", entry_dv), 1: ("dv", entry_dv)}
            stage_tr_prev = [0]
            ev_last_av = 0
            for h in range(NHL):
                bar_pe, bar_ac, bar_dv = P.cnt["pe"], P.cnt["ac"], P.cnt["dv"]
                bar_gpc = P.cnt.get("gpc", 0)
                wbase = A_WQKV + h * 6144
                evQ = {}
                evK = {}
                evV = {}
                pair_lastS = {}
                first_prod = [True]

                def make_prod(r):
                    def gen():
                        ld = {}

                        def emit_load(i):
                            nh, fh = ((0, 0), (0, 1), (1, 0), (1, 1))[i]
                            b = (r * 4 + i) % 3
                            w = [(cc_g, ev_ccg), ("pe", xg_consumed[b])]
                            ld[(nh, fh)] = (b, P.emit("sy", lambda e, r=r, nh=nh, fh=fh, b=b, j=lyr: e.dma_start(
                                out=BF(A_XG + b * 4096, 4096).rearrange("p (c t) -> p c t", t=512),
                                in_=xg_out[j].ap()[r * D + fh * 1024:r * D + fh * 1024 + 1024, nh * 512:(nh + 1) * 512]
                                    .rearrange("(c p) t -> p c t", p=128)),
                                waits=w, inc=f"xg{b}", amt=16))
                        for i in range(3):
                            emit_load(i)
                        ev_stage = {}
                        for g, (nh, pi) in enumerate((n, p) for n in (0, 1) for p in (0, 1, 2)):
                            bk = g % 2
                            for dc in range(DC):
                                fh = dc // 8
                                b, ev_ld = ld[(nh, fh)]
                                wts = []
                                if dc == 0:
                                    wts = [(f"xg{b}", ev_ld), ("wqkv", ev_wqkv_all)]
                                    fe, fv = bank_free[bk]
                                    if fv:
                                        wts.append((fe, fv))
                                    if first_prod[0]:
                                        wts += [("ac", bar_ac), ("dv", bar_dv)]
                                        first_prod[0] = False
                                elif dc == 8:
                                    wts = [(f"xg{b}", ev_ld)]
                                P.emit("pe", lambda e, bk=bk, pi=pi, dc=dc, b=b, wbase=wbase: e.matmul(
                                    pP[:, bk * 512:(bk + 1) * 512],
                                    BF(wbase + pi * 2048 + dc * 128, 128),
                                    BF(A_XG + b * 4096 + (dc % 8) * 512, 512),
                                    start=(dc == 0), stop=(dc == DC - 1)),
                                    waits=wts, inc="pe")
                                if pi == 2 and dc == 7:
                                    xg_consumed[ld[(nh, 0)][0]] = P.cnt["pe"]
                                    if nh == 0:
                                        emit_load(3)
                                if pi == 2 and dc == 15:
                                    xg_consumed[ld[(nh, 1)][0]] = P.cnt["pe"]
                                yield
                            ev_mm = P.cnt["pe"]
                            if pi == 0:
                                wts = [("pe", max(ev_mm, pair_lastS.get(r - 4, bar_pe)))]
                                evq = P.emit("ac", lambda e, r=r, nh=nh, bk=bk: e.copy(
                                    BF(A_QT + (r % 4) * 1024 + nh * 512, 512),
                                    pP[:, bk * 512:(bk + 1) * 512]), waits=wts, inc="ac")
                                evQ[r] = evq
                                bank_free[bk] = ("ac", evq)
                            elif pi == 1:
                                evk = P.emit("dv", lambda e, r=r, nh=nh, bk=bk: e.tensor_copy(
                                    BF(A_KT + r * 1024 + nh * 512, 512),
                                    pP[:, bk * 512:(bk + 1) * 512]),
                                    waits=[("pe", max(ev_mm, bar_pe))], inc="dv")
                                evK[r] = evk
                                bank_free[bk] = ("dv", evk)
                            else:
                                wts = [("pe", max(ev_mm, stage_tr_prev[0]))]
                                evs = P.emit("ac", lambda e, nh=nh, bk=bk: e.copy(
                                    BF(A_ZT + 512 + nh * 1024, 512),
                                    pP[:, bk * 512:(bk + 1) * 512]), waits=wts, inc="ac")
                                ev_stage[nh] = evs
                                bank_free[bk] = ("ac", evs)
                        ev_tr = 0
                        for tt in range(8):
                            src = A_ZT + 512 + (tt // 4) * 1024 + (tt % 4) * 128
                            wts = [("ac", ev_stage[tt // 4]), ("dv", vcp_list[-1]),
                                   ("ld", ev_consts)]
                            ev_tr = P.emit("pe", lambda e, src=src, tt=tt: e.transpose(
                                pT[:, tt * 128:(tt + 1) * 128], BF(src, 128), sb_idn[:, :]),
                                waits=wts, inc="pe")
                            yield
                        stage_tr_prev[0] = ev_tr
                        evV[r] = P.emit("dv", lambda e, r=r: e.tensor_copy(
                            BF(A_V + r * 1024, 1024), pT[:, :]),
                            waits=[("pe", ev_tr)], inc="dv")
                        vcp_list.append(evV[r])
                    return gen()

                gens = [make_prod(r) for r in range(8)]
                gi = [0]

                def pump(n):
                    while n > 0 and gi[0] < 8:
                        try:
                            next(gens[gi[0]])
                            n -= 1
                        except StopIteration:
                            gi[0] += 1

                def drain_through(r):
                    while gi[0] <= r:
                        try:
                            next(gens[gi[0]])
                        except StopIteration:
                            gi[0] += 1

                tiles = []
                for qs in range(NQS):
                    nkc = (qs + 1) * 4
                    for j, kc in enumerate(range(nkc)):
                        qlo = max(qs * QS, kc * 128)
                        tiles.append((qs, kc, qlo, qlo - qs * QS,
                                      kc * 128 >= qs * QS, j == 0, j == nkc - 1))
                T_h = len(tiles)
                last_t = {}
                for lt, td in enumerate(tiles):
                    if td[6]:
                        last_t[td[0]] = lt
                due_rowmm = {last_t[s]: s for s in last_t}
                due_accopy = {last_t[s] + 2: s for s in last_t}
                due_bcast = {last_t[s] + 2: s for s in last_t}
                due_rbcopy = {last_t[s] + 4: s for s in last_t}
                due_close = {last_t[s] + 8: s for s in last_t}
                ev_S = {}
                ev_exp = {}
                ev_avmm = {}
                ev_summ = {}
                ev_mask = {}
                ev_rowmm = {}
                ev_accopy = {}
                ev_rexp = {}
                ev_bcast = {}
                ev_rbcopy = {}
                ev_avlast = {}
                done_close = set()
                sum_first = {}
                sum_last = {}

                def tile_j(lt):
                    qs = tiles[lt][0]
                    return lt - (last_t[qs] - 4 * (qs + 1) + 1)

                def is_pair_lead(lt):
                    if lt % 2 != 0 or lt + 1 >= T_h:
                        return False
                    a, b = tiles[lt], tiles[lt + 1]
                    return a[0] == b[0] and a[3] == 0 and b[3] == 0 and not (a[4] or b[4])

                def core_S(lt):
                    qs, kc, qlo, off, diag, first, last = tiles[lt]
                    pb = (lt % 2) * 512
                    r = qs // 2
                    ncols = QS - off
                    acw = max(ev_exp.get(lt - 2, 0), bar_ac if lt < 2 else 0)
                    wts = []
                    if first and qs % 2 == 0:
                        acw = max(acw, evQ[r])
                        wts.append(("dv", evK[r]))
                    wts.append(("ac", acw))
                    qsrc = A_QT + (r % 4) * 1024 + qlo - r * 1024
                    ev_S[lt] = P.emit("pe", lambda e, kc=kc, qsrc=qsrc, ncols=ncols, pb=pb: e.matmul(
                        pA[:, pb:pb + ncols],
                        BF(A_KT + kc * 128, 128), BF(qsrc, ncols),
                        start=True, stop=True), waits=wts, inc="pe")

                def ptwar(wts, lt):
                    sv = ev_summ.get(lt)
                    if sv is None:
                        wts.append(("dv", bar_dv))
                        if bar_gpc:
                            wts.append(("gpc", bar_gpc))
                    else:
                        wts.append(sv)
                    return wts

                def core_exp(lt, paired):
                    qs, kc, qlo, off, diag, first, last = tiles[lt]
                    pb = (lt % 2) * 512
                    pt = A_PT + (lt % 4) * 512
                    ncols = (QS - off) if not paired else 1024
                    wts = [("pe", ev_S[lt + 1] if paired else ev_S[lt])]
                    ptwar(wts, lt - 4)
                    if paired:
                        ptwar(wts, lt - 3)
                    ev = P.emit("ac", lambda e, ncols=ncols, pb=pb, pt=pt: e.activation(
                        BF(pt, ncols), pA[:, pb:pb + ncols],
                        AF.Exp, scale=sb_cv[:, 2:3]), waits=wts, inc="ac")
                    ev_exp[lt] = ev
                    if paired:
                        ev_exp[lt + 1] = ev

                def core_sums(lt):
                    qs, kc, qlo, off, diag, first, last = tiles[lt]
                    pt = A_PT + (lt % 4) * 512
                    ncols = QS - off
                    spar = qs % 2
                    j = tile_j(lt)
                    eng = "gp" if (j % 3 == 2 and not diag) else "dv"
                    sem = "gpc" if eng == "gp" else "dv"
                    base = (F_T2 if eng == "gp" else F_ACC) + spar * 512
                    if diag:
                        ev_mask[lt] = (sem, P.emit(eng, lambda e, pt=pt: e.tensor_mul(
                            BF(pt, 128), BF(pt, 128), sb_mask[:, :]),
                            waits=[("ac", ev_exp[lt])], inc=sem))
                        wts = []
                    else:
                        wts = [("ac", ev_exp[lt])]
                    if (qs, eng) not in sum_first:
                        sum_first[(qs, eng)] = lt
                        wts.append(("pe", rowmm_by_sumpar[spar]))
                        if eng == "gp":
                            wts.append(("dv", bar_dv))
                        fn = lambda e, off=off, ncols=ncols, pt=pt, base=base: e.tensor_copy(
                            FF(base + off, ncols), BF(pt, ncols))
                    else:
                        fn = lambda e, off=off, ncols=ncols, pt=pt, base=base: e.tensor_add(
                            FF(base + off, ncols), FF(base + off, ncols), BF(pt, ncols))
                    ev_summ[lt] = (sem, P.emit(eng, fn, waits=wts, inc=sem))
                    sum_last[(qs, eng)] = ev_summ[lt]

                def core_AV(lt):
                    qs, kc, qlo, off, diag, first, last = tiles[lt]
                    pt = A_PT + (lt % 4) * 512
                    ncols = QS - off
                    bank = (qs % 2) * 512
                    r = qs // 2
                    wts = [("ac", ev_exp[lt])]
                    dvw = 0
                    gpw = 0
                    if lt in ev_mask:
                        msem, mval = ev_mask[lt]
                        if msem == "dv":
                            dvw = mval
                        else:
                            gpw = mval
                    if first:
                        dvw = max(dvw, ztm_by_bank[qs % 2])
                        if qs % 2 == 0:
                            dvw = max(dvw, evV[r])
                    if dvw:
                        wts.append(("dv", dvw))
                    if gpw:
                        wts.append(("gpc", gpw))
                    ev = P.emit("pe", lambda e, kc=kc, off=off, ncols=ncols, pt=pt, bank=bank, first=first, last=last: e.matmul(
                        pB[:, bank + off:bank + off + ncols], BF(A_V + kc * 128, 128),
                        BF(pt, ncols), start=first, stop=last), waits=wts, inc="pe")
                    ev_avmm[lt] = ev
                    if last:
                        ev_avlast[qs] = ev
                    return ev

                def pe_hook(u):
                    if u in due_rowmm:
                        s = due_rowmm[u]
                        spar = s % 2
                        has_gp = (s, "gp") in sum_last
                        wts = [sum_last[(s, "dv")],
                               ("ac", ev_rbcopy.get(s - 1, bar_ac))]
                        if has_gp:
                            wts.append(sum_last[(s, "gp")])
                        ev_rowmm[s] = P.emit("pe", lambda e, spar=spar, hg=has_gp: e.matmul(
                            pD[0:1, :], ones_f[:, 0:1], FF(F_ACC + spar * 512, 512),
                            start=True, stop=not hg), waits=wts, inc="pe")
                        if has_gp:
                            ev_rowmm[s] = P.emit("pe", lambda e, spar=spar: e.matmul(
                                pD[0:1, :], ones_f[:, 0:1], FF(F_T2 + spar * 512, 512),
                                start=False, stop=True), inc="pe")
                        rowmm_by_sumpar[spar] = ev_rowmm[s]
                    if u in due_bcast:
                        s = due_bcast[u]
                        ev_bcast[s] = P.emit("pe", lambda e: e.matmul(
                            pD[:, :], onesr_f[0:1, :], rows[0:1, 0:512],
                            start=True, stop=True),
                            waits=[("ac", ev_rexp[s])], inc="pe")

                def ac_hook(t):
                    if t in due_accopy:
                        s = due_accopy[t]
                        ev_accopy[s] = P.emit("ac", lambda e: e.copy(
                            rows[0:1, 0:512], pD[0:1, :]),
                            waits=[("pe", ev_rowmm[s])], inc="ac")
                        P.emit("ac", lambda e: e.activation(
                            rows[0:1, 0:512], rows[0:1, 0:512], AF.Ln), inc="ac")
                        ev_rexp[s] = P.emit("ac", lambda e: e.activation(
                            rows[0:1, 0:512], rows[0:1, 0:512], AF.Exp,
                            scale=sb_cv[0:1, 5:6]), inc="ac")
                    if t in due_rbcopy:
                        s = due_rbcopy[t]
                        ev_rbcopy[s] = P.emit("ac", lambda e: e.copy(
                            FF(1024, 512), pD[:, :]),
                            waits=[("pe", ev_bcast[s])], inc="ac")

                def close_strip(s):
                    done_close.add(s)
                    spar = s % 2
                    zpar = s % 2
                    wts = [("pe", ev_avlast[s]), ("ac", ev_rbcopy[s]),
                           (f"zt{zpar}", zt_store_by_par[zpar])]
                    ev_zt = P.emit("dv", lambda e, spar=spar, zpar=zpar: e.tensor_mul(
                        BF(A_ZT + zpar * 1024, 512), pB[:, spar * 512:spar * 512 + 512],
                        FF(1024, 512)), waits=wts, inc="dv")
                    ztm_by_bank[spar] = ev_zt
                    ev_st = P.emit("sy", lambda e, h=h, s=s, zpar=zpar, j=lyr: e.dma_start(
                        out=za_in[j][s % 2].ap()[s // 2, h * 128:(h + 1) * 128, :],
                        in_=BF(A_ZT + zpar * 1024, 512)),
                        waits=[("dv", ev_zt)], inc=f"zt{zpar}", amt=16)
                    zt_store_by_par[zpar] = ev_st

                def dv_hook(t):
                    if t in due_close:
                        close_strip(due_close[t])

                if os.environ.get("KDRAIN"):
                    drain_through(7)
                pending_lead = None
                for lt in range(T_h):
                    qs = tiles[lt][0]
                    if tiles[lt][5] and qs % 2 == 0:
                        drain_through(qs // 2)
                    core_S(lt)
                    if lt == last_t[qs] and qs % 2 == 1:
                        pair_lastS[qs // 2] = P.cnt["pe"]
                    pump(2)
                    if lt >= 2:
                        core_AV(lt - 2)
                        pe_hook(lt - 2)
                        pump(2)
                    if pending_lead is not None:
                        core_exp(pending_lead, paired=True)
                        ac_hook(lt)
                        core_sums(pending_lead)
                        dv_hook(lt - 1)
                        core_sums(lt)
                        dv_hook(lt)
                        pending_lead = None
                    elif is_pair_lead(lt):
                        pending_lead = lt
                        ac_hook(lt)
                    else:
                        core_exp(lt, paired=False)
                        ac_hook(lt)
                        core_sums(lt)
                        dv_hook(lt)
                for u in (T_h - 2, T_h - 1):
                    core_AV(u)
                    pe_hook(u)
                drain_through(7)
                for s in sorted(last_t):
                    if s not in ev_rowmm:
                        pe_hook(last_t[s])
                    if s not in ev_accopy:
                        ac_hook(last_t[s] + 2)
                    if s not in ev_bcast:
                        pe_hook(last_t[s] + 2)
                    if s not in ev_rbcopy:
                        ac_hook(last_t[s] + 4)
                    if s not in done_close:
                        close_strip(s)
                ev_last_av = max(ev_avlast.values())

            if phase <= 3:
                return
            ev_ccz_p = []
            for par in range(2):
                ev_ccz_p.append(P.emit("gp", lambda e, j=lyr, par=par: e.collective_compute(
                    "AllToAll", ALU.bypass, replica_groups=[list(range(NC))],
                    ins=[za_in[j][par].ap().rearrange("s h t -> (s h) t")],
                    outs=[za_out[j][par].ap()]),
                    waits=[(f"zt{par}", P.cnt[f"zt{par}"])], inc=cc_z, amt="dflt"))

            ev_wo = {}
            for wh in range(2):
                ev_wo[wh] = P.emit("gp", lambda e, wh=wh, wo=wo: e.dma_start(
                    out=BF(A_WO, 32768).rearrange("p (c d) -> p c d", d=2048)[:, :, wh * 1024:(wh + 1) * 1024],
                    in_=wo.ap().rearrange("(c p) d -> p c d", p=128)[:, :, wh * 1024:(wh + 1) * 1024]),
                    waits=[("pe", P.cnt["pe"]), ("ac", P.cnt["ac"]), ("dv", P.cnt["dv"])],
                    inc="wo", amt=16)
            ev_my = {}
            for par in range(2):
                ev_my[par] = P.emit("sy", lambda e, j=lyr, par=par: e.dma_start(
                    out=BF(A_ZTMY, 16384).rearrange("p (c t) -> p c t", t=1024)[:, :, par * 512:(par + 1) * 512],
                    in_=za_out[j][par].ap().rearrange("(c p) t -> p c t", p=128)),
                    waits=[(cc_z, ev_ccz_p[par]), ("pe", ev_last_av)], inc="my", amt=16)

            if phase <= 4:
                return
            bar_ac, bar_dv = P.cnt["ac"], P.cnt["dv"]
            ev_evac = {}
            res_dv_last = 0
            for nh in range(2):
                for dm in range(DC):
                    ps = pA if dm % 2 == 0 else pB
                    for fc in range(DC):
                        wts = []
                        if fc == 0:
                            wts = [("my", ev_my[nh]), ("wo", ev_wo[1]),
                                   ("dv", max(ev_evac.get((nh, dm - 2), 0), bar_dv)),
                                   ("ac", bar_ac)]
                        P.emit("pe", lambda e, ps=ps, dm=dm, fc=fc, nh=nh: e.matmul(
                            ps[:, nh * 512:(nh + 1) * 512],
                            BF(A_WO + fc * 2048 + dm * 128, 128),
                            BF(A_ZTMY + fc * 1024 + nh * 512, 512),
                            start=(fc == 0), stop=(fc == DC - 1)),
                            waits=wts, inc="pe")
                    ev_mm = P.cnt["pe"]
                    res_dv_last = P.emit("dv", lambda e, ps=ps, dm=dm, nh=nh: e.tensor_add(
                        res[:, dm, nh * 512:(nh + 1) * 512],
                        res[:, dm, nh * 512:(nh + 1) * 512],
                        ps[:, nh * 512:(nh + 1) * 512]),
                        waits=[("pe", ev_mm)], inc="dv")
                    ev_evac[(nh, dm)] = res_dv_last
                    if nh == 1 and st["next_ln"]:
                        ln_chunk(dm, "dv", max(res_dv_last, ev_evac[(0, dm)]))
            st["res_dv"] = res_dv_last
            st["attn_done"] = (P.cnt["pe"], res_dv_last)

        def emit_ffn(pass_idx):
            ln_finish(None)
            ln_dv = st["ln_dv"]
            ev_row_cp = 0
            ev_xs = {}
            for th in range(2):
                for c in range(DC):
                    P.emit("pe", lambda e, c=c, th=th: e.matmul(
                        pD[0:1, :], ones_b[:, 0:1], BF(A_XLN + c * 1024 + th * 512, 512),
                        start=(c == 0), stop=(c == DC - 1)),
                        waits=[("dv", ln_dv), ("dv", ev_row_cp)], inc="pe")
                ev_mm = P.cnt["pe"]
                ev_row_cp = P.emit("dv", lambda e, th=th: e.tensor_copy(
                    rows[0:1, th * 512:(th + 1) * 512], pD[0:1, :]), waits=[("pe", ev_mm)], inc="dv")
                ev_sc = P.emit("dv", lambda e, th=th: e.tensor_scalar_mul(
                    rows[0:1, th * 512:(th + 1) * 512], rows[0:1, th * 512:(th + 1) * 512], sb_cv[0:1, 4:5]), inc="dv")
                ev_bc = P.emit("pe", lambda e, th=th: e.matmul(
                    pA[:, th * 512:(th + 1) * 512], onesr_f[0:1, :], rows[0:1, th * 512:(th + 1) * 512],
                    start=True, stop=True), waits=[("dv", ev_sc)], inc="pe")
                ev_xs[th] = P.emit("ac", lambda e, th=th: e.copy(
                    FF(F_ACC, 1024)[:, th * 512:(th + 1) * 512], pA[:, th * 512:(th + 1) * 512]),
                    waits=[("pe", ev_bc)], inc="ac")
            xs128B = FF(F_ACC, 1024)

            pre = st["ffn_pre"]
            ev_wld = {}
            ev_h_done = {}
            for ih in range(2):
                ev_gu_read = {}
                last_h = 0
                last_gu_mm = 0
                for ic in range(32):
                    gi = ih * 32 + ic
                    for nm, wsrc, off in (("g", wgT, A_WS), ("u", wuT, A_WS + 4096)):
                        key = (nm, ih, ic)
                        if pre is not None and ih == 0 and ic < 2 and (nm, ic) in st["ffn_pre"]:
                            ev_wld[key] = st["ffn_pre"][(nm, ic)]
                            continue
                        wts = []
                        if ic >= 2 or ih > 0:
                            wts.append(("pe", ev_gu_read.get(("mm", ic - 2), st.get("wd_done", 0))))
                        ev_wld[key] = P.emit("gp", lambda e, gi=gi, ic=ic, wsrc=wsrc, off=off: e.dma_start(
                            out=BF(off + (ic % 2) * 2048, 2048).rearrange("p (c i) -> p c i", i=128),
                            in_=wsrc.ap()[:, gi * 128:(gi + 1) * 128].rearrange("(c p) i -> p c i", p=128)),
                            waits=wts, inc=f"wg{ic % 2}", amt=16)
                    for nm, ps, off in (("g", pA, A_WS), ("u", pB, A_WS + 4096)):
                        for dc in range(DC):
                            wts = []
                            if dc == 0:
                                wts = [(f"wg{ic % 2}", ev_wld[(nm, ih, ic)]),
                                       ("dv", ev_gu_read.get((nm, ic - 1), 0)),
                                       ("ac", ev_xs[1])]
                            for nh in range(2):
                                P.emit("pe", lambda e, ps=ps, off=off, ic=ic, dc=dc, nh=nh: e.matmul(
                                    ps[:, nh * 512:(nh + 1) * 512],
                                    BF(off + (ic % 2) * 2048 + dc * 128, 128),
                                    BF(A_XLN + dc * 1024 + nh * 512, 512),
                                    start=(dc == 0), stop=(dc == DC - 1)),
                                    waits=wts, inc="pe")
                                wts = []
                        ev_gu_read[("mm_" + nm, ic)] = P.cnt["pe"]
                    ev_gu_read[("mm", ic)] = P.cnt["pe"]
                    last_gu_mm = P.cnt["pe"]
                    ev_tg = P.emit("dv", lambda e: e.tensor_sub(FF(F_T1, 1024), pA[:, :], xs128B),
                                   waits=[("pe", ev_gu_read[("mm_g", ic)])], inc="dv")
                    ev_gu_read[("g", ic)] = ev_tg
                    ev_sl = P.emit("ac", lambda e, gi=gi: e.activation(
                        FF(F_SG, 1024), FF(F_T1, 1024), AF.Silu, scale=sb_gs[:, gi:gi + 1]),
                        waits=[("dv", ev_tg)], inc="ac")
                    ev_tu = P.emit("dv", lambda e: e.tensor_sub(FF(F_T2, 1024), pB[:, :], xs128B),
                                   waits=[("pe", ev_gu_read[("mm_u", ic)])], inc="dv")
                    ev_gu_read[("u", ic)] = ev_tu
                    last_h = P.emit("dv", lambda e, gi=gi, ic=ic: e.scalar_tensor_tensor(
                        BF(A_HT + ic * 1024, 1024), FF(F_T2, 1024), sb_us[:, gi:gi + 1],
                        FF(F_SG, 1024), ALU.mult, ALU.mult),
                        waits=[("ac", ev_sl)], inc="dv")
                    ev_h_done[(ih, ic)] = last_h
                ev_row_cp2 = 0
                ev_hs = {}
                for th in range(2):
                    for ic in range(32):
                        wts = [("dv", ev_h_done[(ih, ic)])] if th == 0 else []
                        if ic == 0:
                            wts.append(("dv", ev_row_cp2))
                        P.emit("pe", lambda e, ic=ic, th=th: e.matmul(
                            pD[0:1, :], ones_b[:, 0:1], BF(A_HT + ic * 1024 + th * 512, 512),
                            start=(ic == 0), stop=(ic == 31)), waits=wts, inc="pe")
                    ev_mm = P.cnt["pe"]
                    ev_row_cp2 = P.emit("dv", lambda e, th=th: e.tensor_copy(
                        rows[0:1, 1024 + th * 512:1024 + (th + 1) * 512], pD[0:1, :]),
                        waits=[("pe", ev_mm)], inc="dv")
                    ev_sc = P.emit("dv", lambda e, th=th: e.tensor_scalar_mul(
                        rows[0:1, 1024 + th * 512:1024 + (th + 1) * 512],
                        rows[0:1, 1024 + th * 512:1024 + (th + 1) * 512], sb_cv[0:1, 4:5]), inc="dv")
                    ev_bc = P.emit("pe", lambda e, th=th: e.matmul(
                        pA[:, th * 512:(th + 1) * 512], onesr_f[0:1, :],
                        rows[0:1, 1024 + th * 512:1024 + (th + 1) * 512],
                        start=True, stop=True),
                        waits=[("dv", ev_sc), ("dv", ev_gu_read[("g", 31)])], inc="pe")
                    ev_hs[th] = P.emit("ac", lambda e, th=th: e.copy(
                        FF(F_ACC2, 1024)[:, th * 512:(th + 1) * 512], pA[:, th * 512:(th + 1) * 512]),
                        waits=[("pe", ev_bc)], inc="ac")
                hs128B = FF(F_ACC2, 1024)

                ev_wd = {}
                ev_devac = {}
                res_dv_last = 0
                for dm in range(DC):
                    wts = [("pe", last_gu_mm)]
                    if dm >= 2:
                        wts.append(("pe", ev_wd.get(("mm", dm - 2), 0)))
                    ev_wd[dm] = P.emit("gp", lambda e, ih=ih, dm=dm: e.dma_start(
                        out=BF(A_WS + (dm % 2) * 4096, 4096).rearrange("p (g i) -> p g i", i=128),
                        in_=wdT.ap().rearrange("(g p) d -> p g d", p=128)[:, ih * 32:(ih + 1) * 32, dm * 128:(dm + 1) * 128]),
                        waits=wts, inc=f"wd{dm % 2}", amt=16)
                    ps = pP if dm % 2 == 0 else pB
                    for ic in range(32):
                        wts = []
                        if ic == 0:
                            wts = [(f"wd{dm % 2}", ev_wd[dm]), ("dv", ev_devac.get(dm - 2, 0)),
                                   ("ac", ev_hs[1])]
                            if dm == 0 or dm == 1:
                                wts.append(("dv", ev_h_done[(ih, 31)]))
                        for nh in range(2):
                            P.emit("pe", lambda e, ps=ps, ih=ih, dm=dm, ic=ic, nh=nh: e.matmul(
                                ps[:, nh * 512:(nh + 1) * 512],
                                BF(A_WS + (dm % 2) * 4096 + ic * 128, 128),
                                BF(A_HT + ic * 1024 + nh * 512, 512),
                                start=(ic == 0), stop=(ic == 31)),
                                waits=wts, inc="pe")
                            wts = []
                    ev_wd[("mm", dm)] = P.cnt["pe"]
                    ev_mm = P.cnt["pe"]
                    ev_sub = P.emit("dv", lambda e, ps=ps: e.tensor_sub(FF(F_T2, 1024), ps[:, :], hs128B),
                                    waits=[("pe", ev_mm)], inc="dv")
                    res_dv_last = P.emit("dv", lambda e, dm=dm: e.scalar_tensor_tensor(
                        res[:, dm, :], FF(F_T2, 1024), sb_ds[:, dm:dm + 1], res[:, dm, :],
                        ALU.mult, ALU.add), inc="dv")
                    ev_devac[dm] = res_dv_last
                    if ih == 1 and st["next_ln"]:
                        ln_chunk(dm, "dv", res_dv_last)
                st["wd_done"] = ev_wd[("mm", DC - 1)]
            st["res_dv"] = res_dv_last

        import os
        _nsub = int(os.environ.get("KNSUB", "4"))
        _ph = int(os.environ.get("KPHASE", "9"))
        for q in range(4):
            ev_q = P.emit("sy", lambda e, q=q: e.dma_start(
                out=res[:, q * 4:(q + 1) * 4, :],
                in_=xT_in.ap().rearrange("(c p) t -> p c t", p=128)[:, q * 4:(q + 1) * 4, :]),
                inc="ld", amt=16)
            for c in range(q * 4, q * 4 + 4):
                ln_chunk(c, "ld", ev_q)
        emit_attn(0, _ph)
        if _nsub >= 2:
            emit_ffn(0)
        if _nsub >= 3:
            emit_attn(1)
        if _nsub >= 4:
            st["next_ln"] = False
            emit_ffn(1)

        if os.environ.get("KDBG"):
            w = [("dv", st["res_dv"]), ("pe", P.cnt["pe"]), ("ac", P.cnt["ac"])]
            P.emit("gp", lambda e: e.dma_start(
                out=outT.ap()[0:128, :], in_=abf[:, A_KT:A_KT + 1024]),
                waits=w, inc="out", amt=16)
            P.emit("gp", lambda e: e.dma_start(
                out=outT.ap()[128:256, :], in_=abf[:, A_QT + 3 * 1024:A_QT + 4 * 1024]),
                inc="out", amt=16)
            P.emit("gp", lambda e: e.dma_start(
                out=outT.ap()[256:384, :], in_=abf[:, A_V:A_V + 1024]),
                inc="out", amt=16)
        else:
            P.emit("sy", lambda e: e.dma_start(
                out=outT.ap().rearrange("(c p) t -> p c t", p=128), in_=res[:, :, :]),
                waits=[("dv", st["res_dv"])], inc="out", amt=16)
        ev_out = P.cnt["out"]
        P.emit("sy", lambda e: e.wait_ge(s_out, ev_out))

        def realize(eng_name, eng):
            hwm = {}
            for fn, waits, incsem, after, amt in P.ops[eng_name]:
                for snm, val in waits:
                    if val and val > hwm.get(snm, 0):
                        eng.wait_ge(SEMS[snm], val)
                        hwm[snm] = val
                ins = fn(eng)
                if incsem is not None:
                    if amt == "dflt":
                        ins.then_inc(SEMS[incsem])
                    else:
                        ins.then_inc(SEMS[incsem], amt)

        @block.tensor
        def _(t):
            realize("pe", t)

        @block.scalar
        def _(s):
            realize("ac", s)

        @block.vector
        def _(v):
            realize("dv", v)

        @block.gpsimd
        def _(g):
            with allow_ici_all_to_all():
                realize("gp", g)

        @block.sync
        def _(sy):
            realize("sy", sy)

    return nc


_NC_CACHE = None


def make_in_maps(L0_attn_a_x, L0_attn_a_W_K, L0_attn_a_W_Q, L0_attn_a_W_V, L0_attn_a_W_O,
                 L0_attn_b_w_qkv, L0_attn_b_w_out,
                 L0_ffn_gate_w, L0_ffn_gate_scale, L0_ffn_up_w, L0_ffn_up_scale,
                 L0_ffn_down_w, L0_ffn_down_scale):
    x = np.asarray(L0_attn_a_x, dtype=np.float32).reshape(S, D)
    W_K = np.asarray(L0_attn_a_W_K, dtype=np.float32)
    W_Q = np.asarray(L0_attn_a_W_Q, dtype=np.float32)
    W_V = np.asarray(L0_attn_a_W_V, dtype=np.float32)
    W_O = np.asarray(L0_attn_a_W_O, dtype=np.float32)
    w_qkv = np.asarray(L0_attn_b_w_qkv, dtype=np.float32)
    w_out = np.asarray(L0_attn_b_w_out, dtype=np.float32)
    gw = np.asarray(L0_ffn_gate_w, dtype=np.int32)
    uw = np.asarray(L0_ffn_up_w, dtype=np.int32)
    dw = np.asarray(L0_ffn_down_w, dtype=np.int32)
    gs = np.asarray(L0_ffn_gate_scale, dtype=np.float32)
    us = np.asarray(L0_ffn_up_scale, dtype=np.float32)
    ds = np.asarray(L0_ffn_down_scale, dtype=np.float32)

    C = np.ascontiguousarray
    woT_a = C(W_O.T)
    woT_b = C(w_out)
    wgT = C(gw.T)
    wuT = C(uw.T)
    wdT = C(dw.T)
    gsT = C(gs.reshape(IC, 128).T)
    usT = C(us.reshape(IC, 128).T)
    dsT = C(ds.reshape(DC, 128).T)
    idn = np.eye(128, dtype=ml_dtypes.bfloat16)
    mask = np.triu(np.ones((128, 128))).astype(ml_dtypes.bfloat16)
    cvec = np.tile(np.array([[EPS, -0.5, INV_SQRT_HD, 1.0 / D, 128.0, -1.0, 0.0, 0.0]],
                            dtype=np.float32), (128, 1))

    in_maps = []
    for c in range(NC):
        h0, h1 = 2 * c, 2 * c + 1
        m = dict(
            xT_in=C(x[c * T:(c + 1) * T, :].T),
            wqT_a=C(np.stack([W_Q[h0].T, W_Q[h1].T])),
            wkT_a=C(np.stack([W_K[h0].T, W_K[h1].T])),
            wvT_a=C(np.stack([W_V[h0].T, W_V[h1].T])),
            woT_a=woT_a,
            wqT_b=C(np.stack([w_qkv[:, h0 * HD:(h0 + 1) * HD], w_qkv[:, h1 * HD:(h1 + 1) * HD]])),
            wkT_b=C(np.stack([w_qkv[:, D + h0 * HD:D + (h0 + 1) * HD], w_qkv[:, D + h1 * HD:D + (h1 + 1) * HD]])),
            wvT_b=C(np.stack([w_qkv[:, 2 * D + h0 * HD:2 * D + (h0 + 1) * HD], w_qkv[:, 2 * D + h1 * HD:2 * D + (h1 + 1) * HD]])),
            woT_b=woT_b,
            wgT=wgT, wuT=wuT, wdT=wdT, gsT=gsT, usT=usT, dsT=dsT,
            idn=idn, mask=mask,
            cvec=cvec,
        )
        in_maps.append(m)
    return in_maps


def kernel(**inputs):
    global _NC_CACHE
    in_maps = make_in_maps(**inputs)
    if _NC_CACHE is None:
        _NC_CACHE = build_program()
    res = run_bass_kernel_spmd(_NC_CACHE, in_maps, core_ids=list(range(NC)))
    out = np.empty((S, D), dtype=np.float32)
    for c in range(NC):
        out[c * T:(c + 1) * T, :] = res.results[c]["outT"].T
    return out.reshape(1, S, D)


# version 17
# speedup vs baseline: 1.0116x; 1.0116x over previous
import os
import numpy as np
import ml_dtypes
import concourse.bass as bass
import concourse.mybir as mybir
from concourse.bass_utils import run_bass_kernel_spmd
from concourse.aot_env import allow_ici_all_to_all

F32 = mybir.dt.float32
BF16 = mybir.dt.bfloat16
I32 = mybir.dt.int32
AF = mybir.ActivationFunctionType
ALU = mybir.AluOpType

NC = 8
S = 8192
T = 1024
D = 2048
DC = D // 128
HD = 128
NHL = 2
FI = 8192
IC = FI // 128
QS = 512
NQS = S // QS
INV_SQRT_HD = 1.0 / np.sqrt(128.0)
EPS = 1e-5

A_XLN = 0
A_QT = 0
A_KT = 4096
A_V = 12288
A_WO = 0
A_HT = 16384
A_WQKV = 20480
A_ZTMY = 32768
A_XG = 32768
A_PT = 45056
A_ZT = 47104
A_WS = 49152
A_END = 57344

F_ACC = 0
F_ACC2 = 1024
F_SG = 2048
F_T1 = 3072
F_T2 = 4096
F_RSTDB = 0
F_MRSTDB = 1024
F_END = 5120


class Plan:
    ENGS = ("pe", "ac", "dv", "gp", "sy")

    def __init__(self):
        self.ops = {e: [] for e in self.ENGS}
        self.cnt = {}

    def post(self, sem, amt):
        self.cnt[sem] = self.cnt.get(sem, 0) + amt
        return self.cnt[sem]

    def emit(self, eng, fn, waits=(), inc=None, amt=1):
        after = self.post(inc, 1 if amt == "dflt" else amt) if inc is not None else None
        self.ops[eng].append((fn, tuple(waits), inc, after, amt))
        return after


def build_program():
    nc = bass.Bass("TRN2", target_bir_lowering=False, detect_race_conditions=False)

    xT_in = nc.dram_tensor("xT_in", [D, T], F32, kind="ExternalInput")
    wqT_a = nc.dram_tensor("wqT_a", [NHL, D, HD], F32, kind="ExternalInput")
    wkT_a = nc.dram_tensor("wkT_a", [NHL, D, HD], F32, kind="ExternalInput")
    wvT_a = nc.dram_tensor("wvT_a", [NHL, D, HD], F32, kind="ExternalInput")
    woT_a = nc.dram_tensor("woT_a", [D, D], F32, kind="ExternalInput")
    wqT_b = nc.dram_tensor("wqT_b", [NHL, D, HD], F32, kind="ExternalInput")
    wkT_b = nc.dram_tensor("wkT_b", [NHL, D, HD], F32, kind="ExternalInput")
    wvT_b = nc.dram_tensor("wvT_b", [NHL, D, HD], F32, kind="ExternalInput")
    woT_b = nc.dram_tensor("woT_b", [D, D], F32, kind="ExternalInput")
    wgT = nc.dram_tensor("wgT", [D, FI], I32, kind="ExternalInput")
    wuT = nc.dram_tensor("wuT", [D, FI], I32, kind="ExternalInput")
    wdT = nc.dram_tensor("wdT", [FI, D], I32, kind="ExternalInput")
    gsT = nc.dram_tensor("gsT", [128, IC], F32, kind="ExternalInput")
    usT = nc.dram_tensor("usT", [128, IC], F32, kind="ExternalInput")
    dsT = nc.dram_tensor("dsT", [128, DC], F32, kind="ExternalInput")
    cvec_i = nc.dram_tensor("cvec", [128, 8], F32, kind="ExternalInput")
    idn_i = nc.dram_tensor("idn", [128, 128], BF16, kind="ExternalInput")
    mask_i = nc.dram_tensor("mask", [128, 128], BF16, kind="ExternalInput")
    outT = nc.dram_tensor("outT", [D, T], F32, kind="ExternalOutput")

    xg_in = [nc.dram_tensor(f"xg_in{j}", [D, T], BF16) for j in range(2)]
    xg_out = [nc.dram_tensor(f"xg_out{j}", [NC * D, T], BF16, addr_space="Shared")
              for j in range(2)]
    za_in = [[nc.dram_tensor(f"za_in{j}_{par}", [NC, NHL * HD, T // 2], BF16)
              for par in range(2)] for j in range(2)]
    za_out = [[nc.dram_tensor(f"za_out{j}_{par}", [D, T // 2], BF16)
               for par in range(2)] for j in range(2)]

    from contextlib import ExitStack
    with ExitStack() as ctx:
        E = ctx.enter_context
        abf = E(nc.sbuf_tensor("abf", [128, A_END], BF16))
        af = E(nc.sbuf_tensor("af", [128, F_END], F32))
        res = E(nc.sbuf_tensor("res", [128, DC, T], F32))
        rows = E(nc.sbuf_tensor("rows", [1, 2048], F32))
        sb_idn = E(nc.sbuf_tensor("sb_idn", [128, 128], BF16))
        sb_mask = E(nc.sbuf_tensor("sb_mask", [128, 128], BF16))
        sb_gs = E(nc.sbuf_tensor("sb_gs", [128, IC], F32))
        sb_us = E(nc.sbuf_tensor("sb_us", [128, IC], F32))
        sb_ds = E(nc.sbuf_tensor("sb_ds", [128, DC], F32))
        sb_cv = E(nc.sbuf_tensor("sb_cv", [128, 8], F32))
        ones_f = E(nc.sbuf_tensor("ones_f", [128, 1], F32))
        ones_b = E(nc.sbuf_tensor("ones_b", [128, 1], BF16))
        onesr_f = E(nc.sbuf_tensor("onesr_f", [1, 128], F32))
        pA = E(nc.psum_tensor("pA", [128, 1024], F32))
        pB = E(nc.psum_tensor("pB", [128, 1024], F32))
        pP = E(nc.psum_tensor("pP", [128, 1024], F32))
        pD = E(nc.psum_tensor("pD", [128, 512], F32))
        pT = E(nc.psum_tensor("pT", [128, 1024], BF16))
        dma_sem_names = ["ld", "xgw", "xg0", "xg1", "xg2", "zt0", "zt1", "my",
                         "wqkv", "wo", "wg0", "wg1", "wd0", "wd1"]
        dma_sems = {n: E(nc.semaphore(f"s_{n}")) for n in dma_sem_names}
        s_pe = E(nc.semaphore("s_pe"))
        s_gpc = E(nc.semaphore("s_gpc"))
        s_ac = E(nc.semaphore("s_ac"))
        s_dv = E(nc.semaphore("s_dv"))
        s_cc = [E(nc.semaphore(f"s_cc{j}")) for j in range(4)]
        s_out = E(nc.semaphore("s_out"))
        block = E(nc.Block())

        P = Plan()
        SEMS = {"pe": s_pe, "ac": s_ac, "dv": s_dv, "gpc": s_gpc,
                "cc0": s_cc[0], "cc1": s_cc[1], "cc2": s_cc[2], "cc3": s_cc[3],
                "out": s_out, **dma_sems}

        def BF(c0, n):
            return abf[:, c0:c0 + n]

        def FF(c0, n):
            return af[:, c0:c0 + n]

        for name, dst, src in (("idn", sb_idn, idn_i), ("mask", sb_mask, mask_i),
                               ("gs", sb_gs, gsT), ("us", sb_us, usT), ("ds", sb_ds, dsT),
                               ("cv", sb_cv, cvec_i)):
            P.emit("sy", lambda e, d=dst, s=src: e.dma_start(out=d[:, :], in_=s.ap()),
                   inc="ld", amt=16)
        ev_consts = P.cnt["ld"]
        P.emit("dv", lambda e: e.memset(ones_f[:, :], 1.0), inc="dv")
        P.emit("dv", lambda e: e.memset(ones_b[:, :], 1.0), inc="dv")
        P.emit("dv", lambda e: e.memset(onesr_f[:, :], 1.0), inc="dv")
        ev_ones = P.cnt["dv"]

        st = {"res_dv": 0, "lnp": {}, "next_ln": True}

        def ln_chunk(c, wsem, wval):
            stl = st["lnp"]
            acc, acc2 = FF(F_ACC, 1024), FF(F_T1, 1024)
            sq = FF(F_SG, 1024)
            w = [(wsem, wval)] if wsem else []
            if c == 0:
                stl["acc"] = P.emit("dv", lambda e: e.tensor_copy(acc, res[:, 0, :]),
                                    waits=w + [("ld", ev_consts)], inc="dv")
            else:
                stl["acc"] = P.emit("dv", lambda e, c=c: e.tensor_add(acc, acc, res[:, c, :]),
                                    waits=w, inc="dv")
            wts = list(w)
            if c > 0:
                wts.append(("dv", stl["sqa"]))
            ev_s = P.emit("ac", lambda e, c=c: e.activation(sq, res[:, c, :], AF.Square),
                          waits=wts, inc="ac")
            if c == 0:
                stl["sqa"] = P.emit("dv", lambda e: e.tensor_copy(acc2, sq),
                                    waits=[("ac", ev_s)], inc="dv")
            else:
                stl["sqa"] = P.emit("dv", lambda e, c=c: e.tensor_add(acc2, acc2, sq),
                                    waits=[("ac", ev_s)], inc="dv")

        def ln_finish(write_xg_idx):
            acc, acc2 = FF(F_ACC, 1024), FF(F_T1, 1024)
            ev_acc = st["lnp"]["acc"]
            ev_acc2 = st["lnp"]["sqa"]
            ln_ac_bar = P.cnt.get("ac", 0)
            r_sum = [rows[0:1, 0:512], rows[0:1, 1024:1536]]
            r_sq = [rows[0:1, 512:1024], rows[0:1, 1536:2048]]
            ev_row_cp = 0
            for th in range(2):
                ev_mm = P.emit("pe", lambda e, th=th: e.matmul(
                    pD[0:1, :], ones_f[:, 0:1], acc[:, th * 512:(th + 1) * 512],
                    start=True, stop=True),
                    waits=[("dv", ev_acc), ("dv", ev_row_cp), ("ac", ln_ac_bar)], inc="pe")
                ev_row_cp = P.emit("dv", lambda e, th=th: e.tensor_copy(r_sum[th], pD[0:1, :]),
                                   waits=[("pe", ev_mm)], inc="dv")
                ev_mm = P.emit("pe", lambda e, th=th: e.matmul(
                    pD[0:1, :], ones_f[:, 0:1], acc2[:, th * 512:(th + 1) * 512],
                    start=True, stop=True),
                    waits=[("dv", ev_acc2), ("dv", ev_row_cp)], inc="pe")
                ev_row_cp = P.emit("dv", lambda e, th=th: e.tensor_copy(r_sq[th], pD[0:1, :]),
                                   waits=[("pe", ev_mm)], inc="dv")
            tmp = af[0:1, 0:512]
            ev_bc = {}
            for th in range(2):
                P.emit("dv", lambda e, th=th: e.tensor_scalar_mul(r_sum[th], r_sum[th], sb_cv[0:1, 3:4]), inc="dv")
                P.emit("dv", lambda e, th=th: e.tensor_scalar_mul(r_sq[th], r_sq[th], sb_cv[0:1, 3:4]), inc="dv")
                P.emit("dv", lambda e, th=th: e.tensor_mul(tmp, r_sum[th], r_sum[th]), inc="dv")
                ev_var = P.emit("dv", lambda e, th=th: e.tensor_sub(r_sq[th], r_sq[th], tmp), inc="dv")
                ev_ln = P.emit("ac", lambda e, th=th: e.activation(r_sq[th], r_sq[th], AF.Ln, bias=sb_cv[0:1, 0:1]),
                               waits=[("dv", ev_var)], inc="ac")
                ev_rstd = P.emit("ac", lambda e, th=th: e.activation(r_sq[th], r_sq[th], AF.Exp, scale=sb_cv[0:1, 1:2]),
                                 inc="ac")
                ev_mr = P.emit("dv", lambda e, th=th: e.tensor_mul(r_sum[th], r_sum[th], r_sq[th]),
                               waits=[("ac", ev_rstd)], inc="dv")
                ev1 = P.emit("pe", lambda e, th=th: e.matmul(
                    pA[:, th * 512:(th + 1) * 512], onesr_f[0:1, :], r_sq[th],
                    start=True, stop=True), waits=[("ac", ev_rstd), ("dv", ev_ones)], inc="pe")
                ev2 = P.emit("pe", lambda e, th=th: e.matmul(
                    pB[:, th * 512:(th + 1) * 512], onesr_f[0:1, :], r_sum[th],
                    start=True, stop=True), waits=[("dv", ev_mr)], inc="pe")
                P.emit("ac", lambda e, th=th: e.copy(
                    FF(F_RSTDB, 1024)[:, th * 512:(th + 1) * 512], pA[:, th * 512:(th + 1) * 512]),
                    waits=[("pe", ev1)], inc="ac")
                ev_bc[th] = P.emit("ac", lambda e, th=th: e.copy(
                    FF(F_MRSTDB, 1024)[:, th * 512:(th + 1) * 512], pB[:, th * 512:(th + 1) * 512]),
                    waits=[("pe", ev2)], inc="ac")
            t1 = [FF(F_SG, 1024), FF(F_T2, 1024)]
            last_norm = 0
            ev_xgw = []
            for c in range(DC):
                P.emit("dv", lambda e, c=c: e.tensor_mul(t1[c % 2], res[:, c, :], FF(F_RSTDB, 1024)),
                       waits=[("ac", ev_bc[1])], inc="dv")
                last_norm = P.emit("dv", lambda e, c=c: e.tensor_sub(
                    BF(A_XLN + c * 1024, 1024), t1[c % 2], FF(F_MRSTDB, 1024)), inc="dv")
                if write_xg_idx is not None and c == DC - 1:
                    ev_xgw.append(P.emit("sy", lambda e, j=write_xg_idx: e.dma_start(
                        out=xg_in[j].ap().rearrange("(c p) t -> p c t", p=128),
                        in_=abf[:, 0:DC * 1024].rearrange("p (c t) -> p c t", t=1024)),
                        waits=[("dv", last_norm)], inc="xgw", amt=16))
            st["ln_dv"] = last_norm
            st["ln_ac"] = ev_bc[1]
            if write_xg_idx is not None:
                return ev_xgw
            return None

        vcp_list = [0]
        zt_store_by_par = {0: 0, 1: 0}
        rowmm_by_sumpar = {0: 0, 1: 0}
        ztm_by_bank = {0: 0, 1: 0}
        xg_consumed = {0: 0, 1: 0, 2: 0}

        def emit_attn(lyr, phase=9):
            cc_g = f"cc{lyr * 2}"
            cc_z = f"cc{lyr * 2 + 1}"
            wq, wk, wv = (wqT_a, wkT_a, wvT_a) if lyr == 0 else (wqT_b, wkT_b, wvT_b)
            wo = woT_a if lyr == 0 else woT_b

            wq_waits = []
            if lyr == 1:
                wq_waits = [("pe", P.cnt["pe"]), ("dv", P.cnt["dv"])]
            ev_wqkv = {}
            for h in range(NHL):
                for pi, wsrc in enumerate((wq, wk, wv)):
                    ev_wqkv[(h, pi)] = P.emit("gp", lambda e, h=h, pi=pi, wsrc=wsrc: e.dma_start(
                        out=BF(A_WQKV + h * 6144 + pi * 2048, 2048).rearrange("p (c i) -> p c i", i=128),
                        in_=wsrc.ap()[h, :, :].rearrange("(c p) i -> p c i", p=128)),
                        waits=wq_waits, inc="wqkv", amt=16)
                    wq_waits = []
            ev_wqkv_all = P.cnt["wqkv"]

            ev_xgw = ln_finish(lyr)
            ev_ccg = P.emit("gp", lambda e, j=lyr: e.collective_compute(
                "AllGather", ALU.bypass, replica_groups=[list(range(NC))],
                ins=[xg_in[j].ap().opt()], outs=[xg_out[j].ap().opt()]),
                waits=[("xgw", ev_xgw[0])], inc=cc_g, amt="dflt")

            pre_waits = [("pe", P.cnt["pe"])] if lyr == 1 else []
            ev_pre = {}
            for ic in range(2):
                for nm, wsrc, off in (("g", wgT, A_WS), ("u", wuT, A_WS + 4096)):
                    ev_pre[(nm, ic)] = P.emit("gp", lambda e, ic=ic, wsrc=wsrc, off=off: e.dma_start(
                        out=BF(off + ic * 2048, 2048).rearrange("p (c i) -> p c i", i=128),
                        in_=wsrc.ap()[:, ic * 128:(ic + 1) * 128].rearrange("(c p) i -> p c i", p=128)),
                        waits=pre_waits, inc=f"wg{ic % 2}", amt=16)
                    pre_waits = []
            st["ffn_pre"] = ev_pre
            if phase <= 1:
                P.emit("gp", lambda e: e.engine_nop(), waits=[(cc_g, 1)])
                return

            entry_dv = P.cnt["dv"]
            ztm_by_bank[0] = max(ztm_by_bank[0], entry_dv)
            ztm_by_bank[1] = max(ztm_by_bank[1], entry_dv)
            bank_free = {0: ("dv", entry_dv), 1: ("dv", entry_dv)}
            stage_tr_prev = [0]
            ev_last_av = 0
            for h in range(NHL):
                bar_pe, bar_ac, bar_dv = P.cnt["pe"], P.cnt["ac"], P.cnt["dv"]
                bar_gpc = P.cnt.get("gpc", 0)
                wbase = A_WQKV + h * 6144
                evQ = {}
                evK = {}
                evV = {}
                pair_lastS = {}
                first_prod = [True]

                def make_prod(r):
                    def gen():
                        ld = {}

                        def emit_load(i):
                            nh, fh = ((0, 0), (0, 1), (1, 0), (1, 1))[i]
                            b = (r * 4 + i) % 3
                            w = [(cc_g, ev_ccg), ("pe", xg_consumed[b])]
                            ld[(nh, fh)] = (b, P.emit("sy", lambda e, r=r, nh=nh, fh=fh, b=b, j=lyr: e.dma_start(
                                out=BF(A_XG + b * 4096, 4096).rearrange("p (c t) -> p c t", t=512),
                                in_=xg_out[j].ap()[r * D + fh * 1024:r * D + fh * 1024 + 1024, nh * 512:(nh + 1) * 512]
                                    .rearrange("(c p) t -> p c t", p=128)),
                                waits=w, inc=f"xg{b}", amt=16))
                        for i in range(3):
                            emit_load(i)
                        ev_stage = {}
                        for g, (nh, pi) in enumerate((n, p) for n in (0, 1) for p in (0, 1, 2)):
                            bk = g % 2
                            for dc in range(DC):
                                fh = dc // 8
                                b, ev_ld = ld[(nh, fh)]
                                wts = []
                                if dc == 0:
                                    wts = [(f"xg{b}", ev_ld), ("wqkv", ev_wqkv_all)]
                                    fe, fv = bank_free[bk]
                                    if fv:
                                        wts.append((fe, fv))
                                    if first_prod[0]:
                                        wts += [("ac", bar_ac), ("dv", bar_dv)]
                                        first_prod[0] = False
                                elif dc == 8:
                                    wts = [(f"xg{b}", ev_ld)]
                                P.emit("pe", lambda e, bk=bk, pi=pi, dc=dc, b=b, wbase=wbase: e.matmul(
                                    pP[:, bk * 512:(bk + 1) * 512],
                                    BF(wbase + pi * 2048 + dc * 128, 128),
                                    BF(A_XG + b * 4096 + (dc % 8) * 512, 512),
                                    start=(dc == 0), stop=(dc == DC - 1)),
                                    waits=wts, inc="pe")
                                if pi == 2 and dc == 7:
                                    xg_consumed[ld[(nh, 0)][0]] = P.cnt["pe"]
                                    if nh == 0:
                                        emit_load(3)
                                if pi == 2 and dc == 15:
                                    xg_consumed[ld[(nh, 1)][0]] = P.cnt["pe"]
                                yield
                            ev_mm = P.cnt["pe"]
                            if pi == 0:
                                wts = [("pe", max(ev_mm, pair_lastS.get(r - 4, bar_pe)))]
                                evq = P.emit("ac", lambda e, r=r, nh=nh, bk=bk: e.copy(
                                    BF(A_QT + (r % 4) * 1024 + nh * 512, 512),
                                    pP[:, bk * 512:(bk + 1) * 512]), waits=wts, inc="ac")
                                evQ[r] = evq
                                bank_free[bk] = ("ac", evq)
                            elif pi == 1:
                                evk = P.emit("dv", lambda e, r=r, nh=nh, bk=bk: e.tensor_copy(
                                    BF(A_KT + r * 1024 + nh * 512, 512),
                                    pP[:, bk * 512:(bk + 1) * 512]),
                                    waits=[("pe", max(ev_mm, bar_pe))], inc="dv")
                                evK[r] = evk
                                bank_free[bk] = ("dv", evk)
                            else:
                                wts = [("pe", max(ev_mm, stage_tr_prev[0]))]
                                evs = P.emit("ac", lambda e, nh=nh, bk=bk: e.copy(
                                    BF(A_ZT + 512 + nh * 1024, 512),
                                    pP[:, bk * 512:(bk + 1) * 512]), waits=wts, inc="ac")
                                ev_stage[nh] = evs
                                bank_free[bk] = ("ac", evs)
                        ev_tr = 0
                        for tt in range(8):
                            src = A_ZT + 512 + (tt // 4) * 1024 + (tt % 4) * 128
                            wts = [("ac", ev_stage[tt // 4]), ("dv", vcp_list[-1]),
                                   ("ld", ev_consts)]
                            ev_tr = P.emit("pe", lambda e, src=src, tt=tt: e.transpose(
                                pT[:, tt * 128:(tt + 1) * 128], BF(src, 128), sb_idn[:, :]),
                                waits=wts, inc="pe")
                            yield
                        stage_tr_prev[0] = ev_tr
                        evV[r] = P.emit("dv", lambda e, r=r: e.tensor_copy(
                            BF(A_V + r * 1024, 1024), pT[:, :]),
                            waits=[("pe", ev_tr)], inc="dv")
                        vcp_list.append(evV[r])
                    return gen()

                gens = [make_prod(r) for r in range(8)]
                gi = [0]

                def pump(n):
                    while n > 0 and gi[0] < 8:
                        try:
                            next(gens[gi[0]])
                            n -= 1
                        except StopIteration:
                            gi[0] += 1

                def drain_through(r):
                    while gi[0] <= r:
                        try:
                            next(gens[gi[0]])
                        except StopIteration:
                            gi[0] += 1

                tiles = []
                for qs in range(NQS):
                    nkc = (qs + 1) * 4
                    for j, kc in enumerate(range(nkc)):
                        qlo = max(qs * QS, kc * 128)
                        tiles.append((qs, kc, qlo, qlo - qs * QS,
                                      kc * 128 >= qs * QS, j == 0, j == nkc - 1))
                T_h = len(tiles)
                last_t = {}
                for lt, td in enumerate(tiles):
                    if td[6]:
                        last_t[td[0]] = lt
                due_rowmm = {last_t[s]: s for s in last_t}
                due_accopy = {last_t[s] + 2: s for s in last_t}
                due_bcast = {last_t[s] + 2: s for s in last_t}
                due_rbcopy = {last_t[s] + 4: s for s in last_t}
                due_close = {last_t[s] + 8: s for s in last_t}
                ev_S = {}
                ev_exp = {}
                ev_avmm = {}
                ev_summ = {}
                ev_mask = {}
                ev_rowmm = {}
                ev_accopy = {}
                ev_rexp = {}
                ev_bcast = {}
                ev_rbcopy = {}
                ev_avlast = {}
                done_close = set()
                sum_first = {}
                sum_last = {}

                def tile_j(lt):
                    qs = tiles[lt][0]
                    return lt - (last_t[qs] - 4 * (qs + 1) + 1)

                def is_pair_lead(lt):
                    if lt % 2 != 0 or lt + 1 >= T_h:
                        return False
                    a, b = tiles[lt], tiles[lt + 1]
                    return a[0] == b[0] and a[3] == 0 and b[3] == 0 and not (a[4] or b[4])

                def core_S(lt):
                    qs, kc, qlo, off, diag, first, last = tiles[lt]
                    pb = (lt % 2) * 512
                    r = qs // 2
                    ncols = QS - off
                    acw = max(ev_exp.get(lt - 2, 0), bar_ac if lt < 2 else 0)
                    wts = []
                    if first and qs % 2 == 0:
                        acw = max(acw, evQ[r])
                        wts.append(("dv", evK[r]))
                    wts.append(("ac", acw))
                    qsrc = A_QT + (r % 4) * 1024 + qlo - r * 1024
                    ev_S[lt] = P.emit("pe", lambda e, kc=kc, qsrc=qsrc, ncols=ncols, pb=pb: e.matmul(
                        pA[:, pb:pb + ncols],
                        BF(A_KT + kc * 128, 128), BF(qsrc, ncols),
                        start=True, stop=True), waits=wts, inc="pe")

                def ptwar(wts, lt):
                    sv = ev_summ.get(lt)
                    if sv is None:
                        wts.append(("dv", bar_dv))
                        if bar_gpc:
                            wts.append(("gpc", bar_gpc))
                    else:
                        wts.append(sv)
                    return wts

                def core_exp(lt, paired):
                    qs, kc, qlo, off, diag, first, last = tiles[lt]
                    pb = (lt % 2) * 512
                    pt = A_PT + (lt % 4) * 512
                    ncols = (QS - off) if not paired else 1024
                    wts = [("pe", ev_S[lt + 1] if paired else ev_S[lt])]
                    ptwar(wts, lt - 4)
                    if paired:
                        ptwar(wts, lt - 3)
                    ev = P.emit("ac", lambda e, ncols=ncols, pb=pb, pt=pt: e.activation(
                        BF(pt, ncols), pA[:, pb:pb + ncols],
                        AF.Exp, scale=sb_cv[:, 2:3]), waits=wts, inc="ac")
                    ev_exp[lt] = ev
                    if paired:
                        ev_exp[lt + 1] = ev

                def core_sums(lt):
                    qs, kc, qlo, off, diag, first, last = tiles[lt]
                    pt = A_PT + (lt % 4) * 512
                    ncols = QS - off
                    spar = qs % 2
                    j = tile_j(lt)
                    eng = "dv"
                    sem = "gpc" if eng == "gp" else "dv"
                    base = (F_T2 if eng == "gp" else F_ACC) + spar * 512
                    if diag:
                        ev_mask[lt] = (sem, P.emit(eng, lambda e, pt=pt: e.tensor_mul(
                            BF(pt, 128), BF(pt, 128), sb_mask[:, :]),
                            waits=[("ac", ev_exp[lt])], inc=sem))
                        wts = []
                    else:
                        wts = [("ac", ev_exp[lt])]
                    if (qs, eng) not in sum_first:
                        sum_first[(qs, eng)] = lt
                        wts.append(("pe", rowmm_by_sumpar[spar]))
                        if eng == "gp":
                            wts.append(("dv", bar_dv))
                        fn = lambda e, off=off, ncols=ncols, pt=pt, base=base: e.tensor_copy(
                            FF(base + off, ncols), BF(pt, ncols))
                    else:
                        fn = lambda e, off=off, ncols=ncols, pt=pt, base=base: e.tensor_add(
                            FF(base + off, ncols), FF(base + off, ncols), BF(pt, ncols))
                    ev_summ[lt] = (sem, P.emit(eng, fn, waits=wts, inc=sem))
                    sum_last[(qs, eng)] = ev_summ[lt]

                def core_AV(lt):
                    qs, kc, qlo, off, diag, first, last = tiles[lt]
                    pt = A_PT + (lt % 4) * 512
                    ncols = QS - off
                    bank = (qs % 2) * 512
                    r = qs // 2
                    wts = [("ac", ev_exp[lt])]
                    dvw = 0
                    gpw = 0
                    if lt in ev_mask:
                        msem, mval = ev_mask[lt]
                        if msem == "dv":
                            dvw = mval
                        else:
                            gpw = mval
                    if first:
                        dvw = max(dvw, ztm_by_bank[qs % 2])
                        if qs % 2 == 0:
                            dvw = max(dvw, evV[r])
                    if dvw:
                        wts.append(("dv", dvw))
                    if gpw:
                        wts.append(("gpc", gpw))
                    ev = P.emit("pe", lambda e, kc=kc, off=off, ncols=ncols, pt=pt, bank=bank, first=first, last=last: e.matmul(
                        pB[:, bank + off:bank + off + ncols], BF(A_V + kc * 128, 128),
                        BF(pt, ncols), start=first, stop=last), waits=wts, inc="pe")
                    ev_avmm[lt] = ev
                    if last:
                        ev_avlast[qs] = ev
                    return ev

                def pe_hook(u):
                    if u in due_rowmm:
                        s = due_rowmm[u]
                        spar = s % 2
                        has_gp = (s, "gp") in sum_last
                        wts = [sum_last[(s, "dv")],
                               ("ac", ev_rbcopy.get(s - 1, bar_ac))]
                        if has_gp:
                            wts.append(sum_last[(s, "gp")])
                        ev_rowmm[s] = P.emit("pe", lambda e, spar=spar, hg=has_gp: e.matmul(
                            pD[0:1, :], ones_f[:, 0:1], FF(F_ACC + spar * 512, 512),
                            start=True, stop=not hg), waits=wts, inc="pe")
                        if has_gp:
                            ev_rowmm[s] = P.emit("pe", lambda e, spar=spar: e.matmul(
                                pD[0:1, :], ones_f[:, 0:1], FF(F_T2 + spar * 512, 512),
                                start=False, stop=True), inc="pe")
                        rowmm_by_sumpar[spar] = ev_rowmm[s]
                    if u in due_bcast:
                        s = due_bcast[u]
                        ev_bcast[s] = P.emit("pe", lambda e: e.matmul(
                            pD[:, :], onesr_f[0:1, :], rows[0:1, 0:512],
                            start=True, stop=True),
                            waits=[("ac", ev_rexp[s])], inc="pe")

                def ac_hook(t):
                    if t in due_accopy:
                        s = due_accopy[t]
                        ev_accopy[s] = P.emit("ac", lambda e: e.copy(
                            rows[0:1, 0:512], pD[0:1, :]),
                            waits=[("pe", ev_rowmm[s])], inc="ac")
                        P.emit("ac", lambda e: e.activation(
                            rows[0:1, 0:512], rows[0:1, 0:512], AF.Ln), inc="ac")
                        ev_rexp[s] = P.emit("ac", lambda e: e.activation(
                            rows[0:1, 0:512], rows[0:1, 0:512], AF.Exp,
                            scale=sb_cv[0:1, 5:6]), inc="ac")
                    if t in due_rbcopy:
                        s = due_rbcopy[t]
                        ev_rbcopy[s] = P.emit("ac", lambda e: e.copy(
                            FF(1024, 512), pD[:, :]),
                            waits=[("pe", ev_bcast[s])], inc="ac")

                def close_strip(s):
                    done_close.add(s)
                    spar = s % 2
                    zpar = s % 2
                    wts = [("pe", ev_avlast[s]), ("ac", ev_rbcopy[s]),
                           (f"zt{zpar}", zt_store_by_par[zpar])]
                    ev_zt = P.emit("dv", lambda e, spar=spar, zpar=zpar: e.tensor_mul(
                        BF(A_ZT + zpar * 1024, 512), pB[:, spar * 512:spar * 512 + 512],
                        FF(1024, 512)), waits=wts, inc="dv")
                    ztm_by_bank[spar] = ev_zt
                    ev_st = P.emit("sy", lambda e, h=h, s=s, zpar=zpar, j=lyr: e.dma_start(
                        out=za_in[j][s % 2].ap()[s // 2, h * 128:(h + 1) * 128, :],
                        in_=BF(A_ZT + zpar * 1024, 512)),
                        waits=[("dv", ev_zt)], inc=f"zt{zpar}", amt=16)
                    zt_store_by_par[zpar] = ev_st

                def dv_hook(t):
                    if t in due_close:
                        close_strip(due_close[t])

                if os.environ.get("KDRAIN"):
                    drain_through(7)
                pending_lead = None
                for lt in range(T_h):
                    qs = tiles[lt][0]
                    if tiles[lt][5] and qs % 2 == 0:
                        drain_through(qs // 2)
                    core_S(lt)
                    if lt == last_t[qs] and qs % 2 == 1:
                        pair_lastS[qs // 2] = P.cnt["pe"]
                    pump(2)
                    if lt >= 2:
                        core_AV(lt - 2)
                        pe_hook(lt - 2)
                        pump(2)
                    if pending_lead is not None:
                        core_exp(pending_lead, paired=True)
                        ac_hook(lt)
                        core_sums(pending_lead)
                        dv_hook(lt - 1)
                        core_sums(lt)
                        dv_hook(lt)
                        pending_lead = None
                    elif is_pair_lead(lt):
                        pending_lead = lt
                        ac_hook(lt)
                    else:
                        core_exp(lt, paired=False)
                        ac_hook(lt)
                        core_sums(lt)
                        dv_hook(lt)
                for u in (T_h - 2, T_h - 1):
                    core_AV(u)
                    pe_hook(u)
                drain_through(7)
                for s in sorted(last_t):
                    if s not in ev_rowmm:
                        pe_hook(last_t[s])
                    if s not in ev_accopy:
                        ac_hook(last_t[s] + 2)
                    if s not in ev_bcast:
                        pe_hook(last_t[s] + 2)
                    if s not in ev_rbcopy:
                        ac_hook(last_t[s] + 4)
                    if s not in done_close:
                        close_strip(s)
                ev_last_av = max(ev_avlast.values())

            if phase <= 3:
                return
            ev_ccz_p = []
            for par in range(2):
                ev_ccz_p.append(P.emit("gp", lambda e, j=lyr, par=par: e.collective_compute(
                    "AllToAll", ALU.bypass, replica_groups=[list(range(NC))],
                    ins=[za_in[j][par].ap().rearrange("s h t -> (s h) t")],
                    outs=[za_out[j][par].ap()]),
                    waits=[(f"zt{par}", P.cnt[f"zt{par}"])], inc=cc_z, amt="dflt"))

            ev_wo = {}
            for wh in range(2):
                ev_wo[wh] = P.emit("gp", lambda e, wh=wh, wo=wo: e.dma_start(
                    out=BF(A_WO, 32768).rearrange("p (c d) -> p c d", d=2048)[:, :, wh * 1024:(wh + 1) * 1024],
                    in_=wo.ap().rearrange("(c p) d -> p c d", p=128)[:, :, wh * 1024:(wh + 1) * 1024]),
                    waits=[("pe", P.cnt["pe"]), ("ac", P.cnt["ac"]), ("dv", P.cnt["dv"])],
                    inc="wo", amt=16)
            ev_my = {}
            for par in range(2):
                ev_my[par] = P.emit("sy", lambda e, j=lyr, par=par: e.dma_start(
                    out=BF(A_ZTMY, 16384).rearrange("p (c t) -> p c t", t=1024)[:, :, par * 512:(par + 1) * 512],
                    in_=za_out[j][par].ap().rearrange("(c p) t -> p c t", p=128)),
                    waits=[(cc_z, ev_ccz_p[par]), ("pe", ev_last_av)], inc="my", amt=16)

            if phase <= 4:
                return
            bar_ac, bar_dv = P.cnt["ac"], P.cnt["dv"]
            ev_evac = {}
            res_dv_last = 0
            for nh in range(2):
                for dm in range(DC):
                    ps = pA if dm % 2 == 0 else pB
                    for fc in range(DC):
                        wts = []
                        if fc == 0:
                            wts = [("my", ev_my[nh]), ("wo", ev_wo[1]),
                                   ("dv", max(ev_evac.get((nh, dm - 2), 0), bar_dv)),
                                   ("ac", bar_ac)]
                        P.emit("pe", lambda e, ps=ps, dm=dm, fc=fc, nh=nh: e.matmul(
                            ps[:, nh * 512:(nh + 1) * 512],
                            BF(A_WO + fc * 2048 + dm * 128, 128),
                            BF(A_ZTMY + fc * 1024 + nh * 512, 512),
                            start=(fc == 0), stop=(fc == DC - 1)),
                            waits=wts, inc="pe")
                    ev_mm = P.cnt["pe"]
                    res_dv_last = P.emit("dv", lambda e, ps=ps, dm=dm, nh=nh: e.tensor_add(
                        res[:, dm, nh * 512:(nh + 1) * 512],
                        res[:, dm, nh * 512:(nh + 1) * 512],
                        ps[:, nh * 512:(nh + 1) * 512]),
                        waits=[("pe", ev_mm)], inc="dv")
                    ev_evac[(nh, dm)] = res_dv_last
                    if nh == 1 and st["next_ln"]:
                        ln_chunk(dm, "dv", max(res_dv_last, ev_evac[(0, dm)]))
            st["res_dv"] = res_dv_last
            st["attn_done"] = (P.cnt["pe"], res_dv_last)

        def emit_ffn(pass_idx):
            ln_finish(None)
            ln_dv = st["ln_dv"]
            ev_row_cp = 0
            ev_xs = {}
            for th in range(2):
                for c in range(DC):
                    P.emit("pe", lambda e, c=c, th=th: e.matmul(
                        pD[0:1, :], ones_b[:, 0:1], BF(A_XLN + c * 1024 + th * 512, 512),
                        start=(c == 0), stop=(c == DC - 1)),
                        waits=[("dv", ln_dv), ("dv", ev_row_cp)], inc="pe")
                ev_mm = P.cnt["pe"]
                ev_row_cp = P.emit("dv", lambda e, th=th: e.tensor_copy(
                    rows[0:1, th * 512:(th + 1) * 512], pD[0:1, :]), waits=[("pe", ev_mm)], inc="dv")
                ev_sc = P.emit("dv", lambda e, th=th: e.tensor_scalar_mul(
                    rows[0:1, th * 512:(th + 1) * 512], rows[0:1, th * 512:(th + 1) * 512], sb_cv[0:1, 4:5]), inc="dv")
                ev_bc = P.emit("pe", lambda e, th=th: e.matmul(
                    pA[:, th * 512:(th + 1) * 512], onesr_f[0:1, :], rows[0:1, th * 512:(th + 1) * 512],
                    start=True, stop=True), waits=[("dv", ev_sc)], inc="pe")
                ev_xs[th] = P.emit("ac", lambda e, th=th: e.copy(
                    FF(F_ACC, 1024)[:, th * 512:(th + 1) * 512], pA[:, th * 512:(th + 1) * 512]),
                    waits=[("pe", ev_bc)], inc="ac")
            xs128B = FF(F_ACC, 1024)

            pre = st["ffn_pre"]
            ev_wld = {}
            ev_h_done = {}
            for ih in range(2):
                ev_gu_read = {}
                last_h = 0
                last_gu_mm = 0
                for ic in range(32):
                    gi = ih * 32 + ic
                    for nm, wsrc, off in (("g", wgT, A_WS), ("u", wuT, A_WS + 4096)):
                        key = (nm, ih, ic)
                        if pre is not None and ih == 0 and ic < 2 and (nm, ic) in st["ffn_pre"]:
                            ev_wld[key] = st["ffn_pre"][(nm, ic)]
                            continue
                        wts = []
                        if ic >= 2 or ih > 0:
                            wts.append(("pe", ev_gu_read.get(("mm", ic - 2), st.get("wd_done", 0))))
                        ev_wld[key] = P.emit("gp", lambda e, gi=gi, ic=ic, wsrc=wsrc, off=off: e.dma_start(
                            out=BF(off + (ic % 2) * 2048, 2048).rearrange("p (c i) -> p c i", i=128),
                            in_=wsrc.ap()[:, gi * 128:(gi + 1) * 128].rearrange("(c p) i -> p c i", p=128)),
                            waits=wts, inc=f"wg{ic % 2}", amt=16)
                    for nm, ps, off in (("g", pA, A_WS), ("u", pB, A_WS + 4096)):
                        for dc in range(DC):
                            wts = []
                            if dc == 0:
                                wts = [(f"wg{ic % 2}", ev_wld[(nm, ih, ic)]),
                                       ("dv", ev_gu_read.get((nm, ic - 1), 0)),
                                       ("ac", ev_xs[1])]
                            for nh in range(2):
                                P.emit("pe", lambda e, ps=ps, off=off, ic=ic, dc=dc, nh=nh: e.matmul(
                                    ps[:, nh * 512:(nh + 1) * 512],
                                    BF(off + (ic % 2) * 2048 + dc * 128, 128),
                                    BF(A_XLN + dc * 1024 + nh * 512, 512),
                                    start=(dc == 0), stop=(dc == DC - 1)),
                                    waits=wts, inc="pe")
                                wts = []
                        ev_gu_read[("mm_" + nm, ic)] = P.cnt["pe"]
                    ev_gu_read[("mm", ic)] = P.cnt["pe"]
                    last_gu_mm = P.cnt["pe"]
                    ev_tg = P.emit("dv", lambda e: e.tensor_sub(FF(F_T1, 1024), pA[:, :], xs128B),
                                   waits=[("pe", ev_gu_read[("mm_g", ic)])], inc="dv")
                    ev_gu_read[("g", ic)] = ev_tg
                    ev_sl = P.emit("ac", lambda e, gi=gi: e.activation(
                        FF(F_SG, 1024), FF(F_T1, 1024), AF.Silu, scale=sb_gs[:, gi:gi + 1]),
                        waits=[("dv", ev_tg)], inc="ac")
                    ev_tu = P.emit("dv", lambda e: e.tensor_sub(FF(F_T2, 1024), pB[:, :], xs128B),
                                   waits=[("pe", ev_gu_read[("mm_u", ic)])], inc="dv")
                    ev_gu_read[("u", ic)] = ev_tu
                    last_h = P.emit("dv", lambda e, gi=gi, ic=ic: e.scalar_tensor_tensor(
                        BF(A_HT + ic * 1024, 1024), FF(F_T2, 1024), sb_us[:, gi:gi + 1],
                        FF(F_SG, 1024), ALU.mult, ALU.mult),
                        waits=[("ac", ev_sl)], inc="dv")
                    ev_h_done[(ih, ic)] = last_h
                ev_row_cp2 = 0
                ev_hs = {}
                for th in range(2):
                    for ic in range(32):
                        wts = [("dv", ev_h_done[(ih, ic)])] if th == 0 else []
                        if ic == 0:
                            wts.append(("dv", ev_row_cp2))
                        P.emit("pe", lambda e, ic=ic, th=th: e.matmul(
                            pD[0:1, :], ones_b[:, 0:1], BF(A_HT + ic * 1024 + th * 512, 512),
                            start=(ic == 0), stop=(ic == 31)), waits=wts, inc="pe")
                    ev_mm = P.cnt["pe"]
                    ev_row_cp2 = P.emit("dv", lambda e, th=th: e.tensor_copy(
                        rows[0:1, 1024 + th * 512:1024 + (th + 1) * 512], pD[0:1, :]),
                        waits=[("pe", ev_mm)], inc="dv")
                    ev_sc = P.emit("dv", lambda e, th=th: e.tensor_scalar_mul(
                        rows[0:1, 1024 + th * 512:1024 + (th + 1) * 512],
                        rows[0:1, 1024 + th * 512:1024 + (th + 1) * 512], sb_cv[0:1, 4:5]), inc="dv")
                    ev_bc = P.emit("pe", lambda e, th=th: e.matmul(
                        pA[:, th * 512:(th + 1) * 512], onesr_f[0:1, :],
                        rows[0:1, 1024 + th * 512:1024 + (th + 1) * 512],
                        start=True, stop=True),
                        waits=[("dv", ev_sc), ("dv", ev_gu_read[("g", 31)])], inc="pe")
                    ev_hs[th] = P.emit("ac", lambda e, th=th: e.copy(
                        FF(F_ACC2, 1024)[:, th * 512:(th + 1) * 512], pA[:, th * 512:(th + 1) * 512]),
                        waits=[("pe", ev_bc)], inc="ac")
                hs128B = FF(F_ACC2, 1024)

                ev_wd = {}
                ev_devac = {}
                res_dv_last = 0
                for dm in range(DC):
                    wts = [("pe", last_gu_mm)]
                    if dm >= 2:
                        wts.append(("pe", ev_wd.get(("mm", dm - 2), 0)))
                    ev_wd[dm] = P.emit("gp", lambda e, ih=ih, dm=dm: e.dma_start(
                        out=BF(A_WS + (dm % 2) * 4096, 4096).rearrange("p (g i) -> p g i", i=128),
                        in_=wdT.ap().rearrange("(g p) d -> p g d", p=128)[:, ih * 32:(ih + 1) * 32, dm * 128:(dm + 1) * 128]),
                        waits=wts, inc=f"wd{dm % 2}", amt=16)
                    ps = pP if dm % 2 == 0 else pB
                    for ic in range(32):
                        wts = []
                        if ic == 0:
                            wts = [(f"wd{dm % 2}", ev_wd[dm]), ("dv", ev_devac.get(dm - 2, 0)),
                                   ("ac", ev_hs[1])]
                            if dm == 0 or dm == 1:
                                wts.append(("dv", ev_h_done[(ih, 31)]))
                        for nh in range(2):
                            P.emit("pe", lambda e, ps=ps, ih=ih, dm=dm, ic=ic, nh=nh: e.matmul(
                                ps[:, nh * 512:(nh + 1) * 512],
                                BF(A_WS + (dm % 2) * 4096 + ic * 128, 128),
                                BF(A_HT + ic * 1024 + nh * 512, 512),
                                start=(ic == 0), stop=(ic == 31)),
                                waits=wts, inc="pe")
                            wts = []
                    ev_wd[("mm", dm)] = P.cnt["pe"]
                    ev_mm = P.cnt["pe"]
                    ev_sub = P.emit("dv", lambda e, ps=ps: e.tensor_sub(FF(F_T2, 1024), ps[:, :], hs128B),
                                    waits=[("pe", ev_mm)], inc="dv")
                    res_dv_last = P.emit("dv", lambda e, dm=dm: e.scalar_tensor_tensor(
                        res[:, dm, :], FF(F_T2, 1024), sb_ds[:, dm:dm + 1], res[:, dm, :],
                        ALU.mult, ALU.add), inc="dv")
                    ev_devac[dm] = res_dv_last
                    if ih == 1 and st["next_ln"]:
                        ln_chunk(dm, "dv", res_dv_last)
                st["wd_done"] = ev_wd[("mm", DC - 1)]
            st["res_dv"] = res_dv_last

        import os
        _nsub = int(os.environ.get("KNSUB", "4"))
        _ph = int(os.environ.get("KPHASE", "9"))
        for q in range(4):
            ev_q = P.emit("sy", lambda e, q=q: e.dma_start(
                out=res[:, q * 4:(q + 1) * 4, :],
                in_=xT_in.ap().rearrange("(c p) t -> p c t", p=128)[:, q * 4:(q + 1) * 4, :]),
                inc="ld", amt=16)
            for c in range(q * 4, q * 4 + 4):
                ln_chunk(c, "ld", ev_q)
        emit_attn(0, _ph)
        if _nsub >= 2:
            emit_ffn(0)
        if _nsub >= 3:
            emit_attn(1)
        if _nsub >= 4:
            st["next_ln"] = False
            emit_ffn(1)

        if os.environ.get("KDBG"):
            w = [("dv", st["res_dv"]), ("pe", P.cnt["pe"]), ("ac", P.cnt["ac"])]
            P.emit("gp", lambda e: e.dma_start(
                out=outT.ap()[0:128, :], in_=abf[:, A_KT:A_KT + 1024]),
                waits=w, inc="out", amt=16)
            P.emit("gp", lambda e: e.dma_start(
                out=outT.ap()[128:256, :], in_=abf[:, A_QT + 3 * 1024:A_QT + 4 * 1024]),
                inc="out", amt=16)
            P.emit("gp", lambda e: e.dma_start(
                out=outT.ap()[256:384, :], in_=abf[:, A_V:A_V + 1024]),
                inc="out", amt=16)
        else:
            P.emit("sy", lambda e: e.dma_start(
                out=outT.ap().rearrange("(c p) t -> p c t", p=128), in_=res[:, :, :]),
                waits=[("dv", st["res_dv"])], inc="out", amt=16)
        ev_out = P.cnt["out"]
        P.emit("sy", lambda e: e.wait_ge(s_out, ev_out))

        def realize(eng_name, eng):
            hwm = {}
            for fn, waits, incsem, after, amt in P.ops[eng_name]:
                for snm, val in waits:
                    if val and val > hwm.get(snm, 0):
                        eng.wait_ge(SEMS[snm], val)
                        hwm[snm] = val
                ins = fn(eng)
                if incsem is not None:
                    if amt == "dflt":
                        ins.then_inc(SEMS[incsem])
                    else:
                        ins.then_inc(SEMS[incsem], amt)

        @block.tensor
        def _(t):
            realize("pe", t)

        @block.scalar
        def _(s):
            realize("ac", s)

        @block.vector
        def _(v):
            realize("dv", v)

        @block.gpsimd
        def _(g):
            with allow_ici_all_to_all():
                realize("gp", g)

        @block.sync
        def _(sy):
            realize("sy", sy)

    return nc


_NC_CACHE = None


def make_in_maps(L0_attn_a_x, L0_attn_a_W_K, L0_attn_a_W_Q, L0_attn_a_W_V, L0_attn_a_W_O,
                 L0_attn_b_w_qkv, L0_attn_b_w_out,
                 L0_ffn_gate_w, L0_ffn_gate_scale, L0_ffn_up_w, L0_ffn_up_scale,
                 L0_ffn_down_w, L0_ffn_down_scale):
    x = np.asarray(L0_attn_a_x, dtype=np.float32).reshape(S, D)
    W_K = np.asarray(L0_attn_a_W_K, dtype=np.float32)
    W_Q = np.asarray(L0_attn_a_W_Q, dtype=np.float32)
    W_V = np.asarray(L0_attn_a_W_V, dtype=np.float32)
    W_O = np.asarray(L0_attn_a_W_O, dtype=np.float32)
    w_qkv = np.asarray(L0_attn_b_w_qkv, dtype=np.float32)
    w_out = np.asarray(L0_attn_b_w_out, dtype=np.float32)
    gw = np.asarray(L0_ffn_gate_w, dtype=np.int32)
    uw = np.asarray(L0_ffn_up_w, dtype=np.int32)
    dw = np.asarray(L0_ffn_down_w, dtype=np.int32)
    gs = np.asarray(L0_ffn_gate_scale, dtype=np.float32)
    us = np.asarray(L0_ffn_up_scale, dtype=np.float32)
    ds = np.asarray(L0_ffn_down_scale, dtype=np.float32)

    C = np.ascontiguousarray
    woT_a = C(W_O.T)
    woT_b = C(w_out)
    wgT = C(gw.T)
    wuT = C(uw.T)
    wdT = C(dw.T)
    gsT = C(gs.reshape(IC, 128).T)
    usT = C(us.reshape(IC, 128).T)
    dsT = C(ds.reshape(DC, 128).T)
    idn = np.eye(128, dtype=ml_dtypes.bfloat16)
    mask = np.triu(np.ones((128, 128))).astype(ml_dtypes.bfloat16)
    cvec = np.tile(np.array([[EPS, -0.5, INV_SQRT_HD, 1.0 / D, 128.0, -1.0, 0.0, 0.0]],
                            dtype=np.float32), (128, 1))

    in_maps = []
    for c in range(NC):
        h0, h1 = 2 * c, 2 * c + 1
        m = dict(
            xT_in=C(x[c * T:(c + 1) * T, :].T),
            wqT_a=C(np.stack([W_Q[h0].T, W_Q[h1].T])),
            wkT_a=C(np.stack([W_K[h0].T, W_K[h1].T])),
            wvT_a=C(np.stack([W_V[h0].T, W_V[h1].T])),
            woT_a=woT_a,
            wqT_b=C(np.stack([w_qkv[:, h0 * HD:(h0 + 1) * HD], w_qkv[:, h1 * HD:(h1 + 1) * HD]])),
            wkT_b=C(np.stack([w_qkv[:, D + h0 * HD:D + (h0 + 1) * HD], w_qkv[:, D + h1 * HD:D + (h1 + 1) * HD]])),
            wvT_b=C(np.stack([w_qkv[:, 2 * D + h0 * HD:2 * D + (h0 + 1) * HD], w_qkv[:, 2 * D + h1 * HD:2 * D + (h1 + 1) * HD]])),
            woT_b=woT_b,
            wgT=wgT, wuT=wuT, wdT=wdT, gsT=gsT, usT=usT, dsT=dsT,
            idn=idn, mask=mask,
            cvec=cvec,
        )
        in_maps.append(m)
    return in_maps


def kernel(**inputs):
    global _NC_CACHE
    in_maps = make_in_maps(**inputs)
    if _NC_CACHE is None:
        _NC_CACHE = build_program()
    res = run_bass_kernel_spmd(_NC_CACHE, in_maps, core_ids=list(range(NC)))
    out = np.empty((S, D), dtype=np.float32)
    for c in range(NC):
        out[c * T:(c + 1) * T, :] = res.results[c]["outT"].T
    return out.reshape(1, S, D)


# version 19
# speedup vs baseline: 1.1551x; 1.1419x over previous
import os
import numpy as np
import ml_dtypes
import concourse.bass as bass
import concourse.mybir as mybir
from concourse.bass_utils import run_bass_kernel_spmd
from concourse.aot_env import allow_ici_all_to_all

F32 = mybir.dt.float32
BF16 = mybir.dt.bfloat16
I32 = mybir.dt.int32
AF = mybir.ActivationFunctionType
ALU = mybir.AluOpType

NC = 8
S = 8192
T = 1024
D = 2048
DC = D // 128
HD = 128
NHL = 2
FI = 8192
IC = FI // 128
QS = 512
NQS = S // QS
INV_SQRT_HD = 1.0 / np.sqrt(128.0)
EPS = 1e-5

A_XLN = 0
A_QT = 0
A_KT = 4096
A_V = 12288
A_WO = 0
A_HT = 16384
A_WQKV = 20480
A_ZTMY = 32768
A_XG = 32768
A_PT = 45056
A_ZT = 47104
A_WS = 49152
A_END = 57344

F_ACC = 0
F_ACC2 = 1024
F_SG = 2048
F_T1 = 3072
F_T2 = 4096
F_RSTDB = 0
F_MRSTDB = 1024
F_END = 5120


class Plan:
    ENGS = ("pe", "ac", "dv", "gp", "sy")

    def __init__(self):
        self.ops = {e: [] for e in self.ENGS}
        self.cnt = {}

    def post(self, sem, amt):
        self.cnt[sem] = self.cnt.get(sem, 0) + amt
        return self.cnt[sem]

    def emit(self, eng, fn, waits=(), inc=None, amt=1):
        after = self.post(inc, 1 if amt == "dflt" else amt) if inc is not None else None
        self.ops[eng].append((fn, tuple(waits), inc, after, amt))
        return after


def build_program():
    nc = bass.Bass("TRN2", target_bir_lowering=False, detect_race_conditions=False)

    xT_in = nc.dram_tensor("xT_in", [D, T], F32, kind="ExternalInput")
    wqT_a = nc.dram_tensor("wqT_a", [NHL, D, HD], F32, kind="ExternalInput")
    wkT_a = nc.dram_tensor("wkT_a", [NHL, D, HD], F32, kind="ExternalInput")
    wvT_a = nc.dram_tensor("wvT_a", [NHL, D, HD], F32, kind="ExternalInput")
    woT_a = nc.dram_tensor("woT_a", [D, D], F32, kind="ExternalInput")
    wqT_b = nc.dram_tensor("wqT_b", [NHL, D, HD], F32, kind="ExternalInput")
    wkT_b = nc.dram_tensor("wkT_b", [NHL, D, HD], F32, kind="ExternalInput")
    wvT_b = nc.dram_tensor("wvT_b", [NHL, D, HD], F32, kind="ExternalInput")
    woT_b = nc.dram_tensor("woT_b", [D, D], F32, kind="ExternalInput")
    wgT = nc.dram_tensor("wgT", [D, FI], I32, kind="ExternalInput")
    wuT = nc.dram_tensor("wuT", [D, FI], I32, kind="ExternalInput")
    wdT = nc.dram_tensor("wdT", [FI, D], I32, kind="ExternalInput")
    gsT = nc.dram_tensor("gsT", [128, IC], F32, kind="ExternalInput")
    usT = nc.dram_tensor("usT", [128, IC], F32, kind="ExternalInput")
    dsT = nc.dram_tensor("dsT", [128, DC], F32, kind="ExternalInput")
    cvec_i = nc.dram_tensor("cvec", [128, 8], F32, kind="ExternalInput")
    idn_i = nc.dram_tensor("idn", [128, 128], BF16, kind="ExternalInput")
    mask_i = nc.dram_tensor("mask", [128, 128], BF16, kind="ExternalInput")
    outT = nc.dram_tensor("outT", [D, T], F32, kind="ExternalOutput")

    xg_in = [nc.dram_tensor(f"xg_in{j}", [D, T], BF16) for j in range(2)]
    xg_out = [nc.dram_tensor(f"xg_out{j}", [NC * D, T], BF16, addr_space="Shared")
              for j in range(2)]
    za_in = [[nc.dram_tensor(f"za_in{j}_{par}", [NC, NHL * HD, T // 2], BF16)
              for par in range(2)] for j in range(2)]
    za_out = [[nc.dram_tensor(f"za_out{j}_{par}", [D, T // 2], BF16)
               for par in range(2)] for j in range(2)]

    from contextlib import ExitStack
    with ExitStack() as ctx:
        E = ctx.enter_context
        abf = E(nc.sbuf_tensor("abf", [128, A_END], BF16))
        af = E(nc.sbuf_tensor("af", [128, F_END], F32))
        res = E(nc.sbuf_tensor("res", [128, DC, T], F32))
        rows = E(nc.sbuf_tensor("rows", [1, 2048], F32))
        sb_idn = E(nc.sbuf_tensor("sb_idn", [128, 128], BF16))
        sb_mask = E(nc.sbuf_tensor("sb_mask", [128, 128], BF16))
        sb_gs = E(nc.sbuf_tensor("sb_gs", [128, IC], F32))
        sb_us = E(nc.sbuf_tensor("sb_us", [128, IC], F32))
        sb_ds = E(nc.sbuf_tensor("sb_ds", [128, DC], F32))
        sb_cv = E(nc.sbuf_tensor("sb_cv", [128, 8], F32))
        ones_f = E(nc.sbuf_tensor("ones_f", [128, 1], F32))
        ones_b = E(nc.sbuf_tensor("ones_b", [128, 1], BF16))
        onesr_f = E(nc.sbuf_tensor("onesr_f", [1, 128], F32))
        pA = E(nc.psum_tensor("pA", [128, 1024], F32))
        pB = E(nc.psum_tensor("pB", [128, 1024], F32))
        pP = E(nc.psum_tensor("pP", [128, 1024], F32))
        pD = E(nc.psum_tensor("pD", [128, 512], F32))
        pT = E(nc.psum_tensor("pT", [128, 1024], BF16))
        dma_sem_names = ["ld", "xgw", "xg0", "xg1", "xg2", "zt0", "zt1", "my",
                         "wqkv", "wo", "wg0", "wg1", "wd0", "wd1"]
        dma_sems = {n: E(nc.semaphore(f"s_{n}")) for n in dma_sem_names}
        s_pe = E(nc.semaphore("s_pe"))
        s_ac = E(nc.semaphore("s_ac"))
        s_dv = E(nc.semaphore("s_dv"))
        s_cc = [E(nc.semaphore(f"s_cc{j}")) for j in range(4)]
        s_out = E(nc.semaphore("s_out"))
        block = E(nc.Block())

        P = Plan()
        SEMS = {"pe": s_pe, "ac": s_ac, "dv": s_dv,
                "cc0": s_cc[0], "cc1": s_cc[1], "cc2": s_cc[2], "cc3": s_cc[3],
                "out": s_out, **dma_sems}

        def BF(c0, n):
            return abf[:, c0:c0 + n]

        def FF(c0, n):
            return af[:, c0:c0 + n]

        for name, dst, src in (("idn", sb_idn, idn_i), ("mask", sb_mask, mask_i),
                               ("gs", sb_gs, gsT), ("us", sb_us, usT), ("ds", sb_ds, dsT),
                               ("cv", sb_cv, cvec_i)):
            P.emit("sy", lambda e, d=dst, s=src: e.dma_start(out=d[:, :], in_=s.ap()),
                   inc="ld", amt=16)
        ev_consts = P.cnt["ld"]
        P.emit("dv", lambda e: e.memset(ones_f[:, :], 1.0), inc="dv")
        P.emit("dv", lambda e: e.memset(ones_b[:, :], 1.0), inc="dv")
        P.emit("dv", lambda e: e.memset(onesr_f[:, :], 1.0), inc="dv")
        ev_ones = P.cnt["dv"]

        st = {"res_dv": 0, "lnp": {}, "next_ln": True}

        def ln_chunk(c, wsem, wval):
            stl = st["lnp"]
            acc, acc2 = FF(F_ACC, 1024), FF(F_T1, 1024)
            sq = FF(F_SG, 1024)
            w = [(wsem, wval)] if wsem else []
            if c == 0:
                stl["acc"] = P.emit("dv", lambda e: e.tensor_copy(acc, res[:, 0, :]),
                                    waits=w + [("ld", ev_consts)], inc="dv")
            else:
                stl["acc"] = P.emit("dv", lambda e, c=c: e.tensor_add(acc, acc, res[:, c, :]),
                                    waits=w, inc="dv")
            wts = list(w)
            if c > 0:
                wts.append(("dv", stl["sqa"]))
            ev_s = P.emit("ac", lambda e, c=c: e.activation(sq, res[:, c, :], AF.Square),
                          waits=wts, inc="ac")
            if c == 0:
                stl["sqa"] = P.emit("dv", lambda e: e.tensor_copy(acc2, sq),
                                    waits=[("ac", ev_s)], inc="dv")
            else:
                stl["sqa"] = P.emit("dv", lambda e, c=c: e.tensor_add(acc2, acc2, sq),
                                    waits=[("ac", ev_s)], inc="dv")

        def ln_finish(write_xg_idx):
            acc, acc2 = FF(F_ACC, 1024), FF(F_T1, 1024)
            ev_acc = st["lnp"]["acc"]
            ev_acc2 = st["lnp"]["sqa"]
            ln_ac_bar = P.cnt.get("ac", 0)
            r_sum = [rows[0:1, 0:512], rows[0:1, 1024:1536]]
            r_sq = [rows[0:1, 512:1024], rows[0:1, 1536:2048]]
            ev_row_cp = 0
            for th in range(2):
                ev_mm = P.emit("pe", lambda e, th=th: e.matmul(
                    pD[0:1, :], ones_f[:, 0:1], acc[:, th * 512:(th + 1) * 512],
                    start=True, stop=True),
                    waits=[("dv", ev_acc), ("dv", ev_row_cp), ("ac", ln_ac_bar)], inc="pe")
                ev_row_cp = P.emit("dv", lambda e, th=th: e.tensor_copy(r_sum[th], pD[0:1, :]),
                                   waits=[("pe", ev_mm)], inc="dv")
                ev_mm = P.emit("pe", lambda e, th=th: e.matmul(
                    pD[0:1, :], ones_f[:, 0:1], acc2[:, th * 512:(th + 1) * 512],
                    start=True, stop=True),
                    waits=[("dv", ev_acc2), ("dv", ev_row_cp)], inc="pe")
                ev_row_cp = P.emit("dv", lambda e, th=th: e.tensor_copy(r_sq[th], pD[0:1, :]),
                                   waits=[("pe", ev_mm)], inc="dv")
            tmp = af[0:1, 0:512]
            ev_bc = {}
            for th in range(2):
                P.emit("dv", lambda e, th=th: e.tensor_scalar_mul(r_sum[th], r_sum[th], sb_cv[0:1, 3:4]), inc="dv")
                P.emit("dv", lambda e, th=th: e.tensor_scalar_mul(r_sq[th], r_sq[th], sb_cv[0:1, 3:4]), inc="dv")
                P.emit("dv", lambda e, th=th: e.tensor_mul(tmp, r_sum[th], r_sum[th]), inc="dv")
                ev_var = P.emit("dv", lambda e, th=th: e.tensor_sub(r_sq[th], r_sq[th], tmp), inc="dv")
                ev_ln = P.emit("ac", lambda e, th=th: e.activation(r_sq[th], r_sq[th], AF.Ln, bias=sb_cv[0:1, 0:1]),
                               waits=[("dv", ev_var)], inc="ac")
                ev_rstd = P.emit("ac", lambda e, th=th: e.activation(r_sq[th], r_sq[th], AF.Exp, scale=sb_cv[0:1, 1:2]),
                                 inc="ac")
                ev_mr = P.emit("dv", lambda e, th=th: e.tensor_mul(r_sum[th], r_sum[th], r_sq[th]),
                               waits=[("ac", ev_rstd)], inc="dv")
                ev1 = P.emit("pe", lambda e, th=th: e.matmul(
                    pA[:, th * 512:(th + 1) * 512], onesr_f[0:1, :], r_sq[th],
                    start=True, stop=True), waits=[("ac", ev_rstd), ("dv", ev_ones)], inc="pe")
                ev2 = P.emit("pe", lambda e, th=th: e.matmul(
                    pB[:, th * 512:(th + 1) * 512], onesr_f[0:1, :], r_sum[th],
                    start=True, stop=True), waits=[("dv", ev_mr)], inc="pe")
                P.emit("ac", lambda e, th=th: e.copy(
                    FF(F_RSTDB, 1024)[:, th * 512:(th + 1) * 512], pA[:, th * 512:(th + 1) * 512]),
                    waits=[("pe", ev1)], inc="ac")
                ev_bc[th] = P.emit("ac", lambda e, th=th: e.copy(
                    FF(F_MRSTDB, 1024)[:, th * 512:(th + 1) * 512], pB[:, th * 512:(th + 1) * 512]),
                    waits=[("pe", ev2)], inc="ac")
            t1 = [FF(F_SG, 1024), FF(F_T2, 1024)]
            last_norm = 0
            ev_xgw = []
            for c in range(DC):
                P.emit("dv", lambda e, c=c: e.tensor_mul(t1[c % 2], res[:, c, :], FF(F_RSTDB, 1024)),
                       waits=[("ac", ev_bc[1])], inc="dv")
                last_norm = P.emit("dv", lambda e, c=c: e.tensor_sub(
                    BF(A_XLN + c * 1024, 1024), t1[c % 2], FF(F_MRSTDB, 1024)), inc="dv")
                if write_xg_idx is not None and c == DC - 1:
                    ev_xgw.append(P.emit("sy", lambda e, j=write_xg_idx: e.dma_start(
                        out=xg_in[j].ap().rearrange("(c p) t -> p c t", p=128),
                        in_=abf[:, 0:DC * 1024].rearrange("p (c t) -> p c t", t=1024)),
                        waits=[("dv", last_norm)], inc="xgw", amt=16))
            st["ln_dv"] = last_norm
            st["ln_ac"] = ev_bc[1]
            if write_xg_idx is not None:
                return ev_xgw
            return None

        vcp_list = [0]
        zt_store_by_par = {0: 0, 1: 0}
        rowmm_by_sumpar = {0: 0, 1: 0}
        ztm_by_bank = {0: 0, 1: 0}
        xg_consumed = {0: 0, 1: 0, 2: 0}

        def emit_attn(lyr, phase=9):
            cc_g = f"cc{lyr * 2}"
            cc_z = f"cc{lyr * 2 + 1}"
            wq, wk, wv = (wqT_a, wkT_a, wvT_a) if lyr == 0 else (wqT_b, wkT_b, wvT_b)
            wo = woT_a if lyr == 0 else woT_b

            wq_waits = []
            if lyr == 1:
                wq_waits = [("pe", P.cnt["pe"]), ("dv", P.cnt["dv"])]
            ev_wqkv = {}
            for h in range(NHL):
                for pi, wsrc in enumerate((wq, wk, wv)):
                    ev_wqkv[(h, pi)] = P.emit("gp", lambda e, h=h, pi=pi, wsrc=wsrc: e.dma_start(
                        out=BF(A_WQKV + h * 6144 + pi * 2048, 2048).rearrange("p (c i) -> p c i", i=128),
                        in_=wsrc.ap()[h, :, :].rearrange("(c p) i -> p c i", p=128)),
                        waits=wq_waits, inc="wqkv", amt=16)
                    wq_waits = []
            ev_wqkv_all = P.cnt["wqkv"]

            ev_xgw = ln_finish(lyr)
            ev_ccg = P.emit("gp", lambda e, j=lyr: e.collective_compute(
                "AllGather", ALU.bypass, replica_groups=[list(range(NC))],
                ins=[xg_in[j].ap().opt()], outs=[xg_out[j].ap().opt()]),
                waits=[("xgw", ev_xgw[0])], inc=cc_g, amt="dflt")

            pre_waits = [("pe", P.cnt["pe"])] if lyr == 1 else []
            ev_pre = {}
            for ic in range(2):
                for nm, wsrc, off in (("g", wgT, A_WS), ("u", wuT, A_WS + 4096)):
                    ev_pre[(nm, ic)] = P.emit("gp", lambda e, ic=ic, wsrc=wsrc, off=off: e.dma_start(
                        out=BF(off + ic * 2048, 2048).rearrange("p (c i) -> p c i", i=128),
                        in_=wsrc.ap()[:, ic * 128:(ic + 1) * 128].rearrange("(c p) i -> p c i", p=128)),
                        waits=pre_waits, inc=f"wg{ic % 2}", amt=16)
                    pre_waits = []
            st["ffn_pre"] = ev_pre
            if phase <= 1:
                P.emit("gp", lambda e: e.engine_nop(), waits=[(cc_g, 1)])
                return

            entry_dv = P.cnt["dv"]
            ztm_by_bank[0] = max(ztm_by_bank[0], entry_dv)
            ztm_by_bank[1] = max(ztm_by_bank[1], entry_dv)
            bank_free = {0: ("dv", entry_dv), 1: ("dv", entry_dv)}
            stage_tr_prev = [0]
            ev_last_av = 0
            for h in range(NHL):
                bar_pe, bar_ac, bar_dv = P.cnt["pe"], P.cnt["ac"], P.cnt["dv"]
                wbase = A_WQKV + h * 6144
                evQ = {}
                evK = {}
                evV = {}
                pair_lastS = {}
                first_prod = [True]

                def make_prod(r):
                    def gen():
                        ld = {}

                        def emit_load(i):
                            nh, fh = ((0, 0), (0, 1), (1, 0), (1, 1))[i]
                            b = (r * 4 + i) % 3
                            w = [(cc_g, ev_ccg), ("pe", xg_consumed[b])]
                            ld[(nh, fh)] = (b, P.emit("sy", lambda e, r=r, nh=nh, fh=fh, b=b, j=lyr: e.dma_start(
                                out=BF(A_XG + b * 4096, 4096).rearrange("p (c t) -> p c t", t=512),
                                in_=xg_out[j].ap()[r * D + fh * 1024:r * D + fh * 1024 + 1024, nh * 512:(nh + 1) * 512]
                                    .rearrange("(c p) t -> p c t", p=128)),
                                waits=w, inc=f"xg{b}", amt=16))
                        for i in range(3):
                            emit_load(i)
                        ev_stage = {}
                        for g, (nh, pi) in enumerate((n, p) for n in (0, 1) for p in (0, 1, 2)):
                            bk = g % 2
                            for dc in range(DC):
                                fh = dc // 8
                                b, ev_ld = ld[(nh, fh)]
                                wts = []
                                if dc == 0:
                                    wts = [(f"xg{b}", ev_ld), ("wqkv", ev_wqkv_all)]
                                    fe, fv = bank_free[bk]
                                    if fv:
                                        wts.append((fe, fv))
                                    if first_prod[0]:
                                        wts += [("ac", bar_ac), ("dv", bar_dv)]
                                        first_prod[0] = False
                                elif dc == 8:
                                    wts = [(f"xg{b}", ev_ld)]
                                P.emit("pe", lambda e, bk=bk, pi=pi, dc=dc, b=b, wbase=wbase: e.matmul(
                                    pP[:, bk * 512:(bk + 1) * 512],
                                    BF(wbase + pi * 2048 + dc * 128, 128),
                                    BF(A_XG + b * 4096 + (dc % 8) * 512, 512),
                                    start=(dc == 0), stop=(dc == DC - 1)),
                                    waits=wts, inc="pe")
                                if pi == 2 and dc == 7:
                                    xg_consumed[ld[(nh, 0)][0]] = P.cnt["pe"]
                                    if nh == 0:
                                        emit_load(3)
                                if pi == 2 and dc == 15:
                                    xg_consumed[ld[(nh, 1)][0]] = P.cnt["pe"]
                                yield
                            ev_mm = P.cnt["pe"]
                            if pi == 0:
                                wts = [("pe", max(ev_mm, pair_lastS.get(r - 4, bar_pe)))]
                                evq = P.emit("ac", lambda e, r=r, nh=nh, bk=bk: e.copy(
                                    BF(A_QT + (r % 4) * 1024 + nh * 512, 512),
                                    pP[:, bk * 512:(bk + 1) * 512]), waits=wts, inc="ac")
                                evQ[r] = evq
                                bank_free[bk] = ("ac", evq)
                            elif pi == 1:
                                evk = P.emit("dv", lambda e, r=r, nh=nh, bk=bk: e.tensor_copy(
                                    BF(A_KT + r * 1024 + nh * 512, 512),
                                    pP[:, bk * 512:(bk + 1) * 512]),
                                    waits=[("pe", max(ev_mm, bar_pe))], inc="dv")
                                evK[r] = evk
                                bank_free[bk] = ("dv", evk)
                            else:
                                wts = [("pe", max(ev_mm, stage_tr_prev[0]))]
                                evs = P.emit("ac", lambda e, nh=nh, bk=bk: e.copy(
                                    BF(A_ZT + 512 + nh * 1024, 512),
                                    pP[:, bk * 512:(bk + 1) * 512]), waits=wts, inc="ac")
                                ev_stage[nh] = evs
                                bank_free[bk] = ("ac", evs)
                        ev_tr = 0
                        for tt in range(8):
                            src = A_ZT + 512 + (tt // 4) * 1024 + (tt % 4) * 128
                            wts = [("ac", ev_stage[tt // 4]), ("dv", vcp_list[-1]),
                                   ("ld", ev_consts)]
                            ev_tr = P.emit("pe", lambda e, src=src, tt=tt: e.transpose(
                                pT[:, tt * 128:(tt + 1) * 128], BF(src, 128), sb_idn[:, :]),
                                waits=wts, inc="pe")
                            yield
                        stage_tr_prev[0] = ev_tr
                        evV[r] = P.emit("dv", lambda e, r=r: e.tensor_copy(
                            BF(A_V + r * 1024, 1024), pT[:, :]),
                            waits=[("pe", ev_tr)], inc="dv")
                        vcp_list.append(evV[r])
                    return gen()

                gens = [make_prod(r) for r in range(8)]
                gi = [0]

                def pump(n):
                    while n > 0 and gi[0] < 8:
                        try:
                            next(gens[gi[0]])
                            n -= 1
                        except StopIteration:
                            gi[0] += 1

                def drain_through(r):
                    while gi[0] <= r:
                        try:
                            next(gens[gi[0]])
                        except StopIteration:
                            gi[0] += 1

                tiles = []
                for qs in range(NQS):
                    nkc = (qs + 1) * 4
                    for j, kc in enumerate(range(nkc)):
                        qlo = max(qs * QS, kc * 128)
                        tiles.append((qs, kc, qlo, qlo - qs * QS,
                                      kc * 128 >= qs * QS, j == 0, j == nkc - 1))
                T_h = len(tiles)
                last_t = {}
                for lt, td in enumerate(tiles):
                    if td[6]:
                        last_t[td[0]] = lt
                due_rowmm = {last_t[s]: s for s in last_t}
                due_bcast = {last_t[s] + 1: s for s in last_t}
                due_accopy = {last_t[s] + 2: s for s in last_t}
                due_close = {last_t[s] + 8: s for s in last_t}
                ev_S = {}
                ev_exp = {}
                ev_avmm = {}
                ev_summ = {}
                ev_mask = {}
                ev_rowmm = {}
                ev_accopy = {}
                ev_bcast = {}
                ev_recip = {}
                ev_avlast = {}
                done_close = set()

                def core_S(lt):
                    qs, kc, qlo, off, diag, first, last = tiles[lt]
                    pb = (lt % 2) * 512
                    r = qs // 2
                    ncols = QS - off
                    acw = max(ev_exp.get(lt - 2, 0), bar_ac if lt < 2 else 0)
                    wts = []
                    if first and qs % 2 == 0:
                        acw = max(acw, evQ[r])
                        wts.append(("dv", evK[r]))
                    wts.append(("ac", acw))
                    qsrc = A_QT + (r % 4) * 1024 + qlo - r * 1024
                    ev_S[lt] = P.emit("pe", lambda e, kc=kc, qsrc=qsrc, ncols=ncols, pb=pb: e.matmul(
                        pA[:, pb:pb + ncols],
                        BF(A_KT + kc * 128, 128), BF(qsrc, ncols),
                        start=True, stop=True), waits=wts, inc="pe")

                def core_exp(lt):
                    qs, kc, qlo, off, diag, first, last = tiles[lt]
                    pb = (lt % 2) * 512
                    pt = A_PT + (lt % 4) * 512
                    ncols = QS - off
                    wts = [("pe", ev_S[lt]), ("dv", ev_summ.get(lt - 4, bar_dv))]
                    ev_exp[lt] = P.emit("ac", lambda e, ncols=ncols, pb=pb, pt=pt: e.activation(
                        BF(pt, ncols), pA[:, pb:pb + ncols],
                        AF.Exp, scale=sb_cv[:, 2:3]), waits=wts, inc="ac")

                def core_sums(lt):
                    qs, kc, qlo, off, diag, first, last = tiles[lt]
                    pt = A_PT + (lt % 4) * 512
                    ncols = QS - off
                    spar = qs % 2
                    if diag:
                        ev_mask[lt] = P.emit("dv", lambda e, pt=pt: e.tensor_mul(
                            BF(pt, 128), BF(pt, 128), sb_mask[:, :]),
                            waits=[("ac", ev_exp[lt])], inc="dv")
                        wts = []
                    else:
                        wts = [("ac", ev_exp[lt])]
                    if first:
                        wts.append(("pe", rowmm_by_sumpar[spar]))
                        fn = lambda e, off=off, ncols=ncols, pt=pt, spar=spar: e.tensor_copy(
                            FF(F_ACC + spar * 512 + off, ncols), BF(pt, ncols))
                    else:
                        fn = lambda e, off=off, ncols=ncols, pt=pt, spar=spar: e.tensor_add(
                            FF(F_ACC + spar * 512 + off, ncols),
                            FF(F_ACC + spar * 512 + off, ncols), BF(pt, ncols))
                    ev_summ[lt] = P.emit("dv", fn, waits=wts, inc="dv")

                def core_AV(lt):
                    qs, kc, qlo, off, diag, first, last = tiles[lt]
                    pt = A_PT + (lt % 4) * 512
                    ncols = QS - off
                    bank = (qs % 2) * 512
                    r = qs // 2
                    wts = [("ac", ev_exp[lt])]
                    dvw = ev_mask.get(lt, 0)
                    if first:
                        dvw = max(dvw, ztm_by_bank[qs % 2])
                        if qs % 2 == 0:
                            dvw = max(dvw, evV[r])
                    if dvw:
                        wts.append(("dv", dvw))
                    ev = P.emit("pe", lambda e, kc=kc, off=off, ncols=ncols, pt=pt, bank=bank, first=first, last=last: e.matmul(
                        pB[:, bank + off:bank + off + ncols], BF(A_V + kc * 128, 128),
                        BF(pt, ncols), start=first, stop=last), waits=wts, inc="pe")
                    ev_avmm[lt] = ev
                    if last:
                        ev_avlast[qs] = ev
                    return ev

                def pe_hook(u):
                    if u in due_rowmm:
                        s = due_rowmm[u]
                        spar = s % 2
                        ev_rowmm[s] = P.emit("pe", lambda e, spar=spar: e.matmul(
                            pD[0:1, :], ones_f[:, 0:1], FF(F_ACC + spar * 512, 512),
                            start=True, stop=True),
                            waits=[("dv", max(ev_summ[last_t[s]], ev_recip.get(s - 1, 0))),
                                   ("ac", ev_accopy.get(s - 1, bar_ac))], inc="pe")
                        rowmm_by_sumpar[spar] = ev_rowmm[s]
                    if u in due_bcast:
                        s = due_bcast[u]
                        ev_bcast[s] = P.emit("pe", lambda e: e.matmul(
                            pD[:, :], onesr_f[0:1, :], rows[0:1, 0:512],
                            start=True, stop=True),
                            waits=[("ac", ev_accopy[s]),
                                   ("dv", ev_recip.get(s - 1, bar_dv))], inc="pe")

                def ac_hook(t):
                    if t in due_accopy:
                        s = due_accopy[t]
                        ev_accopy[s] = P.emit("ac", lambda e: e.copy(
                            rows[0:1, 0:512], pD[0:1, :]),
                            waits=[("pe", ev_rowmm[s])], inc="ac")

                def close_strip(s):
                    done_close.add(s)
                    spar = s % 2
                    zpar = s % 2
                    ev_recip[s] = P.emit("dv", lambda e: e.reciprocal(
                        FF(1024, 512), pD[:, :]),
                        waits=[("pe", ev_bcast[s])], inc="dv")
                    wts = [("pe", ev_avlast[s]), (f"zt{zpar}", zt_store_by_par[zpar])]
                    ev_zt = P.emit("dv", lambda e, spar=spar, zpar=zpar: e.tensor_mul(
                        BF(A_ZT + zpar * 1024, 512), pB[:, spar * 512:spar * 512 + 512],
                        FF(1024, 512)), waits=wts, inc="dv")
                    ztm_by_bank[spar] = ev_zt
                    ev_st = P.emit("sy", lambda e, h=h, s=s, zpar=zpar, j=lyr: e.dma_start(
                        out=za_in[j][s % 2].ap()[s // 2, h * 128:(h + 1) * 128, :],
                        in_=BF(A_ZT + zpar * 1024, 512)),
                        waits=[("dv", ev_zt)], inc=f"zt{zpar}", amt=16)
                    zt_store_by_par[zpar] = ev_st

                def dv_hook(t):
                    if t in due_close:
                        close_strip(due_close[t])

                if os.environ.get("KDRAIN"):
                    drain_through(7)
                for lt in range(T_h):
                    qs = tiles[lt][0]
                    if tiles[lt][5] and qs % 2 == 0:
                        drain_through(qs // 2)
                        if qs == 14:
                            st["prod_pe"] = P.cnt["pe"]
                    core_S(lt)
                    if lt == last_t[qs] and qs % 2 == 1:
                        pair_lastS[qs // 2] = P.cnt["pe"]
                    pump(2)
                    if lt >= 2:
                        core_AV(lt - 2)
                        pe_hook(lt - 2)
                        pump(2)
                    core_exp(lt)
                    ac_hook(lt)
                    core_sums(lt)
                    dv_hook(lt)
                for u in (T_h - 2, T_h - 1):
                    core_AV(u)
                    pe_hook(u)
                drain_through(7)
                for s in sorted(last_t):
                    if s not in ev_rowmm:
                        pe_hook(last_t[s])
                    if s not in ev_accopy:
                        ac_hook(last_t[s] + 2)
                    if s not in ev_bcast:
                        pe_hook(last_t[s] + 1)
                    if s not in done_close:
                        close_strip(s)
                ev_last_av = max(ev_avlast.values())

            if phase <= 3:
                return
            ev_ccz_p = []
            for par in range(2):
                ev_ccz_p.append(P.emit("gp", lambda e, j=lyr, par=par: e.collective_compute(
                    "AllToAll", ALU.bypass, replica_groups=[list(range(NC))],
                    ins=[za_in[j][par].ap().rearrange("s h t -> (s h) t")],
                    outs=[za_out[j][par].ap()]),
                    waits=[(f"zt{par}", P.cnt[f"zt{par}"])], inc=cc_z, amt="dflt"))

            ev_wo = {}
            wo_bar = [("pe", P.cnt["pe"]), ("ac", P.cnt["ac"]), ("dv", P.cnt["dv"])]
            for wh in range(4):
                P.emit("gp", lambda e, wh=wh, wo=wo: e.dma_start(
                    out=BF(A_WO, 32768).rearrange("p (c d) -> p c d", d=2048)[:, :, wh * 512:(wh + 1) * 512],
                    in_=wo.ap().rearrange("(c p) d -> p c d", p=128)[:, :, wh * 512:(wh + 1) * 512]),
                    waits=list(wo_bar), inc="wo", amt=16)
                wo_bar = []
            ev_wo[1] = P.cnt["wo"]
            ev_my = {}
            for par in range(2):
                pw = st["prod_pe"] if par == 0 else ev_last_av
                ev_my[par] = P.emit("sy", lambda e, j=lyr, par=par: e.dma_start(
                    out=BF(A_ZTMY + par * 8192, 8192).rearrange("p (c t) -> p c t", t=512),
                    in_=za_out[j][par].ap().rearrange("(c p) t -> p c t", p=128)),
                    waits=[(cc_z, ev_ccz_p[par]), ("pe", pw)], inc="my", amt=16)

            if phase <= 4:
                return
            bar_ac, bar_dv = P.cnt["ac"], P.cnt["dv"]
            ev_evac = {}
            res_dv_last = 0
            for nh in range(2):
                for dm in range(DC):
                    ps = pA if dm % 2 == 0 else pB
                    for fc in range(DC):
                        wts = []
                        if fc == 0:
                            wts = [("my", ev_my[nh]), ("wo", ev_wo[1]),
                                   ("dv", max(ev_evac.get((nh, dm - 2), 0), bar_dv)),
                                   ("ac", bar_ac)]
                        P.emit("pe", lambda e, ps=ps, dm=dm, fc=fc, nh=nh: e.matmul(
                            ps[:, nh * 512:(nh + 1) * 512],
                            BF(A_WO + fc * 2048 + dm * 128, 128),
                            BF(A_ZTMY + nh * 8192 + fc * 512, 512),
                            start=(fc == 0), stop=(fc == DC - 1)),
                            waits=wts, inc="pe")
                    ev_mm = P.cnt["pe"]
                    res_dv_last = P.emit("dv", lambda e, ps=ps, dm=dm, nh=nh: e.tensor_add(
                        res[:, dm, nh * 512:(nh + 1) * 512],
                        res[:, dm, nh * 512:(nh + 1) * 512],
                        ps[:, nh * 512:(nh + 1) * 512]),
                        waits=[("pe", ev_mm)], inc="dv")
                    ev_evac[(nh, dm)] = res_dv_last
                    if nh == 1 and st["next_ln"]:
                        ln_chunk(dm, "dv", max(res_dv_last, ev_evac[(0, dm)]))
            st["res_dv"] = res_dv_last
            st["attn_done"] = (P.cnt["pe"], res_dv_last)

        def emit_ffn(pass_idx):
            ln_finish(None)
            ln_dv = st["ln_dv"]
            ev_row_cp = 0
            ev_xs = {}
            for th in range(2):
                for c in range(DC):
                    P.emit("pe", lambda e, c=c, th=th: e.matmul(
                        pD[0:1, :], ones_b[:, 0:1], BF(A_XLN + c * 1024 + th * 512, 512),
                        start=(c == 0), stop=(c == DC - 1)),
                        waits=[("dv", ln_dv), ("dv", ev_row_cp)], inc="pe")
                ev_mm = P.cnt["pe"]
                ev_row_cp = P.emit("dv", lambda e, th=th: e.tensor_copy(
                    rows[0:1, th * 512:(th + 1) * 512], pD[0:1, :]), waits=[("pe", ev_mm)], inc="dv")
                ev_sc = P.emit("dv", lambda e, th=th: e.tensor_scalar_mul(
                    rows[0:1, th * 512:(th + 1) * 512], rows[0:1, th * 512:(th + 1) * 512], sb_cv[0:1, 4:5]), inc="dv")
                ev_bc = P.emit("pe", lambda e, th=th: e.matmul(
                    pA[:, th * 512:(th + 1) * 512], onesr_f[0:1, :], rows[0:1, th * 512:(th + 1) * 512],
                    start=True, stop=True), waits=[("dv", ev_sc)], inc="pe")
                ev_xs[th] = P.emit("ac", lambda e, th=th: e.copy(
                    FF(F_ACC, 1024)[:, th * 512:(th + 1) * 512], pA[:, th * 512:(th + 1) * 512]),
                    waits=[("pe", ev_bc)], inc="ac")
            xs128B = FF(F_ACC, 1024)

            pre = st["ffn_pre"]
            ev_wld = {}
            ev_h_done = {}
            for ih in range(2):
                ev_gu_read = {}
                last_h = 0
                last_gu_mm = 0
                for ic in range(32):
                    gi = ih * 32 + ic
                    for nm, wsrc, off in (("g", wgT, A_WS), ("u", wuT, A_WS + 4096)):
                        key = (nm, ih, ic)
                        if pre is not None and ih == 0 and ic < 2 and (nm, ic) in st["ffn_pre"]:
                            ev_wld[key] = st["ffn_pre"][(nm, ic)]
                            continue
                        wts = []
                        if ic >= 2 or ih > 0:
                            wts.append(("pe", ev_gu_read.get(("mm", ic - 2), st.get("wd_done", 0))))
                        ev_wld[key] = P.emit("gp", lambda e, gi=gi, ic=ic, wsrc=wsrc, off=off: e.dma_start(
                            out=BF(off + (ic % 2) * 2048, 2048).rearrange("p (c i) -> p c i", i=128),
                            in_=wsrc.ap()[:, gi * 128:(gi + 1) * 128].rearrange("(c p) i -> p c i", p=128)),
                            waits=wts, inc=f"wg{ic % 2}", amt=16)
                    for nm, ps, off in (("g", pA, A_WS), ("u", pB, A_WS + 4096)):
                        for dc in range(DC):
                            wts = []
                            if dc == 0:
                                wts = [(f"wg{ic % 2}", ev_wld[(nm, ih, ic)]),
                                       ("dv", ev_gu_read.get((nm, ic - 1), 0)),
                                       ("ac", ev_xs[1])]
                            for nh in range(2):
                                P.emit("pe", lambda e, ps=ps, off=off, ic=ic, dc=dc, nh=nh: e.matmul(
                                    ps[:, nh * 512:(nh + 1) * 512],
                                    BF(off + (ic % 2) * 2048 + dc * 128, 128),
                                    BF(A_XLN + dc * 1024 + nh * 512, 512),
                                    start=(dc == 0), stop=(dc == DC - 1)),
                                    waits=wts, inc="pe")
                                wts = []
                        ev_gu_read[("mm_" + nm, ic)] = P.cnt["pe"]
                    ev_gu_read[("mm", ic)] = P.cnt["pe"]
                    last_gu_mm = P.cnt["pe"]
                    ev_tg = P.emit("dv", lambda e: e.tensor_sub(FF(F_T1, 1024), pA[:, :], xs128B),
                                   waits=[("pe", ev_gu_read[("mm_g", ic)])], inc="dv")
                    ev_gu_read[("g", ic)] = ev_tg
                    ev_sl = P.emit("ac", lambda e, gi=gi: e.activation(
                        FF(F_SG, 1024), FF(F_T1, 1024), AF.Silu, scale=sb_gs[:, gi:gi + 1]),
                        waits=[("dv", ev_tg)], inc="ac")
                    ev_tu = P.emit("dv", lambda e: e.tensor_sub(FF(F_T2, 1024), pB[:, :], xs128B),
                                   waits=[("pe", ev_gu_read[("mm_u", ic)])], inc="dv")
                    ev_gu_read[("u", ic)] = ev_tu
                    last_h = P.emit("dv", lambda e, gi=gi, ic=ic: e.scalar_tensor_tensor(
                        BF(A_HT + ic * 1024, 1024), FF(F_T2, 1024), sb_us[:, gi:gi + 1],
                        FF(F_SG, 1024), ALU.mult, ALU.mult),
                        waits=[("ac", ev_sl)], inc="dv")
                    ev_h_done[(ih, ic)] = last_h
                ev_row_cp2 = 0
                ev_hs = {}
                for th in range(2):
                    for ic in range(32):
                        wts = [("dv", ev_h_done[(ih, ic)])] if th == 0 else []
                        if ic == 0:
                            wts.append(("dv", ev_row_cp2))
                        P.emit("pe", lambda e, ic=ic, th=th: e.matmul(
                            pD[0:1, :], ones_b[:, 0:1], BF(A_HT + ic * 1024 + th * 512, 512),
                            start=(ic == 0), stop=(ic == 31)), waits=wts, inc="pe")
                    ev_mm = P.cnt["pe"]
                    ev_row_cp2 = P.emit("dv", lambda e, th=th: e.tensor_copy(
                        rows[0:1, 1024 + th * 512:1024 + (th + 1) * 512], pD[0:1, :]),
                        waits=[("pe", ev_mm)], inc="dv")
                    ev_sc = P.emit("dv", lambda e, th=th: e.tensor_scalar_mul(
                        rows[0:1, 1024 + th * 512:1024 + (th + 1) * 512],
                        rows[0:1, 1024 + th * 512:1024 + (th + 1) * 512], sb_cv[0:1, 4:5]), inc="dv")
                    ev_bc = P.emit("pe", lambda e, th=th: e.matmul(
                        pA[:, th * 512:(th + 1) * 512], onesr_f[0:1, :],
                        rows[0:1, 1024 + th * 512:1024 + (th + 1) * 512],
                        start=True, stop=True),
                        waits=[("dv", ev_sc), ("dv", ev_gu_read[("g", 31)])], inc="pe")
                    ev_hs[th] = P.emit("ac", lambda e, th=th: e.copy(
                        FF(F_ACC2, 1024)[:, th * 512:(th + 1) * 512], pA[:, th * 512:(th + 1) * 512]),
                        waits=[("pe", ev_bc)], inc="ac")
                hs128B = FF(F_ACC2, 1024)

                ev_wd = {}
                ev_devac = {}
                res_dv_last = 0
                for dm in range(DC):
                    wts = [("pe", last_gu_mm)]
                    if dm >= 2:
                        wts.append(("pe", ev_wd.get(("mm", dm - 2), 0)))
                    ev_wd[dm] = P.emit("gp", lambda e, ih=ih, dm=dm: e.dma_start(
                        out=BF(A_WS + (dm % 2) * 4096, 4096).rearrange("p (g i) -> p g i", i=128),
                        in_=wdT.ap().rearrange("(g p) d -> p g d", p=128)[:, ih * 32:(ih + 1) * 32, dm * 128:(dm + 1) * 128]),
                        waits=wts, inc=f"wd{dm % 2}", amt=16)
                    ps = pP if dm % 2 == 0 else pB
                    for ic in range(32):
                        wts = []
                        if ic == 0:
                            wts = [(f"wd{dm % 2}", ev_wd[dm]), ("dv", ev_devac.get(dm - 2, 0)),
                                   ("ac", ev_hs[1])]
                            if dm == 0 or dm == 1:
                                wts.append(("dv", ev_h_done[(ih, 31)]))
                        for nh in range(2):
                            P.emit("pe", lambda e, ps=ps, ih=ih, dm=dm, ic=ic, nh=nh: e.matmul(
                                ps[:, nh * 512:(nh + 1) * 512],
                                BF(A_WS + (dm % 2) * 4096 + ic * 128, 128),
                                BF(A_HT + ic * 1024 + nh * 512, 512),
                                start=(ic == 0), stop=(ic == 31)),
                                waits=wts, inc="pe")
                            wts = []
                    ev_wd[("mm", dm)] = P.cnt["pe"]
                    ev_mm = P.cnt["pe"]
                    ev_sub = P.emit("dv", lambda e, ps=ps: e.tensor_sub(FF(F_T2, 1024), ps[:, :], hs128B),
                                    waits=[("pe", ev_mm)], inc="dv")
                    res_dv_last = P.emit("dv", lambda e, dm=dm: e.scalar_tensor_tensor(
                        res[:, dm, :], FF(F_T2, 1024), sb_ds[:, dm:dm + 1], res[:, dm, :],
                        ALU.mult, ALU.add), inc="dv")
                    ev_devac[dm] = res_dv_last
                    if ih == 1 and st["next_ln"]:
                        ln_chunk(dm, "dv", res_dv_last)
                st["wd_done"] = ev_wd[("mm", DC - 1)]
            st["res_dv"] = res_dv_last

        import os
        _nsub = int(os.environ.get("KNSUB", "4"))
        _ph = int(os.environ.get("KPHASE", "9"))
        for q in range(4):
            ev_q = P.emit("sy", lambda e, q=q: e.dma_start(
                out=res[:, q * 4:(q + 1) * 4, :],
                in_=xT_in.ap().rearrange("(c p) t -> p c t", p=128)[:, q * 4:(q + 1) * 4, :]),
                inc="ld", amt=16)
            for c in range(q * 4, q * 4 + 4):
                ln_chunk(c, "ld", ev_q)
        emit_attn(0, _ph)
        if _nsub >= 2:
            emit_ffn(0)
        if _nsub >= 3:
            emit_attn(1)
        if _nsub >= 4:
            st["next_ln"] = False
            emit_ffn(1)

        if os.environ.get("KDBG"):
            w = [("dv", st["res_dv"]), ("pe", P.cnt["pe"]), ("ac", P.cnt["ac"])]
            P.emit("gp", lambda e: e.dma_start(
                out=outT.ap()[0:128, :], in_=abf[:, A_KT:A_KT + 1024]),
                waits=w, inc="out", amt=16)
            P.emit("gp", lambda e: e.dma_start(
                out=outT.ap()[128:256, :], in_=abf[:, A_QT + 3 * 1024:A_QT + 4 * 1024]),
                inc="out", amt=16)
            P.emit("gp", lambda e: e.dma_start(
                out=outT.ap()[256:384, :], in_=abf[:, A_V:A_V + 1024]),
                inc="out", amt=16)
        else:
            P.emit("sy", lambda e: e.dma_start(
                out=outT.ap().rearrange("(c p) t -> p c t", p=128), in_=res[:, :, :]),
                waits=[("dv", st["res_dv"])], inc="out", amt=16)
        ev_out = P.cnt["out"]
        P.emit("sy", lambda e: e.wait_ge(s_out, ev_out))

        def realize(eng_name, eng):
            hwm = {}
            for fn, waits, incsem, after, amt in P.ops[eng_name]:
                for snm, val in waits:
                    if val and val > hwm.get(snm, 0):
                        eng.wait_ge(SEMS[snm], val)
                        hwm[snm] = val
                ins = fn(eng)
                if incsem is not None:
                    if amt == "dflt":
                        ins.then_inc(SEMS[incsem])
                    else:
                        ins.then_inc(SEMS[incsem], amt)

        @block.tensor
        def _(t):
            realize("pe", t)

        @block.scalar
        def _(s):
            realize("ac", s)

        @block.vector
        def _(v):
            realize("dv", v)

        @block.gpsimd
        def _(g):
            with allow_ici_all_to_all():
                realize("gp", g)

        @block.sync
        def _(sy):
            realize("sy", sy)

    return nc


_NC_CACHE = None


def make_in_maps(L0_attn_a_x, L0_attn_a_W_K, L0_attn_a_W_Q, L0_attn_a_W_V, L0_attn_a_W_O,
                 L0_attn_b_w_qkv, L0_attn_b_w_out,
                 L0_ffn_gate_w, L0_ffn_gate_scale, L0_ffn_up_w, L0_ffn_up_scale,
                 L0_ffn_down_w, L0_ffn_down_scale):
    x = np.asarray(L0_attn_a_x, dtype=np.float32).reshape(S, D)
    W_K = np.asarray(L0_attn_a_W_K, dtype=np.float32)
    W_Q = np.asarray(L0_attn_a_W_Q, dtype=np.float32)
    W_V = np.asarray(L0_attn_a_W_V, dtype=np.float32)
    W_O = np.asarray(L0_attn_a_W_O, dtype=np.float32)
    w_qkv = np.asarray(L0_attn_b_w_qkv, dtype=np.float32)
    w_out = np.asarray(L0_attn_b_w_out, dtype=np.float32)
    gw = np.asarray(L0_ffn_gate_w, dtype=np.int32)
    uw = np.asarray(L0_ffn_up_w, dtype=np.int32)
    dw = np.asarray(L0_ffn_down_w, dtype=np.int32)
    gs = np.asarray(L0_ffn_gate_scale, dtype=np.float32)
    us = np.asarray(L0_ffn_up_scale, dtype=np.float32)
    ds = np.asarray(L0_ffn_down_scale, dtype=np.float32)

    C = np.ascontiguousarray
    woT_a = C(W_O.T)
    woT_b = C(w_out)
    wgT = C(gw.T)
    wuT = C(uw.T)
    wdT = C(dw.T)
    gsT = C(gs.reshape(IC, 128).T)
    usT = C(us.reshape(IC, 128).T)
    dsT = C(ds.reshape(DC, 128).T)
    idn = np.eye(128, dtype=ml_dtypes.bfloat16)
    mask = np.triu(np.ones((128, 128))).astype(ml_dtypes.bfloat16)
    cvec = np.tile(np.array([[EPS, -0.5, INV_SQRT_HD, 1.0 / D, 128.0, 0.0, 0.0, 0.0]],
                            dtype=np.float32), (128, 1))

    in_maps = []
    for c in range(NC):
        h0, h1 = 2 * c, 2 * c + 1
        m = dict(
            xT_in=C(x[c * T:(c + 1) * T, :].T),
            wqT_a=C(np.stack([W_Q[h0].T, W_Q[h1].T])),
            wkT_a=C(np.stack([W_K[h0].T, W_K[h1].T])),
            wvT_a=C(np.stack([W_V[h0].T, W_V[h1].T])),
            woT_a=woT_a,
            wqT_b=C(np.stack([w_qkv[:, h0 * HD:(h0 + 1) * HD], w_qkv[:, h1 * HD:(h1 + 1) * HD]])),
            wkT_b=C(np.stack([w_qkv[:, D + h0 * HD:D + (h0 + 1) * HD], w_qkv[:, D + h1 * HD:D + (h1 + 1) * HD]])),
            wvT_b=C(np.stack([w_qkv[:, 2 * D + h0 * HD:2 * D + (h0 + 1) * HD], w_qkv[:, 2 * D + h1 * HD:2 * D + (h1 + 1) * HD]])),
            woT_b=woT_b,
            wgT=wgT, wuT=wuT, wdT=wdT, gsT=gsT, usT=usT, dsT=dsT,
            idn=idn, mask=mask,
            cvec=cvec,
        )
        in_maps.append(m)
    return in_maps


def kernel(**inputs):
    global _NC_CACHE
    in_maps = make_in_maps(**inputs)
    if _NC_CACHE is None:
        _NC_CACHE = build_program()
    res = run_bass_kernel_spmd(_NC_CACHE, in_maps, core_ids=list(range(NC)))
    out = np.empty((S, D), dtype=np.float32)
    for c in range(NC):
        out[c * T:(c + 1) * T, :] = res.results[c]["outT"].T
    return out.reshape(1, S, D)


# version 21
# speedup vs baseline: 1.1553x; 1.0002x over previous
import os
import numpy as np
import ml_dtypes
import concourse.bass as bass
import concourse.mybir as mybir
from concourse.bass_utils import run_bass_kernel_spmd
from concourse.aot_env import allow_ici_all_to_all

F32 = mybir.dt.float32
BF16 = mybir.dt.bfloat16
I32 = mybir.dt.int32
AF = mybir.ActivationFunctionType
ALU = mybir.AluOpType

NC = 8
S = 8192
T = 1024
D = 2048
DC = D // 128
HD = 128
NHL = 2
FI = 8192
IC = FI // 128
QS = 512
NQS = S // QS
INV_SQRT_HD = 1.0 / np.sqrt(128.0)
EPS = 1e-5

A_XLN = 0
A_QT = 0
A_KT = 4096
A_V = 12288
A_WO = 0
A_HT = 16384
A_WQKV = 20480
A_ZTMY = 32768
A_XG = 32768
A_PT = 45056
A_ZT = 47104
A_WS = 49152
A_END = 57344

F_ACC = 0
F_ACC2 = 1024
F_SG = 2048
F_T1 = 3072
F_T2 = 4096
F_RSTDB = 0
F_MRSTDB = 1024
F_END = 5120


class Plan:
    ENGS = ("pe", "ac", "dv", "gp", "sy")

    def __init__(self):
        self.ops = {e: [] for e in self.ENGS}
        self.cnt = {}

    def post(self, sem, amt):
        self.cnt[sem] = self.cnt.get(sem, 0) + amt
        return self.cnt[sem]

    def emit(self, eng, fn, waits=(), inc=None, amt=1):
        after = self.post(inc, 1 if amt == "dflt" else amt) if inc is not None else None
        self.ops[eng].append((fn, tuple(waits), inc, after, amt))
        return after


def build_program():
    nc = bass.Bass("TRN2", target_bir_lowering=False, detect_race_conditions=False)

    xT_in = nc.dram_tensor("xT_in", [D, T], F32, kind="ExternalInput")
    wqT_a = nc.dram_tensor("wqT_a", [NHL, D, HD], F32, kind="ExternalInput")
    wkT_a = nc.dram_tensor("wkT_a", [NHL, D, HD], F32, kind="ExternalInput")
    wvT_a = nc.dram_tensor("wvT_a", [NHL, D, HD], F32, kind="ExternalInput")
    woT_a = nc.dram_tensor("woT_a", [D, D], F32, kind="ExternalInput")
    wqT_b = nc.dram_tensor("wqT_b", [NHL, D, HD], F32, kind="ExternalInput")
    wkT_b = nc.dram_tensor("wkT_b", [NHL, D, HD], F32, kind="ExternalInput")
    wvT_b = nc.dram_tensor("wvT_b", [NHL, D, HD], F32, kind="ExternalInput")
    woT_b = nc.dram_tensor("woT_b", [D, D], F32, kind="ExternalInput")
    wgT = nc.dram_tensor("wgT", [D, FI], I32, kind="ExternalInput")
    wuT = nc.dram_tensor("wuT", [D, FI], I32, kind="ExternalInput")
    wdT = nc.dram_tensor("wdT", [FI, D], I32, kind="ExternalInput")
    gsT = nc.dram_tensor("gsT", [128, IC], F32, kind="ExternalInput")
    usT = nc.dram_tensor("usT", [128, IC], F32, kind="ExternalInput")
    dsT = nc.dram_tensor("dsT", [128, DC], F32, kind="ExternalInput")
    cvec_i = nc.dram_tensor("cvec", [128, 8], F32, kind="ExternalInput")
    idn_i = nc.dram_tensor("idn", [128, 128], BF16, kind="ExternalInput")
    mask_i = nc.dram_tensor("mask", [128, 128], BF16, kind="ExternalInput")
    outT = nc.dram_tensor("outT", [D, T], F32, kind="ExternalOutput")

    xg_in = [nc.dram_tensor(f"xg_in{j}", [D, T], BF16) for j in range(2)]
    xg_out = [nc.dram_tensor(f"xg_out{j}", [NC * D, T], BF16, addr_space="Shared")
              for j in range(2)]
    za_in = [[nc.dram_tensor(f"za_in{j}_{par}", [NC, NHL * HD, T // 2], BF16)
              for par in range(2)] for j in range(2)]
    za_out = [[nc.dram_tensor(f"za_out{j}_{par}", [D, T // 2], BF16)
               for par in range(2)] for j in range(2)]

    from contextlib import ExitStack
    with ExitStack() as ctx:
        E = ctx.enter_context
        abf = E(nc.sbuf_tensor("abf", [128, A_END], BF16))
        af = E(nc.sbuf_tensor("af", [128, F_END], F32))
        res = E(nc.sbuf_tensor("res", [128, DC, T], F32))
        rows = E(nc.sbuf_tensor("rows", [1, 2048], F32))
        sb_idn = E(nc.sbuf_tensor("sb_idn", [128, 128], BF16))
        sb_mask = E(nc.sbuf_tensor("sb_mask", [128, 128], BF16))
        sb_gs = E(nc.sbuf_tensor("sb_gs", [128, IC], F32))
        sb_us = E(nc.sbuf_tensor("sb_us", [128, IC], F32))
        sb_ds = E(nc.sbuf_tensor("sb_ds", [128, DC], F32))
        sb_cv = E(nc.sbuf_tensor("sb_cv", [128, 8], F32))
        ones_f = E(nc.sbuf_tensor("ones_f", [128, 1], F32))
        ones_b = E(nc.sbuf_tensor("ones_b", [128, 1], BF16))
        onesr_f = E(nc.sbuf_tensor("onesr_f", [1, 128], F32))
        pA = E(nc.psum_tensor("pA", [128, 1024], F32))
        pB = E(nc.psum_tensor("pB", [128, 1024], F32))
        pP = E(nc.psum_tensor("pP", [128, 1024], F32))
        pD = E(nc.psum_tensor("pD", [128, 512], F32))
        pT = E(nc.psum_tensor("pT", [128, 1024], BF16))
        dma_sem_names = ["ld", "xgw", "xg0", "xg1", "xg2", "zt0", "zt1", "my",
                         "wqkv", "wo", "wg0", "wg1", "wd0", "wd1"]
        dma_sems = {n: E(nc.semaphore(f"s_{n}")) for n in dma_sem_names}
        s_pe = E(nc.semaphore("s_pe"))
        s_ac = E(nc.semaphore("s_ac"))
        s_dv = E(nc.semaphore("s_dv"))
        s_cc = [E(nc.semaphore(f"s_cc{j}")) for j in range(4)]
        s_out = E(nc.semaphore("s_out"))
        block = E(nc.Block())

        P = Plan()
        SEMS = {"pe": s_pe, "ac": s_ac, "dv": s_dv,
                "cc0": s_cc[0], "cc1": s_cc[1], "cc2": s_cc[2], "cc3": s_cc[3],
                "out": s_out, **dma_sems}

        def BF(c0, n):
            return abf[:, c0:c0 + n]

        def FF(c0, n):
            return af[:, c0:c0 + n]

        for name, dst, src in (("idn", sb_idn, idn_i), ("mask", sb_mask, mask_i),
                               ("gs", sb_gs, gsT), ("us", sb_us, usT), ("ds", sb_ds, dsT),
                               ("cv", sb_cv, cvec_i)):
            P.emit("sy", lambda e, d=dst, s=src: e.dma_start(out=d[:, :], in_=s.ap()),
                   inc="ld", amt=16)
        ev_consts = P.cnt["ld"]
        P.emit("dv", lambda e: e.memset(ones_f[:, :], 1.0), inc="dv")
        P.emit("dv", lambda e: e.memset(ones_b[:, :], 1.0), inc="dv")
        P.emit("dv", lambda e: e.memset(onesr_f[:, :], 1.0), inc="dv")
        ev_ones = P.cnt["dv"]

        st = {"res_dv": 0, "lnp": {}, "next_ln": True}

        def ln_chunk(c, wsem, wval):
            stl = st["lnp"]
            acc, acc2 = FF(F_ACC, 1024), FF(F_T1, 1024)
            sq = FF(F_SG, 1024)
            w = [(wsem, wval)] if wsem else []
            if c == 0:
                stl["acc"] = P.emit("dv", lambda e: e.tensor_copy(acc, res[:, 0, :]),
                                    waits=w + [("ld", ev_consts)], inc="dv")
            else:
                stl["acc"] = P.emit("dv", lambda e, c=c: e.tensor_add(acc, acc, res[:, c, :]),
                                    waits=w, inc="dv")
            wts = list(w)
            if c > 0:
                wts.append(("dv", stl["sqa"]))
            ev_s = P.emit("ac", lambda e, c=c: e.activation(sq, res[:, c, :], AF.Square),
                          waits=wts, inc="ac")
            if c == 0:
                stl["sqa"] = P.emit("dv", lambda e: e.tensor_copy(acc2, sq),
                                    waits=[("ac", ev_s)], inc="dv")
            else:
                stl["sqa"] = P.emit("dv", lambda e, c=c: e.tensor_add(acc2, acc2, sq),
                                    waits=[("ac", ev_s)], inc="dv")

        def ln_finish(write_xg_idx):
            acc, acc2 = FF(F_ACC, 1024), FF(F_T1, 1024)
            ev_acc = st["lnp"]["acc"]
            ev_acc2 = st["lnp"]["sqa"]
            ln_ac_bar = P.cnt.get("ac", 0)
            r_sum = [rows[0:1, 0:512], rows[0:1, 1024:1536]]
            r_sq = [rows[0:1, 512:1024], rows[0:1, 1536:2048]]
            ev_row_cp = 0
            for th in range(2):
                ev_mm = P.emit("pe", lambda e, th=th: e.matmul(
                    pD[0:1, :], ones_f[:, 0:1], acc[:, th * 512:(th + 1) * 512],
                    start=True, stop=True),
                    waits=[("dv", ev_acc), ("dv", ev_row_cp), ("ac", ln_ac_bar)], inc="pe")
                ev_row_cp = P.emit("dv", lambda e, th=th: e.tensor_copy(r_sum[th], pD[0:1, :]),
                                   waits=[("pe", ev_mm)], inc="dv")
                ev_mm = P.emit("pe", lambda e, th=th: e.matmul(
                    pD[0:1, :], ones_f[:, 0:1], acc2[:, th * 512:(th + 1) * 512],
                    start=True, stop=True),
                    waits=[("dv", ev_acc2), ("dv", ev_row_cp)], inc="pe")
                ev_row_cp = P.emit("dv", lambda e, th=th: e.tensor_copy(r_sq[th], pD[0:1, :]),
                                   waits=[("pe", ev_mm)], inc="dv")
            tmp = af[0:1, 0:512]
            ev_bc = {}
            for th in range(2):
                P.emit("dv", lambda e, th=th: e.tensor_scalar_mul(r_sum[th], r_sum[th], sb_cv[0:1, 3:4]), inc="dv")
                P.emit("dv", lambda e, th=th: e.tensor_scalar_mul(r_sq[th], r_sq[th], sb_cv[0:1, 3:4]), inc="dv")
                P.emit("dv", lambda e, th=th: e.tensor_mul(tmp, r_sum[th], r_sum[th]), inc="dv")
                ev_var = P.emit("dv", lambda e, th=th: e.tensor_sub(r_sq[th], r_sq[th], tmp), inc="dv")
                ev_ln = P.emit("ac", lambda e, th=th: e.activation(r_sq[th], r_sq[th], AF.Ln, bias=sb_cv[0:1, 0:1]),
                               waits=[("dv", ev_var)], inc="ac")
                ev_rstd = P.emit("ac", lambda e, th=th: e.activation(r_sq[th], r_sq[th], AF.Exp, scale=sb_cv[0:1, 1:2]),
                                 inc="ac")
                ev_mr = P.emit("dv", lambda e, th=th: e.tensor_mul(r_sum[th], r_sum[th], r_sq[th]),
                               waits=[("ac", ev_rstd)], inc="dv")
                ev1 = P.emit("pe", lambda e, th=th: e.matmul(
                    pA[:, th * 512:(th + 1) * 512], onesr_f[0:1, :], r_sq[th],
                    start=True, stop=True), waits=[("ac", ev_rstd), ("dv", ev_ones)], inc="pe")
                ev2 = P.emit("pe", lambda e, th=th: e.matmul(
                    pB[:, th * 512:(th + 1) * 512], onesr_f[0:1, :], r_sum[th],
                    start=True, stop=True), waits=[("dv", ev_mr)], inc="pe")
                P.emit("ac", lambda e, th=th: e.copy(
                    FF(F_RSTDB, 1024)[:, th * 512:(th + 1) * 512], pA[:, th * 512:(th + 1) * 512]),
                    waits=[("pe", ev1)], inc="ac")
                ev_bc[th] = P.emit("ac", lambda e, th=th: e.copy(
                    FF(F_MRSTDB, 1024)[:, th * 512:(th + 1) * 512], pB[:, th * 512:(th + 1) * 512]),
                    waits=[("pe", ev2)], inc="ac")
            t1 = [FF(F_SG, 1024), FF(F_T2, 1024)]
            last_norm = 0
            ev_xgw = []
            st["lnnorm"] = []
            for c in range(DC):
                P.emit("dv", lambda e, c=c: e.tensor_mul(t1[c % 2], res[:, c, :], FF(F_RSTDB, 1024)),
                       waits=[("ac", ev_bc[1])], inc="dv")
                last_norm = P.emit("dv", lambda e, c=c: e.tensor_sub(
                    BF(A_XLN + c * 1024, 1024), t1[c % 2], FF(F_MRSTDB, 1024)), inc="dv")
                st["lnnorm"].append(last_norm)
                if write_xg_idx is not None and c == DC - 1:
                    ev_xgw.append(P.emit("sy", lambda e, j=write_xg_idx: e.dma_start(
                        out=xg_in[j].ap().rearrange("(c p) t -> p c t", p=128),
                        in_=abf[:, 0:DC * 1024].rearrange("p (c t) -> p c t", t=1024)),
                        waits=[("dv", last_norm)], inc="xgw", amt=16))
            st["ln_dv"] = last_norm
            st["ln_ac"] = ev_bc[1]
            if write_xg_idx is not None:
                return ev_xgw
            return None

        vcp_list = [0]
        zt_store_by_par = {0: 0, 1: 0}
        rowmm_by_sumpar = {0: 0, 1: 0}
        ztm_by_bank = {0: 0, 1: 0}
        xg_consumed = {0: 0, 1: 0, 2: 0}

        def emit_attn(lyr, phase=9):
            cc_g = f"cc{lyr * 2}"
            cc_z = f"cc{lyr * 2 + 1}"
            wq, wk, wv = (wqT_a, wkT_a, wvT_a) if lyr == 0 else (wqT_b, wkT_b, wvT_b)
            wo = woT_a if lyr == 0 else woT_b

            wq_waits = []
            if lyr == 1:
                wq_waits = [("pe", P.cnt["pe"]), ("dv", P.cnt["dv"])]
            ev_wqkv = {}
            for h in range(NHL):
                for pi, wsrc in enumerate((wq, wk, wv)):
                    ev_wqkv[(h, pi)] = P.emit("gp", lambda e, h=h, pi=pi, wsrc=wsrc: e.dma_start(
                        out=BF(A_WQKV + h * 6144 + pi * 2048, 2048).rearrange("p (c i) -> p c i", i=128),
                        in_=wsrc.ap()[h, :, :].rearrange("(c p) i -> p c i", p=128)),
                        waits=wq_waits, inc="wqkv", amt=16)
                    wq_waits = []
            ev_wqkv_all = P.cnt["wqkv"]

            ev_xgw = ln_finish(lyr)
            ev_ccg = P.emit("gp", lambda e, j=lyr: e.collective_compute(
                "AllGather", ALU.bypass, replica_groups=[list(range(NC))],
                ins=[xg_in[j].ap().opt()], outs=[xg_out[j].ap().opt()]),
                waits=[("xgw", ev_xgw[0])], inc=cc_g, amt="dflt")

            pre_waits = [("pe", P.cnt["pe"])] if lyr == 1 else []
            ev_pre = {}
            for ic in range(2):
                for nm, wsrc, off in (("g", wgT, A_WS), ("u", wuT, A_WS + 4096)):
                    ev_pre[(nm, ic)] = P.emit("gp", lambda e, ic=ic, wsrc=wsrc, off=off: e.dma_start(
                        out=BF(off + ic * 2048, 2048).rearrange("p (c i) -> p c i", i=128),
                        in_=wsrc.ap()[:, ic * 128:(ic + 1) * 128].rearrange("(c p) i -> p c i", p=128)),
                        waits=pre_waits, inc=f"wg{ic % 2}", amt=16)
                    pre_waits = []
            st["ffn_pre"] = ev_pre
            if phase <= 1:
                P.emit("gp", lambda e: e.engine_nop(), waits=[(cc_g, 1)])
                return

            entry_dv = P.cnt["dv"]
            ztm_by_bank[0] = max(ztm_by_bank[0], entry_dv)
            ztm_by_bank[1] = max(ztm_by_bank[1], entry_dv)
            bank_free = {0: ("dv", entry_dv), 1: ("dv", entry_dv)}
            stage_tr_prev = [0]
            ev_last_av = 0
            for h in range(NHL):
                bar_pe, bar_ac, bar_dv = P.cnt["pe"], P.cnt["ac"], P.cnt["dv"]
                wbase = A_WQKV + h * 6144
                evQ = {}
                evK = {}
                evV = {}
                pair_lastS = {}
                first_prod = [True]

                def make_prod(r):
                    def gen():
                        ld = {}

                        def emit_load(i):
                            nh, fh = ((0, 0), (0, 1), (1, 0), (1, 1))[i]
                            b = (r * 4 + i) % 3
                            w = [(cc_g, ev_ccg), ("pe", xg_consumed[b])]
                            ld[(nh, fh)] = (b, P.emit("sy", lambda e, r=r, nh=nh, fh=fh, b=b, j=lyr: e.dma_start(
                                out=BF(A_XG + b * 4096, 4096).rearrange("p (c t) -> p c t", t=512),
                                in_=xg_out[j].ap()[r * D + fh * 1024:r * D + fh * 1024 + 1024, nh * 512:(nh + 1) * 512]
                                    .rearrange("(c p) t -> p c t", p=128)),
                                waits=w, inc=f"xg{b}", amt=16))
                        for i in range(3):
                            emit_load(i)
                        ev_stage = {}
                        for g, (nh, pi) in enumerate((n, p) for n in (0, 1) for p in (0, 1, 2)):
                            bk = g % 2
                            for dc in range(DC):
                                fh = dc // 8
                                b, ev_ld = ld[(nh, fh)]
                                wts = []
                                if dc == 0:
                                    wts = [(f"xg{b}", ev_ld), ("wqkv", ev_wqkv_all)]
                                    fe, fv = bank_free[bk]
                                    if fv:
                                        wts.append((fe, fv))
                                    if first_prod[0]:
                                        wts += [("ac", bar_ac), ("dv", bar_dv)]
                                        first_prod[0] = False
                                elif dc == 8:
                                    wts = [(f"xg{b}", ev_ld)]
                                P.emit("pe", lambda e, bk=bk, pi=pi, dc=dc, b=b, wbase=wbase: e.matmul(
                                    pP[:, bk * 512:(bk + 1) * 512],
                                    BF(wbase + pi * 2048 + dc * 128, 128),
                                    BF(A_XG + b * 4096 + (dc % 8) * 512, 512),
                                    start=(dc == 0), stop=(dc == DC - 1)),
                                    waits=wts, inc="pe")
                                if pi == 2 and dc == 7:
                                    xg_consumed[ld[(nh, 0)][0]] = P.cnt["pe"]
                                    if nh == 0:
                                        emit_load(3)
                                if pi == 2 and dc == 15:
                                    xg_consumed[ld[(nh, 1)][0]] = P.cnt["pe"]
                                yield
                            ev_mm = P.cnt["pe"]
                            if pi == 0:
                                wts = [("pe", max(ev_mm, pair_lastS.get(r - 4, bar_pe)))]
                                evq = P.emit("ac", lambda e, r=r, nh=nh, bk=bk: e.copy(
                                    BF(A_QT + (r % 4) * 1024 + nh * 512, 512),
                                    pP[:, bk * 512:(bk + 1) * 512]), waits=wts, inc="ac")
                                evQ[r] = evq
                                bank_free[bk] = ("ac", evq)
                            elif pi == 1:
                                evk = P.emit("dv", lambda e, r=r, nh=nh, bk=bk: e.tensor_copy(
                                    BF(A_KT + r * 1024 + nh * 512, 512),
                                    pP[:, bk * 512:(bk + 1) * 512]),
                                    waits=[("pe", max(ev_mm, bar_pe))], inc="dv")
                                evK[r] = evk
                                bank_free[bk] = ("dv", evk)
                            else:
                                wts = [("pe", max(ev_mm, stage_tr_prev[0]))]
                                evs = P.emit("ac", lambda e, nh=nh, bk=bk: e.copy(
                                    BF(A_ZT + 512 + nh * 1024, 512),
                                    pP[:, bk * 512:(bk + 1) * 512]), waits=wts, inc="ac")
                                ev_stage[nh] = evs
                                bank_free[bk] = ("ac", evs)
                        ev_tr = 0
                        for tt in range(8):
                            src = A_ZT + 512 + (tt // 4) * 1024 + (tt % 4) * 128
                            wts = [("ac", ev_stage[tt // 4]), ("dv", vcp_list[-1]),
                                   ("ld", ev_consts)]
                            ev_tr = P.emit("pe", lambda e, src=src, tt=tt: e.transpose(
                                pT[:, tt * 128:(tt + 1) * 128], BF(src, 128), sb_idn[:, :]),
                                waits=wts, inc="pe")
                            yield
                        stage_tr_prev[0] = ev_tr
                        evV[r] = P.emit("dv", lambda e, r=r: e.tensor_copy(
                            BF(A_V + r * 1024, 1024), pT[:, :]),
                            waits=[("pe", ev_tr)], inc="dv")
                        vcp_list.append(evV[r])
                    return gen()

                gens = [make_prod(r) for r in range(8)]
                gi = [0]

                def pump(n):
                    while n > 0 and gi[0] < 8:
                        try:
                            next(gens[gi[0]])
                            n -= 1
                        except StopIteration:
                            gi[0] += 1

                def drain_through(r):
                    while gi[0] <= r:
                        try:
                            next(gens[gi[0]])
                        except StopIteration:
                            gi[0] += 1

                tiles = []
                for qs in range(NQS):
                    nkc = (qs + 1) * 4
                    for j, kc in enumerate(range(nkc)):
                        qlo = max(qs * QS, kc * 128)
                        tiles.append((qs, kc, qlo, qlo - qs * QS,
                                      kc * 128 >= qs * QS, j == 0, j == nkc - 1))
                T_h = len(tiles)
                last_t = {}
                for lt, td in enumerate(tiles):
                    if td[6]:
                        last_t[td[0]] = lt
                due_rowmm = {last_t[s]: s for s in last_t}
                due_bcast = {last_t[s] + 1: s for s in last_t}
                due_accopy = {last_t[s] + 2: s for s in last_t}
                due_close = {last_t[s] + 8: s for s in last_t}
                ev_S = {}
                ev_exp = {}
                ev_avmm = {}
                ev_summ = {}
                ev_mask = {}
                ev_rowmm = {}
                ev_accopy = {}
                ev_bcast = {}
                ev_recip = {}
                ev_avlast = {}
                done_close = set()

                def core_S(lt):
                    qs, kc, qlo, off, diag, first, last = tiles[lt]
                    pb = (lt % 2) * 512
                    r = qs // 2
                    ncols = QS - off
                    acw = max(ev_exp.get(lt - 2, 0), bar_ac if lt < 2 else 0)
                    wts = []
                    if first and qs % 2 == 0:
                        acw = max(acw, evQ[r])
                        wts.append(("dv", evK[r]))
                    wts.append(("ac", acw))
                    qsrc = A_QT + (r % 4) * 1024 + qlo - r * 1024
                    ev_S[lt] = P.emit("pe", lambda e, kc=kc, qsrc=qsrc, ncols=ncols, pb=pb: e.matmul(
                        pA[:, pb:pb + ncols],
                        BF(A_KT + kc * 128, 128), BF(qsrc, ncols),
                        start=True, stop=True), waits=wts, inc="pe")

                def core_exp(lt):
                    qs, kc, qlo, off, diag, first, last = tiles[lt]
                    pb = (lt % 2) * 512
                    pt = A_PT + (lt % 4) * 512
                    ncols = QS - off
                    wts = [("pe", ev_S[lt]), ("dv", ev_summ.get(lt - 4, bar_dv))]
                    ev_exp[lt] = P.emit("ac", lambda e, ncols=ncols, pb=pb, pt=pt: e.activation(
                        BF(pt, ncols), pA[:, pb:pb + ncols],
                        AF.Exp, scale=sb_cv[:, 2:3]), waits=wts, inc="ac")

                def core_sums(lt):
                    qs, kc, qlo, off, diag, first, last = tiles[lt]
                    pt = A_PT + (lt % 4) * 512
                    ncols = QS - off
                    spar = qs % 2
                    if diag:
                        ev_mask[lt] = P.emit("dv", lambda e, pt=pt: e.tensor_mul(
                            BF(pt, 128), BF(pt, 128), sb_mask[:, :]),
                            waits=[("ac", ev_exp[lt])], inc="dv")
                        wts = []
                    else:
                        wts = [("ac", ev_exp[lt])]
                    if first:
                        wts.append(("pe", rowmm_by_sumpar[spar]))
                        fn = lambda e, off=off, ncols=ncols, pt=pt, spar=spar: e.tensor_copy(
                            FF(F_ACC + spar * 512 + off, ncols), BF(pt, ncols))
                    else:
                        fn = lambda e, off=off, ncols=ncols, pt=pt, spar=spar: e.tensor_add(
                            FF(F_ACC + spar * 512 + off, ncols),
                            FF(F_ACC + spar * 512 + off, ncols), BF(pt, ncols))
                    ev_summ[lt] = P.emit("dv", fn, waits=wts, inc="dv")

                def core_AV(lt):
                    qs, kc, qlo, off, diag, first, last = tiles[lt]
                    pt = A_PT + (lt % 4) * 512
                    ncols = QS - off
                    bank = (qs % 2) * 512
                    r = qs // 2
                    wts = [("ac", ev_exp[lt])]
                    dvw = ev_mask.get(lt, 0)
                    if first:
                        dvw = max(dvw, ztm_by_bank[qs % 2])
                        if qs % 2 == 0:
                            dvw = max(dvw, evV[r])
                    if dvw:
                        wts.append(("dv", dvw))
                    ev = P.emit("pe", lambda e, kc=kc, off=off, ncols=ncols, pt=pt, bank=bank, first=first, last=last: e.matmul(
                        pB[:, bank + off:bank + off + ncols], BF(A_V + kc * 128, 128),
                        BF(pt, ncols), start=first, stop=last), waits=wts, inc="pe")
                    ev_avmm[lt] = ev
                    if last:
                        ev_avlast[qs] = ev
                    return ev

                def pe_hook(u):
                    if u in due_rowmm:
                        s = due_rowmm[u]
                        spar = s % 2
                        ev_rowmm[s] = P.emit("pe", lambda e, spar=spar: e.matmul(
                            pD[0:1, :], ones_f[:, 0:1], FF(F_ACC + spar * 512, 512),
                            start=True, stop=True),
                            waits=[("dv", max(ev_summ[last_t[s]], ev_recip.get(s - 1, 0))),
                                   ("ac", ev_accopy.get(s - 1, bar_ac))], inc="pe")
                        rowmm_by_sumpar[spar] = ev_rowmm[s]
                    if u in due_bcast:
                        s = due_bcast[u]
                        ev_bcast[s] = P.emit("pe", lambda e: e.matmul(
                            pD[:, :], onesr_f[0:1, :], rows[0:1, 0:512],
                            start=True, stop=True),
                            waits=[("ac", ev_accopy[s]),
                                   ("dv", ev_recip.get(s - 1, bar_dv))], inc="pe")

                def ac_hook(t):
                    if t in due_accopy:
                        s = due_accopy[t]
                        ev_accopy[s] = P.emit("ac", lambda e: e.copy(
                            rows[0:1, 0:512], pD[0:1, :]),
                            waits=[("pe", ev_rowmm[s])], inc="ac")

                def close_strip(s):
                    done_close.add(s)
                    spar = s % 2
                    zpar = s % 2
                    ev_recip[s] = P.emit("dv", lambda e: e.reciprocal(
                        FF(1024, 512), pD[:, :]),
                        waits=[("pe", ev_bcast[s])], inc="dv")
                    wts = [("pe", ev_avlast[s]), (f"zt{zpar}", zt_store_by_par[zpar])]
                    ev_zt = P.emit("dv", lambda e, spar=spar, zpar=zpar: e.tensor_mul(
                        BF(A_ZT + zpar * 1024, 512), pB[:, spar * 512:spar * 512 + 512],
                        FF(1024, 512)), waits=wts, inc="dv")
                    ztm_by_bank[spar] = ev_zt
                    ev_st = P.emit("sy", lambda e, h=h, s=s, zpar=zpar, j=lyr: e.dma_start(
                        out=za_in[j][s % 2].ap()[s // 2, h * 128:(h + 1) * 128, :],
                        in_=BF(A_ZT + zpar * 1024, 512)),
                        waits=[("dv", ev_zt)], inc=f"zt{zpar}", amt=16)
                    zt_store_by_par[zpar] = ev_st

                def dv_hook(t):
                    if t in due_close:
                        close_strip(due_close[t])

                if os.environ.get("KDRAIN"):
                    drain_through(7)
                for lt in range(T_h):
                    qs = tiles[lt][0]
                    if tiles[lt][5] and qs % 2 == 0:
                        drain_through(qs // 2)
                    core_S(lt)
                    if lt == last_t[qs] and qs % 2 == 1:
                        pair_lastS[qs // 2] = P.cnt["pe"]
                    pump(2)
                    if lt >= 2:
                        core_AV(lt - 2)
                        pe_hook(lt - 2)
                        pump(2)
                    core_exp(lt)
                    ac_hook(lt)
                    core_sums(lt)
                    dv_hook(lt)
                for u in (T_h - 2, T_h - 1):
                    core_AV(u)
                    pe_hook(u)
                drain_through(7)
                for s in sorted(last_t):
                    if s not in ev_rowmm:
                        pe_hook(last_t[s])
                    if s not in ev_accopy:
                        ac_hook(last_t[s] + 2)
                    if s not in ev_bcast:
                        pe_hook(last_t[s] + 1)
                    if s not in done_close:
                        close_strip(s)
                ev_last_av = max(ev_avlast.values())

            if phase <= 3:
                return
            ev_ccz_p = []
            for par in range(2):
                ev_ccz_p.append(P.emit("gp", lambda e, j=lyr, par=par: e.collective_compute(
                    "AllToAll", ALU.bypass, replica_groups=[list(range(NC))],
                    ins=[za_in[j][par].ap().rearrange("s h t -> (s h) t")],
                    outs=[za_out[j][par].ap()]),
                    waits=[(f"zt{par}", P.cnt[f"zt{par}"])], inc=cc_z, amt="dflt"))

            ev_wo = {}
            for wh in range(2):
                ev_wo[wh] = P.emit("gp", lambda e, wh=wh, wo=wo: e.dma_start(
                    out=BF(A_WO, 32768).rearrange("p (c d) -> p c d", d=2048)[:, :, wh * 1024:(wh + 1) * 1024],
                    in_=wo.ap().rearrange("(c p) d -> p c d", p=128)[:, :, wh * 1024:(wh + 1) * 1024]),
                    waits=[("pe", ev_last_av), ("ac", P.cnt["ac"]), ("dv", P.cnt["dv"])],
                    inc="wo", amt=16)
            ev_my = {}
            for par in range(2):
                ev_my[par] = P.emit("sy", lambda e, j=lyr, par=par: e.dma_start(
                    out=BF(A_ZTMY, 16384).rearrange("p (c t) -> p c t", t=1024)[:, :, par * 512:(par + 1) * 512],
                    in_=za_out[j][par].ap().rearrange("(c p) t -> p c t", p=128)),
                    waits=[(cc_z, ev_ccz_p[par]), ("pe", ev_last_av)], inc="my", amt=16)

            if phase <= 4:
                return
            bar_ac, bar_dv = P.cnt["ac"], P.cnt["dv"]
            ev_evac = {}
            res_dv_last = 0
            for nh in range(2):
                for dm in range(DC):
                    ps = pA if dm % 2 == 0 else pB
                    for fc in range(DC):
                        wts = []
                        if fc == 0:
                            wts = [("my", ev_my[nh]), ("wo", ev_wo[1]),
                                   ("dv", max(ev_evac.get((nh, dm - 2), 0), bar_dv)),
                                   ("ac", bar_ac)]
                        P.emit("pe", lambda e, ps=ps, dm=dm, fc=fc, nh=nh: e.matmul(
                            ps[:, nh * 512:(nh + 1) * 512],
                            BF(A_WO + fc * 2048 + dm * 128, 128),
                            BF(A_ZTMY + fc * 1024 + nh * 512, 512),
                            start=(fc == 0), stop=(fc == DC - 1)),
                            waits=wts, inc="pe")
                    ev_mm = P.cnt["pe"]
                    res_dv_last = P.emit("dv", lambda e, ps=ps, dm=dm, nh=nh: e.tensor_add(
                        res[:, dm, nh * 512:(nh + 1) * 512],
                        res[:, dm, nh * 512:(nh + 1) * 512],
                        ps[:, nh * 512:(nh + 1) * 512]),
                        waits=[("pe", ev_mm)], inc="dv")
                    ev_evac[(nh, dm)] = res_dv_last
                    if nh == 1 and st["next_ln"]:
                        ln_chunk(dm, "dv", max(res_dv_last, ev_evac[(0, dm)]))
            st["res_dv"] = res_dv_last
            st["attn_done"] = (P.cnt["pe"], res_dv_last)

        def emit_ffn(pass_idx):
            ln_finish(None)
            ln_dv = st["ln_dv"]
            ev_row_cp = 0
            ev_xs = {}
            for th in range(2):
                for c in range(DC):
                    P.emit("pe", lambda e, c=c, th=th: e.matmul(
                        pD[0:1, :], ones_b[:, 0:1], BF(A_XLN + c * 1024 + th * 512, 512),
                        start=(c == 0), stop=(c == DC - 1)),
                        waits=[("dv", st["lnnorm"][c]), ("dv", ev_row_cp)], inc="pe")
                ev_mm = P.cnt["pe"]
                ev_row_cp = P.emit("dv", lambda e, th=th: e.tensor_copy(
                    rows[0:1, th * 512:(th + 1) * 512], pD[0:1, :]), waits=[("pe", ev_mm)], inc="dv")
                ev_sc = P.emit("dv", lambda e, th=th: e.tensor_scalar_mul(
                    rows[0:1, th * 512:(th + 1) * 512], rows[0:1, th * 512:(th + 1) * 512], sb_cv[0:1, 4:5]), inc="dv")
                ev_bc = P.emit("pe", lambda e, th=th: e.matmul(
                    pA[:, th * 512:(th + 1) * 512], onesr_f[0:1, :], rows[0:1, th * 512:(th + 1) * 512],
                    start=True, stop=True), waits=[("dv", ev_sc)], inc="pe")
                ev_xs[th] = P.emit("ac", lambda e, th=th: e.copy(
                    FF(F_ACC, 1024)[:, th * 512:(th + 1) * 512], pA[:, th * 512:(th + 1) * 512]),
                    waits=[("pe", ev_bc)], inc="ac")
            xs128B = FF(F_ACC, 1024)

            pre = st["ffn_pre"]
            ev_wld = {}
            ev_h_done = {}
            for ih in range(2):
                ev_gu_read = {}
                last_h = 0
                last_gu_mm = 0
                for ic in range(32):
                    gi = ih * 32 + ic
                    for nm, wsrc, off in (("g", wgT, A_WS), ("u", wuT, A_WS + 4096)):
                        key = (nm, ih, ic)
                        if pre is not None and ih == 0 and ic < 2 and (nm, ic) in st["ffn_pre"]:
                            ev_wld[key] = st["ffn_pre"][(nm, ic)]
                            continue
                        wts = []
                        if ic >= 2 or ih > 0:
                            wts.append(("pe", ev_gu_read.get(("mm", ic - 2), st.get("wd_done", 0))))
                        ev_wld[key] = P.emit("gp", lambda e, gi=gi, ic=ic, wsrc=wsrc, off=off: e.dma_start(
                            out=BF(off + (ic % 2) * 2048, 2048).rearrange("p (c i) -> p c i", i=128),
                            in_=wsrc.ap()[:, gi * 128:(gi + 1) * 128].rearrange("(c p) i -> p c i", p=128)),
                            waits=wts, inc=f"wg{ic % 2}", amt=16)
                    for nm, ps, off in (("g", pA, A_WS), ("u", pB, A_WS + 4096)):
                        for dc in range(DC):
                            wts = []
                            if dc == 0:
                                wts = [(f"wg{ic % 2}", ev_wld[(nm, ih, ic)]),
                                       ("dv", ev_gu_read.get((nm, ic - 1), 0)),
                                       ("ac", ev_xs[1])]
                            for nh in range(2):
                                P.emit("pe", lambda e, ps=ps, off=off, ic=ic, dc=dc, nh=nh: e.matmul(
                                    ps[:, nh * 512:(nh + 1) * 512],
                                    BF(off + (ic % 2) * 2048 + dc * 128, 128),
                                    BF(A_XLN + dc * 1024 + nh * 512, 512),
                                    start=(dc == 0), stop=(dc == DC - 1)),
                                    waits=wts, inc="pe")
                                wts = []
                        ev_gu_read[("mm_" + nm, ic)] = P.cnt["pe"]
                    ev_gu_read[("mm", ic)] = P.cnt["pe"]
                    last_gu_mm = P.cnt["pe"]
                    ev_tg = P.emit("dv", lambda e: e.tensor_sub(FF(F_T1, 1024), pA[:, :], xs128B),
                                   waits=[("pe", ev_gu_read[("mm_g", ic)])], inc="dv")
                    ev_gu_read[("g", ic)] = ev_tg
                    ev_sl = P.emit("ac", lambda e, gi=gi: e.activation(
                        FF(F_SG, 1024), FF(F_T1, 1024), AF.Silu, scale=sb_gs[:, gi:gi + 1]),
                        waits=[("dv", ev_tg)], inc="ac")
                    ev_tu = P.emit("dv", lambda e: e.tensor_sub(FF(F_T2, 1024), pB[:, :], xs128B),
                                   waits=[("pe", ev_gu_read[("mm_u", ic)])], inc="dv")
                    ev_gu_read[("u", ic)] = ev_tu
                    last_h = P.emit("dv", lambda e, gi=gi, ic=ic: e.scalar_tensor_tensor(
                        BF(A_HT + ic * 1024, 1024), FF(F_T2, 1024), sb_us[:, gi:gi + 1],
                        FF(F_SG, 1024), ALU.mult, ALU.mult),
                        waits=[("ac", ev_sl)], inc="dv")
                    ev_h_done[(ih, ic)] = last_h
                ev_row_cp2 = 0
                ev_hs = {}
                for th in range(2):
                    for ic in range(32):
                        wts = [("dv", ev_h_done[(ih, ic)])] if th == 0 else []
                        if ic == 0:
                            wts.append(("dv", ev_row_cp2))
                        P.emit("pe", lambda e, ic=ic, th=th: e.matmul(
                            pD[0:1, :], ones_b[:, 0:1], BF(A_HT + ic * 1024 + th * 512, 512),
                            start=(ic == 0), stop=(ic == 31)), waits=wts, inc="pe")
                    ev_mm = P.cnt["pe"]
                    ev_row_cp2 = P.emit("dv", lambda e, th=th: e.tensor_copy(
                        rows[0:1, 1024 + th * 512:1024 + (th + 1) * 512], pD[0:1, :]),
                        waits=[("pe", ev_mm)], inc="dv")
                    ev_sc = P.emit("dv", lambda e, th=th: e.tensor_scalar_mul(
                        rows[0:1, 1024 + th * 512:1024 + (th + 1) * 512],
                        rows[0:1, 1024 + th * 512:1024 + (th + 1) * 512], sb_cv[0:1, 4:5]), inc="dv")
                    ev_bc = P.emit("pe", lambda e, th=th: e.matmul(
                        pA[:, th * 512:(th + 1) * 512], onesr_f[0:1, :],
                        rows[0:1, 1024 + th * 512:1024 + (th + 1) * 512],
                        start=True, stop=True),
                        waits=[("dv", ev_sc), ("dv", ev_gu_read[("g", 31)])], inc="pe")
                    ev_hs[th] = P.emit("ac", lambda e, th=th: e.copy(
                        FF(F_ACC2, 1024)[:, th * 512:(th + 1) * 512], pA[:, th * 512:(th + 1) * 512]),
                        waits=[("pe", ev_bc)], inc="ac")
                hs128B = FF(F_ACC2, 1024)

                ev_wd = {}
                ev_devac = {}
                res_dv_last = 0
                for dm in range(DC):
                    wts = [("pe", last_gu_mm)]
                    if dm >= 2:
                        wts.append(("pe", ev_wd.get(("mm", dm - 2), 0)))
                    ev_wd[dm] = P.emit("gp", lambda e, ih=ih, dm=dm: e.dma_start(
                        out=BF(A_WS + (dm % 2) * 4096, 4096).rearrange("p (g i) -> p g i", i=128),
                        in_=wdT.ap().rearrange("(g p) d -> p g d", p=128)[:, ih * 32:(ih + 1) * 32, dm * 128:(dm + 1) * 128]),
                        waits=wts, inc=f"wd{dm % 2}", amt=16)
                    ps = pP if dm % 2 == 0 else pB
                    for ic in range(32):
                        wts = []
                        if ic == 0:
                            wts = [(f"wd{dm % 2}", ev_wd[dm]), ("dv", ev_devac.get(dm - 2, 0)),
                                   ("ac", ev_hs[1])]
                            if dm == 0 or dm == 1:
                                wts.append(("dv", ev_h_done[(ih, 31)]))
                        for nh in range(2):
                            P.emit("pe", lambda e, ps=ps, ih=ih, dm=dm, ic=ic, nh=nh: e.matmul(
                                ps[:, nh * 512:(nh + 1) * 512],
                                BF(A_WS + (dm % 2) * 4096 + ic * 128, 128),
                                BF(A_HT + ic * 1024 + nh * 512, 512),
                                start=(ic == 0), stop=(ic == 31)),
                                waits=wts, inc="pe")
                            wts = []
                    ev_wd[("mm", dm)] = P.cnt["pe"]
                    ev_mm = P.cnt["pe"]
                    ev_sub = P.emit("dv", lambda e, ps=ps: e.tensor_sub(FF(F_T2, 1024), ps[:, :], hs128B),
                                    waits=[("pe", ev_mm)], inc="dv")
                    res_dv_last = P.emit("dv", lambda e, dm=dm: e.scalar_tensor_tensor(
                        res[:, dm, :], FF(F_T2, 1024), sb_ds[:, dm:dm + 1], res[:, dm, :],
                        ALU.mult, ALU.add), inc="dv")
                    ev_devac[dm] = res_dv_last
                    if ih == 1 and st["next_ln"]:
                        ln_chunk(dm, "dv", res_dv_last)
                st["wd_done"] = ev_wd[("mm", DC - 1)]
            st["res_dv"] = res_dv_last

        import os
        _nsub = int(os.environ.get("KNSUB", "4"))
        _ph = int(os.environ.get("KPHASE", "9"))
        for q in range(8):
            ev_q = P.emit("sy", lambda e, q=q: e.dma_start(
                out=res[:, q * 2:(q + 1) * 2, :],
                in_=xT_in.ap().rearrange("(c p) t -> p c t", p=128)[:, q * 2:(q + 1) * 2, :]),
                inc="ld", amt=16)
            for c in range(q * 2, q * 2 + 2):
                ln_chunk(c, "ld", ev_q)
        emit_attn(0, _ph)
        if _nsub >= 2:
            emit_ffn(0)
        if _nsub >= 3:
            emit_attn(1)
        if _nsub >= 4:
            st["next_ln"] = False
            emit_ffn(1)

        if os.environ.get("KDBG"):
            w = [("dv", st["res_dv"]), ("pe", P.cnt["pe"]), ("ac", P.cnt["ac"])]
            P.emit("gp", lambda e: e.dma_start(
                out=outT.ap()[0:128, :], in_=abf[:, A_KT:A_KT + 1024]),
                waits=w, inc="out", amt=16)
            P.emit("gp", lambda e: e.dma_start(
                out=outT.ap()[128:256, :], in_=abf[:, A_QT + 3 * 1024:A_QT + 4 * 1024]),
                inc="out", amt=16)
            P.emit("gp", lambda e: e.dma_start(
                out=outT.ap()[256:384, :], in_=abf[:, A_V:A_V + 1024]),
                inc="out", amt=16)
        else:
            P.emit("sy", lambda e: e.dma_start(
                out=outT.ap().rearrange("(c p) t -> p c t", p=128), in_=res[:, :, :]),
                waits=[("dv", st["res_dv"])], inc="out", amt=16)
        ev_out = P.cnt["out"]
        P.emit("sy", lambda e: e.wait_ge(s_out, ev_out))

        def realize(eng_name, eng):
            hwm = {}
            for fn, waits, incsem, after, amt in P.ops[eng_name]:
                for snm, val in waits:
                    if val and val > hwm.get(snm, 0):
                        eng.wait_ge(SEMS[snm], val)
                        hwm[snm] = val
                ins = fn(eng)
                if incsem is not None:
                    if amt == "dflt":
                        ins.then_inc(SEMS[incsem])
                    else:
                        ins.then_inc(SEMS[incsem], amt)

        @block.tensor
        def _(t):
            realize("pe", t)

        @block.scalar
        def _(s):
            realize("ac", s)

        @block.vector
        def _(v):
            realize("dv", v)

        @block.gpsimd
        def _(g):
            with allow_ici_all_to_all():
                realize("gp", g)

        @block.sync
        def _(sy):
            realize("sy", sy)

    return nc


_NC_CACHE = None


def make_in_maps(L0_attn_a_x, L0_attn_a_W_K, L0_attn_a_W_Q, L0_attn_a_W_V, L0_attn_a_W_O,
                 L0_attn_b_w_qkv, L0_attn_b_w_out,
                 L0_ffn_gate_w, L0_ffn_gate_scale, L0_ffn_up_w, L0_ffn_up_scale,
                 L0_ffn_down_w, L0_ffn_down_scale):
    x = np.asarray(L0_attn_a_x, dtype=np.float32).reshape(S, D)
    W_K = np.asarray(L0_attn_a_W_K, dtype=np.float32)
    W_Q = np.asarray(L0_attn_a_W_Q, dtype=np.float32)
    W_V = np.asarray(L0_attn_a_W_V, dtype=np.float32)
    W_O = np.asarray(L0_attn_a_W_O, dtype=np.float32)
    w_qkv = np.asarray(L0_attn_b_w_qkv, dtype=np.float32)
    w_out = np.asarray(L0_attn_b_w_out, dtype=np.float32)
    gw = np.asarray(L0_ffn_gate_w, dtype=np.int32)
    uw = np.asarray(L0_ffn_up_w, dtype=np.int32)
    dw = np.asarray(L0_ffn_down_w, dtype=np.int32)
    gs = np.asarray(L0_ffn_gate_scale, dtype=np.float32)
    us = np.asarray(L0_ffn_up_scale, dtype=np.float32)
    ds = np.asarray(L0_ffn_down_scale, dtype=np.float32)

    C = np.ascontiguousarray
    woT_a = C(W_O.T)
    woT_b = C(w_out)
    wgT = C(gw.T)
    wuT = C(uw.T)
    wdT = C(dw.T)
    gsT = C(gs.reshape(IC, 128).T)
    usT = C(us.reshape(IC, 128).T)
    dsT = C(ds.reshape(DC, 128).T)
    idn = np.eye(128, dtype=ml_dtypes.bfloat16)
    mask = np.triu(np.ones((128, 128))).astype(ml_dtypes.bfloat16)
    cvec = np.tile(np.array([[EPS, -0.5, INV_SQRT_HD, 1.0 / D, 128.0, 0.0, 0.0, 0.0]],
                            dtype=np.float32), (128, 1))

    in_maps = []
    for c in range(NC):
        h0, h1 = 2 * c, 2 * c + 1
        m = dict(
            xT_in=C(x[c * T:(c + 1) * T, :].T),
            wqT_a=C(np.stack([W_Q[h0].T, W_Q[h1].T])),
            wkT_a=C(np.stack([W_K[h0].T, W_K[h1].T])),
            wvT_a=C(np.stack([W_V[h0].T, W_V[h1].T])),
            woT_a=woT_a,
            wqT_b=C(np.stack([w_qkv[:, h0 * HD:(h0 + 1) * HD], w_qkv[:, h1 * HD:(h1 + 1) * HD]])),
            wkT_b=C(np.stack([w_qkv[:, D + h0 * HD:D + (h0 + 1) * HD], w_qkv[:, D + h1 * HD:D + (h1 + 1) * HD]])),
            wvT_b=C(np.stack([w_qkv[:, 2 * D + h0 * HD:2 * D + (h0 + 1) * HD], w_qkv[:, 2 * D + h1 * HD:2 * D + (h1 + 1) * HD]])),
            woT_b=woT_b,
            wgT=wgT, wuT=wuT, wdT=wdT, gsT=gsT, usT=usT, dsT=dsT,
            idn=idn, mask=mask,
            cvec=cvec,
        )
        in_maps.append(m)
    return in_maps


def kernel(**inputs):
    global _NC_CACHE
    in_maps = make_in_maps(**inputs)
    if _NC_CACHE is None:
        _NC_CACHE = build_program()
    res = run_bass_kernel_spmd(_NC_CACHE, in_maps, core_ids=list(range(NC)))
    out = np.empty((S, D), dtype=np.float32)
    for c in range(NC):
        out[c * T:(c + 1) * T, :] = res.results[c]["outT"].T
    return out.reshape(1, S, D)
